# version 01
# baseline (speedup 1.0000x reference)
import numpy as np
import ml_dtypes
from contextlib import ExitStack

import concourse.bass as bass
import concourse.mybir as mybir

F32 = mybir.dt.float32
BF16 = mybir.dt.bfloat16
AF = mybir.ActivationFunctionType
ALU = mybir.AluOpType
AX = mybir.AxisListType.X
BF = ml_dtypes.bfloat16

D = 2048
I = 1408
NEXP = 12
TOK = 2048
TPR = 512
NT = 4
NR = TOK // TPR
NCI = 16
NIC = 11
EPS = 1e-5

PHASES = "full"
DEBUG = False
import os
ATTN_STOP = os.environ.get("ATTN_STOP", "")


def build_program():
    nc = bass.Bass("TRN2", target_bir_lowering=False)
    es = ExitStack()
    streams = {"sync": [], "gpsimd": [], "tensor": [], "vector": [], "scalar": []}
    cnt = {}
    SEM = {}
    for name in ["pe", "act", "dve", "gq", "sq", "cc",
                 "s_const", "s_x", "s_stx", "s_cache",
                 "g_wgu0", "g_wgu1", "g_wgu2", "g_wgu3",
                 "g_wd0", "g_wd1", "g_wd2", "g_acc", "g_attw",
                 "s_agx", "s_wT", "s_qkv", "s_zt", "s_rs0", "s_rs1",
                 "s_ld3a", "s_ld3b", "s_slab", "s_hld", "s_qkva", "s_qkvb"]:
        SEM[name] = es.enter_context(nc.semaphore(name))
        cnt[name] = 0

    def post(name, k=1):
        cnt[name] += k
        return cnt[name]

    def em(eng, fn):
        streams[eng].append(fn)

    x_in = nc.dram_tensor("x", [TOK, D], F32, kind="ExternalInput").ap()
    gate_w = nc.dram_tensor("gate_w", [8, D], F32, kind="ExternalInput").ap()
    sgw = nc.dram_tensor("sgw", [D, 1], F32, kind="ExternalInput").ap()
    w_gate = nc.dram_tensor("w_gate", [8, D, I], F32, kind="ExternalInput").ap()
    w_up = nc.dram_tensor("w_up", [8, D, I], F32, kind="ExternalInput").ap()
    w_down = nc.dram_tensor("w_down", [8, I, D], F32, kind="ExternalInput").ap()
    sw_gate = nc.dram_tensor("sw_gate", [D, 4 * I], F32, kind="ExternalInput").ap()
    sw_up = nc.dram_tensor("sw_up", [D, 4 * I], F32, kind="ExternalInput").ap()
    sw_down = nc.dram_tensor("sw_down", [4 * I, D], F32, kind="ExternalInput").ap()
    wq = nc.dram_tensor("wq", [1024, D], F32, kind="ExternalInput").ap()
    wk = nc.dram_tensor("wk", [1024, D], F32, kind="ExternalInput").ap()
    wv = nc.dram_tensor("wv", [1024, D], F32, kind="ExternalInput").ap()
    wo = nc.dram_tensor("wo", [D, 1024], F32, kind="ExternalInput").ap()
    y_out = nc.dram_tensor("y", [TOK, D], F32, kind="ExternalOutput").ap()

    if DEBUG:
        dbg_xlnT = nc.dram_tensor("dbg_xlnT", [128, NCI * TPR], BF16, kind="ExternalOutput").ap()
        dbg_lgb = nc.dram_tensor("dbg_lgb", [128, NT * 16], F32, kind="ExternalOutput").ap()
        dbg_cmb = nc.dram_tensor("dbg_cmb", [128, NT * 16], F32, kind="ExternalOutput").ap()
        dbg_gs = nc.dram_tensor("dbg_gs", [128, I], BF16, kind="ExternalOutput").ap()
        dbg_h = nc.dram_tensor("dbg_h", [128, I], BF16, kind="ExternalOutput").ap()
        dbg_hT = nc.dram_tensor("dbg_hT", [128, NIC * 128], BF16, kind="ExternalOutput").ap()
        dbg_xt = nc.dram_tensor("dbg_xt", [128, D], F32, kind="ExternalOutput").ap()
    x1s = nc.dram_tensor("x1s", [TOK, D], F32).ap()
    wgu_c = nc.dram_tensor("wgu_c", [NEXP, D, 2 * I], BF16).ap()
    wd_c = nc.dram_tensor("wd_c", [NEXP, I, D], BF16).ap()
    agx_in = nc.dram_tensor("agx_in", [D, TOK], BF16).ap()
    agx_out = nc.dram_tensor("agx_out", [2 * D, TOK], BF16).ap()
    wqT_s = nc.dram_tensor("wqT_s", [D, 1024], BF16).ap()
    wkT_s = nc.dram_tensor("wkT_s", [D, 1024], BF16).ap()
    wvT_s = nc.dram_tensor("wvT_s", [D, 1024], BF16).ap()
    woT_s = nc.dram_tensor("woT_s", [1024, D], BF16).ap()
    qT_sd = nc.dram_tensor("qT_sd", [1024, 4096], BF16).ap()
    kT_sd = nc.dram_tensor("kT_sd", [1024, 4096], BF16).ap()
    vT_sd = nc.dram_tensor("vT_sd", [1024, 4096], BF16).ap()
    zT_sd = nc.dram_tensor("zT_sd", [1024, 4096], BF16).ap()
    rs_in = nc.dram_tensor("rs_in", [4096, D], F32).ap()
    rs_out = nc.dram_tensor("rs_out", [TOK, D], F32).ap()

    ident_bf_d = nc.inline_tensor(np.eye(128, dtype=BF), "ident_bf_d")
    ident_f_d = nc.inline_tensor(np.eye(128, dtype=np.float32), "ident_f_d")
    tri_d = nc.inline_tensor(np.tril(np.ones((128, 128), dtype=np.float32)).astype(BF), "tri01_d")
    kconst_d = nc.inline_tensor(np.full((128, 2), EPS, dtype=np.float32), "kconst_d")

    ident_bf = es.enter_context(nc.sbuf_tensor("ident_bf", [128, 128], BF16))
    ident_f = es.enter_context(nc.sbuf_tensor("ident_f", [128, 128], F32))
    tri01 = es.enter_context(nc.sbuf_tensor("tri01", [128, 128], BF16))
    gwT = es.enter_context(nc.sbuf_tensor("gwT", [128, NCI * 16], F32))
    kconst = es.enter_context(nc.sbuf_tensor("kconst", [128, 2], F32))

    c_tgt = post("s_const", 16 * 4)

    def f_const(S):
        S.dma_start(out=kconst[:, :], in_=kconst_d.ap()).then_inc(SEM["s_const"], 16)
        S.dma_start(out=ident_bf[:, :], in_=ident_bf_d.ap()).then_inc(SEM["s_const"], 16)
        S.dma_start(out=ident_f[:, :], in_=ident_f_d.ap()).then_inc(SEM["s_const"], 16)
        S.dma_start(out=tri01[:, :], in_=tri_d.ap()).then_inc(SEM["s_const"], 16)
    em("sync", f_const)

    fes = ExitStack()
    xt = fes.enter_context(nc.sbuf_tensor("xt", [128, NT * D], F32))
    xlnT = fes.enter_context(nc.sbuf_tensor("xlnT", [128, NCI * TPR], BF16))
    wgu = fes.enter_context(nc.sbuf_tensor("wgu", [128, NCI * 2 * I], BF16))
    wdb = fes.enter_context(nc.sbuf_tensor("wdb", [128, NIC * D], BF16))
    gs = fes.enter_context(nc.sbuf_tensor("gs", [128, 2 * I], BF16))
    hT = fes.enter_context(nc.sbuf_tensor("hT", [128, 2 * NIC * 128], BF16))
    cmb = fes.enter_context(nc.sbuf_tensor("cmb", [128, NT * 16], F32))
    lgb = fes.enter_context(nc.sbuf_tensor("lgb", [128, NT * 16], F32))
    xf32 = fes.enter_context(nc.sbuf_tensor("xf32", [128, 2 * 128], F32))
    xT32 = fes.enter_context(nc.sbuf_tensor("xT32", [128, 2 * 128], F32))
    sst = fes.enter_context(nc.sbuf_tensor("sst", [128, NT * 8], F32))
    t2s = fes.enter_context(nc.sbuf_tensor("t2s", [128, NT * 48], F32))

    psGU = fes.enter_context(nc.psum_tensor("psGU", [128, 1536], F32))
    psD = fes.enter_context(nc.psum_tensor("psD", [128, 2048], F32))
    psTr = fes.enter_context(nc.psum_tensor("psTr", [128, 1024], BF16))

    wgu_v = wgu[:, :].rearrange("p (c n) -> p c n", c=NCI)
    wdb_v = wdb[:, :].rearrange("p (c n) -> p c n", c=NIC)
    xlnT_v = xlnT[:, :].rearrange("p (c n) -> p c n", c=NCI)
    lgps = psD[:, 1536:1552]

    gw_nat = xt[0:8, 0:D]
    sg_nat = fes.enter_context(nc.sbuf_tensor("sg_nat", [128, NCI], F32))
    gl = post("s_const", 16 * 2)

    def f_gwl(S):
        S.dma_start(out=gw_nat, in_=gate_w).then_inc(SEM["s_const"], 16)
        S.dma_start(out=sg_nat[:, :].rearrange("p (c one) -> p c one", one=1),
                    in_=sgw.rearrange("(c p) one -> p c one", p=128),
                    allow_slow_non_contiguous=True).then_inc(SEM["s_const"], 16)
    em("sync", f_gwl)

    gz = post("dve")

    def f_gz(V):
        V.memset(gwT[:, :], 0.0).then_inc(SEM["dve"], 1)
    em("vector", f_gz)

    gwtr, gwev = {}, {}
    for c in range(NCI):
        gwtr[c] = post("pe")

        def f_t(T, c=c):
            if c == 0:
                T.wait_ge(SEM["s_const"], gl)
                T.wait_ge(SEM["dve"], gz)
            if c >= 3:
                T.wait_ge(SEM["dve"], gwev[c - 3])
            T.transpose(psD[:, (c % 3) * 512:(c % 3) * 512 + 8],
                        xt[0:8, c * 128:(c + 1) * 128], ident_f[0:8, 0:8]).then_inc(SEM["pe"], 1)
        em("tensor", f_t)
        gwev[c] = post("dve")

        def f_v(V, c=c):
            V.wait_ge(SEM["pe"], gwtr[c])
            V.tensor_copy(gwT[:, c * 16:c * 16 + 8],
                          psD[:, (c % 3) * 512:(c % 3) * 512 + 8]).then_inc(SEM["dve"], 1)
        em("vector", f_v)
    gws = post("dve")

    def f_sg(V):
        V.wait_ge(SEM["s_const"], gl)
        V.tensor_copy(gwT[:, :].rearrange("p (c n) -> p c n", c=NCI)[:, :, 8:9],
                      sg_nat[:, :].rearrange("p (c one) -> p c one", one=1)).then_inc(SEM["dve"], 1)
    em("vector", f_sg)
    gwT_ready_dve = gws
    gwT_psD_free_dve = gwev[NCI - 1]

    def src_gu(e, half):
        if e < 8:
            t = w_gate if half == 0 else w_up
            return t[e].rearrange("(c p) i -> p c i", p=128)
        j = e - 8
        t = sw_gate if half == 0 else sw_up
        return t.rearrange("(c p) i -> p c i", p=128)[:, :, j * I:(j + 1) * I]

    def src_d(e):
        if e < 8:
            return w_down[e].rearrange("(c p) d -> p c d", p=128)
        j = e - 8
        return sw_down[j * I:(j + 1) * I, :].rearrange("(c p) d -> p c d", p=128)

    def emit_wgu_dma(use_cache, e, wait_pe, wait_sq):
        tgts = []
        for g in range(4):
            cs = slice(g * 4, (g + 1) * 4)
            sname = f"g_wgu{g}"
            post(sname, 16 if use_cache else 32)

            def f(G, g=g, e=e, cs=cs, wp=wait_pe, ws=wait_sq, uc=use_cache, sname=sname):
                if g == 0:
                    if wp is not None:
                        G.wait_ge(SEM["pe"], wp)
                    if ws is not None:
                        G.wait_ge(SEM["s_cache"], ws)
                if uc:
                    G.dma_start(out=wgu_v[:, cs, :],
                                in_=wgu_c[e].rearrange("(c p) n -> p c n", p=128)[:, cs, :]
                                ).then_inc(SEM[sname], 16)
                else:
                    G.dma_start(out=wgu_v[:, cs, 0:I], in_=src_gu(e, 0)[:, cs, :]
                                ).then_inc(SEM[sname], 16)
                    G.dma_start(out=wgu_v[:, cs, I:2 * I], in_=src_gu(e, 1)[:, cs, :]
                                ).then_inc(SEM[sname], 16)
            em("gpsimd", f)
            tgts.append((sname, cnt[sname]))
        return tgts

    def emit_wd_dma(use_cache, e, wait_pe, wait_sq):
        tgts = []
        for gi, (c0, c1) in enumerate([(0, 4), (4, 8), (8, 11)]):
            sname = f"g_wd{gi}"
            post(sname, 16)

            def f(G, gi=gi, c0=c0, c1=c1, e=e, wp=wait_pe, ws=wait_sq, uc=use_cache,
                  sname=sname):
                if gi == 0:
                    if wp is not None:
                        G.wait_ge(SEM["pe"], wp)
                    if ws is not None:
                        G.wait_ge(SEM["s_cache"], ws)
                if uc:
                    G.dma_start(out=wdb_v[:, c0:c1, :],
                                in_=wd_c[e].rearrange("(c p) n -> p c n", p=128)[:, c0:c1, :]
                                ).then_inc(SEM[sname], 16)
                else:
                    G.dma_start(out=wdb_v[:, c0:c1, :],
                                in_=src_d(e)[:, c0:c1, :]).then_inc(SEM[sname], 16)
            em("gpsimd", f)
            tgts.append((sname, cnt[sname]))
        return tgts

    def emit_cache_write(e, wgu_tgts, wd_tgts):
        post("s_cache", 32)

        def f(S, e=e, wgu_tgts=wgu_tgts, wd_tgts=wd_tgts):
            for sname, v in wgu_tgts:
                S.wait_ge(SEM[sname], v)
            S.dma_start(out=wgu_c[e].rearrange("(c p) n -> p c n", p=128),
                        in_=wgu_v[:, :, :]).then_inc(SEM["s_cache"], 16)
            for sname, v in wd_tgts:
                S.wait_ge(SEM[sname], v)
            S.dma_start(out=wd_c[e].rearrange("(c p) n -> p c n", p=128),
                        in_=wdb_v[:, :, :]).then_inc(SEM["s_cache"], 16)
        em("sync", f)
        return cnt["s_cache"]

    def emit_ln_round(ffn_idx, r, x_ready, psD_free_dve, psD_free_pe):
        stats1 = {}
        for t in range(NT):
            xt_t = xt[:, t * D:(t + 1) * D]
            mcol = sst[:, t * 8 + 0:t * 8 + 1]
            qcol = sst[:, t * 8 + 1:t * 8 + 2]
            tcol = sst[:, t * 8 + 2:t * 8 + 3]
            vcol = sst[:, t * 8 + 3:t * 8 + 4]
            stats1[t] = post("dve", 2)

            def f(V, t=t, xt_t=xt_t, mcol=mcol, qcol=qcol,
                  x_ready=x_ready, pfree=psD_free_dve, ppe=psD_free_pe):
                if t == 0:
                    V.wait_ge(SEM[x_ready[0]], x_ready[1])
                    if pfree is not None:
                        V.wait_ge(SEM["dve"], pfree)
                    if ppe is not None:
                        V.wait_ge(SEM["pe"], ppe)
                V.reduce_sum(mcol, xt_t, axis=AX)
                V.scalar_tensor_tensor(psD[:, :], xt_t, 1.0, xt_t, ALU.mult, ALU.mult,
                                       accum_out=qcol).then_inc(SEM["dve"], 2)
            em("vector", f)
            m2t = post("dve")

            def f_m2(V, mcol=mcol, tcol=tcol, s1=stats1[t]):
                V.wait_ge(SEM["dve"], s1)
                V.scalar_tensor_tensor(tcol, mcol, 1.0 / (D * D), mcol, ALU.mult, ALU.mult
                                       ).then_inc(SEM["dve"], 1)
            em("vector", f_m2)
            stats1[t] = m2t
        var_done = {}
        for t in range(NT):
            qcol = sst[:, t * 8 + 1:t * 8 + 2]
            tcol = sst[:, t * 8 + 2:t * 8 + 3]
            vcol = sst[:, t * 8 + 3:t * 8 + 4]
            var_done[t] = post("dve")

            def f(V, t=t, qcol=qcol, tcol=tcol, vcol=vcol):
                V.wait_ge(SEM["dve"], stats1[t])
                V.scalar_tensor_tensor(vcol, qcol, 1.0 / D, tcol, ALU.mult, ALU.subtract
                                       ).then_inc(SEM["dve"], 1)
            em("vector", f)
        ln4 = post("act")
        rstd_done = post("act")

        def f_ln4(S):
            S.wait_ge(SEM["dve"], var_done[NT - 1])
            last = None
            for t in range(NT):
                vcol = sst[:, t * 8 + 3:t * 8 + 4]
                tcol = sst[:, t * 8 + 2:t * 8 + 3]
                last = S.activation(tcol, vcol, AF.Ln, bias=kconst[:, 0:1])
            last.then_inc(SEM["act"], 1)
        em("scalar", f_ln4)

        def f_rstd(S, ln4=ln4):
            S.wait_ge(SEM["act"], ln4)
            last = None
            for t in range(NT):
                tcol = sst[:, t * 8 + 2:t * 8 + 3]
                rcol = sst[:, t * 8 + 4:t * 8 + 5]
                last = S.activation(rcol, tcol, AF.Exp, scale=-0.5)
            last.then_inc(SEM["act"], 1)
        em("scalar", f_rstd)
        negm_done = post("dve")

        def f_negm(V):
            V.wait_ge(SEM["act"], rstd_done)
            last = None
            for t in range(NT):
                mcol = sst[:, t * 8 + 0:t * 8 + 1]
                ncol = sst[:, t * 8 + 5:t * 8 + 6]
                last = V.tensor_scalar_mul(ncol, mcol, -1.0 / D)
            last.then_inc(SEM["dve"], 1)
        em("vector", f_negm)

        lg_mm_last = {}
        prev_tile_lg_dve = None
        for t in range(NT):
            xt_t = xt[:, t * D:(t + 1) * D]
            ncol = sst[:, t * 8 + 5:t * 8 + 6]
            rcol = sst[:, t * 8 + 4:t * 8 + 5]
            vck, trk, evk, mmk = {}, {}, {}, {}
            for c in range(NCI):
                slot = c % 2
                b3 = (c % 3) * 512
                vck[c] = post("dve")

                def f_vc(V, c=c, t=t, slot=slot, ptl=prev_tile_lg_dve, trk=trk,
                         ncol=ncol, rcol=rcol):
                    if c == 0:
                        V.wait_ge(SEM["dve"], negm_done)
                        if ptl is not None:
                            V.wait_ge(SEM["dve"], ptl)
                    if c >= 2:
                        V.wait_ge(SEM["pe"], trk[c - 2])
                    V.tensor_scalar(xf32[:, slot * 128:(slot + 1) * 128],
                                    xt[:, t * D + c * 128:t * D + (c + 1) * 128],
                                    ncol, rcol, ALU.add, ALU.mult).then_inc(SEM["dve"], 1)
                em("vector", f_vc)
                trk[c] = post("pe")

                def f_tc(T, c=c, t=t, slot=slot, b3=b3, vck=vck, evk=evk):
                    T.wait_ge(SEM["dve"], vck[c])
                    if c >= 3:
                        T.wait_ge(SEM["dve"], evk[c - 3])
                    T.transpose(psD[:, b3:b3 + 128], xf32[:, slot * 128:(slot + 1) * 128],
                                ident_f[:, :]).then_inc(SEM["pe"], 1)
                em("tensor", f_tc)
                evk[c] = post("dve", 2)

                def f_ec(V, c=c, t=t, slot=slot, b3=b3, trk=trk, mmk=mmk):
                    V.wait_ge(SEM["pe"], trk[c])
                    if c >= 2:
                        V.wait_ge(SEM["pe"], mmk[c - 2])
                    V.tensor_copy(xT32[:, slot * 128:(slot + 1) * 128], psD[:, b3:b3 + 128])
                    V.tensor_copy(xlnT_v[:, c, t * 128:(t + 1) * 128],
                                  psD[:, b3:b3 + 128]).then_inc(SEM["dve"], 2)
                em("vector", f_ec)
                mmk[c] = post("pe")

                def f_mc(T, c=c, t=t, slot=slot, evk=evk,
                         need_gwT=(ffn_idx == 0 and r == 0)):
                    T.wait_ge(SEM["dve"], evk[c])
                    if need_gwT and c == 0 and t == 0:
                        T.wait_ge(SEM["dve"], gwT_ready_dve)
                    T.matmul(lgps, xT32[:, slot * 128:(slot + 1) * 128],
                             gwT[:, c * 16:c * 16 + 16],
                             start=(c == 0), stop=(c == NCI - 1)).then_inc(SEM["pe"], 1)
                em("tensor", f_mc)
            lg_mm_last[t] = mmk[NCI - 1]
            prev_tile_lg_dve = post("dve")

            def f_lg(V, t=t, mm_last=lg_mm_last[t]):
                V.wait_ge(SEM["pe"], mm_last)
                V.tensor_copy(lgb[:, t * 16:(t + 1) * 16], lgps).then_inc(SEM["dve"], 1)
            em("vector", f_lg)

        lg_all_dve = prev_tile_lg_dve
        ddp = {}

        def em_v(fn, wait_act=None, extra_dve=None):
            tgt = post("dve")

            def f(V, fn=fn, wait_act=wait_act, prev_tgt=tgt - 1, extra_dve=extra_dve):
                V.wait_ge(SEM["dve"], prev_tgt)
                if extra_dve is not None:
                    V.wait_ge(SEM["dve"], extra_dve)
                if wait_act is not None:
                    V.wait_ge(SEM["act"], wait_act)
                fn(V).then_inc(SEM["dve"], 1)
            em("vector", f)
            return tgt

        for t in range(NT):
            base = t * 48
            m1 = t2s[:, base + 0:base + 1]; m2 = t2s[:, base + 1:base + 2]
            ddc = t2s[:, base + 2:base + 3]
            eq1 = t2s[:, base + 8:base + 16]; lgm = t2s[:, base + 16:base + 24]
            eq2 = t2s[:, base + 24:base + 32]
            l8 = lgb[:, t * 16:t * 16 + 8]
            em_v(lambda V, m1=m1, l8=l8: V.reduce_max(m1, l8, axis=AX),
                 extra_dve=lg_all_dve)
            em_v(lambda V, eq1=eq1, l8=l8, m1=m1: V.tensor_scalar(eq1, l8, m1, None, ALU.is_equal))
            em_v(lambda V, lgm=lgm, eq1=eq1, l8=l8:
                 V.scalar_tensor_tensor(lgm, eq1, -1e30, l8, ALU.mult, ALU.add))
            em_v(lambda V, m2=m2, lgm=lgm: V.reduce_max(m2, lgm, axis=AX))
            em_v(lambda V, eq2=eq2, lgm=lgm, m2=m2: V.tensor_scalar(eq2, lgm, m2, None, ALU.is_equal))
            ddp[t] = em_v(lambda V, ddc=ddc, m2=m2, m1=m1: V.tensor_tensor(ddc, m2, m1, ALU.subtract))
        exps_done = post("act")

        def f_exps(S):
            S.wait_ge(SEM["dve"], ddp[NT - 1])
            last = None
            for t in range(NT):
                base = t * 48
                last = S.activation(t2s[:, base + 3:base + 4], t2s[:, base + 2:base + 3], AF.Exp)
            for t in range(NT):
                base = t * 48
                last = S.activation(t2s[:, base + 6:base + 7], lgb[:, t * 16 + 8:t * 16 + 9],
                                    AF.Exp, scale=-1.0)
            last.then_inc(SEM["act"], 1)
        em("scalar", f_exps)
        cmb_done = None
        for t in range(NT):
            base = t * 48
            eec = t2s[:, base + 3:base + 4]; ssc = t2s[:, base + 4:base + 5]
            rrc = t2s[:, base + 5:base + 6]; egc = t2s[:, base + 6:base + 7]
            w1c = t2s[:, base + 7:base + 8]
            eq1 = t2s[:, base + 8:base + 16]; eq2 = t2s[:, base + 24:base + 32]
            t1c = t2s[:, base + 32:base + 40]
            w2c = t2s[:, base + 40:base + 41]
            sgc = t2s[:, base + 41:base + 42]
            hgc = t2s[:, base + 42:base + 43]
            em_v(lambda V, ssc=ssc, eec=eec: V.tensor_scalar_add(ssc, eec, 1.0),
                 wait_act=exps_done)
            em_v(lambda V, rrc=rrc, ssc=ssc: V.reciprocal(rrc, ssc))
            em_v(lambda V, w1c=w1c, rrc=rrc: V.tensor_scalar_mul(w1c, rrc, 0.5))
            em_v(lambda V, w2c=w2c, eec=eec, rrc=rrc: V.tensor_tensor(w2c, eec, rrc, ALU.mult))
            em_v(lambda V, w2c=w2c: V.tensor_scalar_mul(w2c, w2c, 0.5))
            em_v(lambda V, t1c=t1c, eq1=eq1, w1c=w1c: V.tensor_scalar(t1c, eq1, w1c, None, ALU.mult))
            em_v(lambda V, t=t, eq2=eq2, w2c=w2c, t1c=t1c:
                 V.scalar_tensor_tensor(cmb[:, t * 16:t * 16 + 8], eq2, w2c, t1c,
                                        ALU.mult, ALU.add))
            em_v(lambda V, sgc=sgc, egc=egc: V.tensor_scalar_add(sgc, egc, 1.0))
            em_v(lambda V, hgc=hgc, sgc=sgc: V.reciprocal(hgc, sgc))
            cmb_done = em_v(lambda V, t=t, hgc=hgc:
                            V.tensor_scalar_mul(cmb[:, t * 16 + 8:t * 16 + 9], hgc, 0.5))
        return cmb_done, lg_all_dve

    dbg_state = {"done": False}

    def emit_dbg_snapshot(wait_sem, wait_val, out_ap, in_ap):
        if not DEBUG:
            return None
        post("gq", 16)

        def f(G, wait_sem=wait_sem, wait_val=wait_val, out_ap=out_ap, in_ap=in_ap):
            G.wait_ge(SEM[wait_sem], wait_val)
            G.dma_start(out=out_ap, in_=in_ap).then_inc(SEM["gq"], 16)
        em("gpsimd", f)
        return cnt["gq"]

    def emit_g(sd, psGU_free_dve, xln_ready_dve):
        tgt = post("pe")

        def f(T, sd=sd, pf=psGU_free_dve, xr=xln_ready_dve):
            if pf is not None:
                T.wait_ge(SEM["dve"], pf)
            T.wait_ge(SEM["dve"], xr)
            last = None
            for c in range(NCI):
                if sd["t"] == 0 and c % 4 == 0:
                    sname, v = sd["wgu_tgts"][c // 4]
                    T.wait_ge(SEM[sname], v)
                elif sd["t"] != 0 and c == 0:
                    pass
                xs = xlnT_v[:, c, sd["t"] * 128:(sd["t"] + 1) * 128]
                ws = wgu_v[:, c, 0:I]
                T.matmul(psGU[:, 0:512], xs, ws[:, 0:512], start=(c == 0), stop=(c == NCI - 1))
                T.matmul(psGU[:, 512:1024], xs, ws[:, 512:1024], start=(c == 0), stop=(c == NCI - 1))
                last = T.matmul(psGU[:, 1024:1408], xs, ws[:, 1024:1408],
                                start=(c == 0), stop=(c == NCI - 1))
            last.then_inc(SEM["pe"], 1)
        em("tensor", f)
        sd["g"] = tgt

    def emit_silu(sd, gs_free_pe):
        tgt = post("act")

        def f(S, sd=sd, gf=gs_free_pe):
            S.wait_ge(SEM["pe"], sd["g"])
            if gf is not None:
                S.wait_ge(SEM["pe"], gf)
            S.activation(gs[:, sd["slot"] * I:(sd["slot"] + 1) * I], psGU[:, 0:1408],
                         AF.Silu).then_inc(SEM["act"], 1)
        em("scalar", f)
        sd["silu"] = tgt

    def emit_u(sd):
        tgt = post("pe")

        def f(T, sd=sd):
            T.wait_ge(SEM["act"], sd["silu"])
            last = None
            for c in range(NCI):
                xs = xlnT_v[:, c, sd["t"] * 128:(sd["t"] + 1) * 128]
                ws = wgu_v[:, c, I:2 * I]
                T.matmul(psGU[:, 0:512], xs, ws[:, 0:512], start=(c == 0), stop=(c == NCI - 1))
                T.matmul(psGU[:, 512:1024], xs, ws[:, 512:1024], start=(c == 0), stop=(c == NCI - 1))
                last = T.matmul(psGU[:, 1024:1408], xs, ws[:, 1024:1408],
                                start=(c == 0), stop=(c == NCI - 1))
            last.then_inc(SEM["pe"], 1)
        em("tensor", f)
        sd["u"] = tgt

    def emit_h(sd, cmb_ready_dve):
        tgt = post("dve")
        e, t = sd["e"], sd["t"]
        col = t * 16 + (e if e < 8 else 8)

        def f(V, sd=sd, col=col, cr=cmb_ready_dve):
            V.wait_ge(SEM["pe"], sd["u"])
            V.wait_ge(SEM["dve"], cr)
            if sd.get("_h_extra_wait"):
                V.wait_ge(SEM["gq"], sd["_h_extra_wait"])
            V.scalar_tensor_tensor(gs[:, sd["slot"] * I:(sd["slot"] + 1) * I],
                                   psGU[:, 0:1408], cmb[:, col:col + 1],
                                   gs[:, sd["slot"] * I:(sd["slot"] + 1) * I],
                                   ALU.mult, ALU.mult).then_inc(SEM["dve"], 1)
        em("vector", f)
        sd["h"] = tgt

    def emit_tr1(sd, psTr_free_dve):
        tgt = post("pe")

        def f(T, sd=sd, pf=psTr_free_dve):
            T.wait_ge(SEM["dve"], sd["h"])
            if sd.get("_dbg_h"):
                T.wait_ge(SEM["gq"], sd["_dbg_h"])
            if pf is not None:
                T.wait_ge(SEM["dve"], pf)
            last = None
            for c in range(8):
                last = T.transpose(psTr[:, c * 128:(c + 1) * 128],
                                   gs[:, sd["slot"] * I + c * 128:sd["slot"] * I + (c + 1) * 128],
                                   ident_bf[:, :])
            last.then_inc(SEM["pe"], 1)
        em("tensor", f)
        sd["tr1"] = tgt

    def emit_ev1(sd, hT_free_pe):
        tgt = post("dve")

        def f(V, sd=sd, hf=hT_free_pe):
            V.wait_ge(SEM["pe"], sd["tr1"])
            if hf is not None:
                V.wait_ge(SEM["pe"], hf)
            V.tensor_copy(hT[:, sd["slot"] * NIC * 128:sd["slot"] * NIC * 128 + 1024],
                          psTr[:, 0:1024]).then_inc(SEM["dve"], 1)
        em("vector", f)
        sd["ev1"] = tgt

    def emit_dnA(sd, psD_free_dve, psD_free_pe):
        tgt = post("pe")

        def f(T, sd=sd, pf=psD_free_dve, pp=psD_free_pe):
            T.wait_ge(SEM["dve"], sd["ev1"])
            if pf is not None:
                T.wait_ge(SEM["dve"], pf)
            for sname, v in sd["wd_tgts"][0:2]:
                T.wait_ge(SEM[sname], v)
            last = None
            base = sd["slot"] * NIC * 128
            for ic in range(6):
                hs = hT[:, base + ic * 128:base + (ic + 1) * 128]
                for qq in range(4):
                    last = T.matmul(psD[:, qq * 512:(qq + 1) * 512], hs,
                                    wdb_v[:, ic, qq * 512:(qq + 1) * 512],
                                    start=(ic == 0), stop=False)
            last.then_inc(SEM["pe"], 1)
        em("tensor", f)
        sd["dnA"] = tgt

    def emit_tr2(sd):
        tgt = post("pe")

        def f(T, sd=sd):
            T.wait_ge(SEM["dve"], sd["ev1"])
            last = None
            for c in range(8, NIC):
                last = T.transpose(psTr[:, (c - 8) * 128:(c - 7) * 128],
                                   gs[:, sd["slot"] * I + c * 128:sd["slot"] * I + (c + 1) * 128],
                                   ident_bf[:, :])
            last.then_inc(SEM["pe"], 1)
        em("tensor", f)
        sd["tr2"] = tgt

    def emit_ev2(sd):
        tgt = post("dve")

        def f(V, sd=sd):
            V.wait_ge(SEM["pe"], sd["tr2"])
            V.tensor_copy(hT[:, sd["slot"] * NIC * 128 + 1024:sd["slot"] * NIC * 128 + 1408],
                          psTr[:, 0:384]).then_inc(SEM["dve"], 1)
        em("vector", f)
        sd["ev2"] = tgt

    def emit_dnB(sd):
        tgt = post("pe")

        def f(T, sd=sd):
            T.wait_ge(SEM["dve"], sd["ev2"])
            sname, v = sd["wd_tgts"][2]
            T.wait_ge(SEM[sname], v)
            last = None
            base = sd["slot"] * NIC * 128
            for ic in range(6, NIC):
                hs = hT[:, base + ic * 128:base + (ic + 1) * 128]
                for qq in range(4):
                    last = T.matmul(psD[:, qq * 512:(qq + 1) * 512], hs,
                                    wdb_v[:, ic, qq * 512:(qq + 1) * 512],
                                    start=False, stop=(ic == NIC - 1))
            last.then_inc(SEM["pe"], 1)
        em("tensor", f)
        sd["dnB"] = tgt

    def emit_xevac(sd):
        tgt = post("dve")

        def f(V, sd=sd):
            V.wait_ge(SEM["pe"], sd["dnB"])
            V.tensor_tensor(xt[:, sd["t"] * D:(sd["t"] + 1) * D],
                            xt[:, sd["t"] * D:(sd["t"] + 1) * D],
                            psD[:, :], ALU.add).then_inc(SEM["dve"], 1)
        em("vector", f)
        sd["xev"] = tgt

    def emit_lagged(prev_sd, carry):
        emit_tr1(prev_sd, prev_sd["_psTr_free"])
        emit_ev1(prev_sd, prev_sd["_hT_free"])
        emit_dnA(prev_sd, prev_sd["_psD_free"], None)
        emit_tr2(prev_sd)
        emit_ev2(prev_sd)


    def emit_attention(carry):
        SQ = "sq"; GQ = "gq"
        if ATTN_STOP == "none":
            carry["rs_done_cc"] = cnt["cc"]
            carry["rs_out"] = x1s
            carry["rs_ready"] = ("cc", 0)
            carry["prev_sd"] = None
            carry["prev2_sd"] = None
            return carry
        wgu_flat = wgu[:, :]
        wdb_flat = wdb[:, :]
        xlnT_flat = xlnT[:, :]
        xl2T_v = xlnT_flat.rearrange("p (c n) -> p c n", c=NCI)
        for tg in range(4):
            post("s_x", 16)
            ls = carry.get("last_store_sq")
            lx = carry.get("last_xev_dve")

            def f_x(S, tg=tg, ls=ls):
                if ls is not None:
                    for sname, v in ls:
                        S.wait_ge(SEM[sname], v)
                if tg > 0:
                    S.wait_ge(SEM["dve"], carry["_ln2_done"][tg - 1])
                S.dma_start(out=xt[:, :].rearrange("p (t n) -> p t n", t=NT),
                            in_=x1s[tg * TPR:(tg + 1) * TPR, :]
                            .rearrange("(t p) n -> p t n", p=128)).then_inc(SEM["s_x"], 16)
            em("sync", f_x)
            x_ready = ("s_x", cnt["s_x"])
            stats1 = {}
            for t in range(NT):
                xt_t = xt[:, t * D:(t + 1) * D]
                mcol = sst[:, t * 8 + 0:t * 8 + 1]
                qcol = sst[:, t * 8 + 1:t * 8 + 2]
                tcol = sst[:, t * 8 + 2:t * 8 + 3]
                vcol = sst[:, t * 8 + 3:t * 8 + 4]
                stats1[t] = post("dve", 2)

                def f(V, t=t, xt_t=xt_t, mcol=mcol, qcol=qcol,
                      x_ready=x_ready, lx=lx, tg=tg):
                    if t == 0:
                        V.wait_ge(SEM[x_ready[0]], x_ready[1])
                        if tg == 0 and lx is not None:
                            V.wait_ge(SEM["dve"], lx)
                        if tg > 0:
                            V.wait_ge(SEM["dve"], carry["_ln2_done"][tg - 1])
                            V.wait_ge(SEM["s_agx"], carry["_agx_st"][tg - 1])
                    V.reduce_sum(mcol, xt_t, axis=AX)
                    V.scalar_tensor_tensor(psD[:, :], xt_t, 1.0, xt_t, ALU.mult, ALU.mult,
                                           accum_out=qcol).then_inc(SEM["dve"], 2)
                em("vector", f)
                m2t = post("dve")

                def f_m2(V, mcol=mcol, tcol=tcol, s1=stats1[t]):
                    V.wait_ge(SEM["dve"], s1)
                    V.scalar_tensor_tensor(tcol, mcol, 1.0 / (D * D), mcol, ALU.mult, ALU.mult
                                           ).then_inc(SEM["dve"], 1)
                em("vector", f_m2)
                stats1[t] = m2t
            var_done = {}
            for t in range(NT):
                qcol = sst[:, t * 8 + 1:t * 8 + 2]
                tcol = sst[:, t * 8 + 2:t * 8 + 3]
                vcol = sst[:, t * 8 + 3:t * 8 + 4]
                var_done[t] = post("dve")

                def f(V, t=t, qcol=qcol, tcol=tcol, vcol=vcol, stats1=stats1):
                    V.wait_ge(SEM["dve"], stats1[t])
                    V.scalar_tensor_tensor(vcol, qcol, 1.0 / D, tcol, ALU.mult, ALU.subtract
                                           ).then_inc(SEM["dve"], 1)
                em("vector", f)
            ln4b = post("act")
            rstd_done = post("act")

            def f_ln4b(S, var_done=var_done):
                S.wait_ge(SEM["dve"], var_done[NT - 1])
                last = None
                for t in range(NT):
                    last = S.activation(sst[:, t * 8 + 2:t * 8 + 3], sst[:, t * 8 + 3:t * 8 + 4],
                                        AF.Ln, bias=kconst[:, 0:1])
                last.then_inc(SEM["act"], 1)
            em("scalar", f_ln4b)

            def f_rstd(S, ln4b=ln4b):
                S.wait_ge(SEM["act"], ln4b)
                last = None
                for t in range(NT):
                    last = S.activation(sst[:, t * 8 + 4:t * 8 + 5], sst[:, t * 8 + 2:t * 8 + 3],
                                        AF.Exp, scale=-0.5)
                last.then_inc(SEM["act"], 1)
            em("scalar", f_rstd)
            negm_done = post("dve")

            def f_negm(V, rstd_done=rstd_done):
                V.wait_ge(SEM["act"], rstd_done)
                last = None
                for t in range(NT):
                    last = V.tensor_scalar_mul(sst[:, t * 8 + 5:t * 8 + 6],
                                               sst[:, t * 8 + 0:t * 8 + 1], -1.0 / D)
                last.then_inc(SEM["dve"], 1)
            em("vector", f_negm)
            if ATTN_STOP == "b1stats":
                carry.setdefault("_ln2_done", {})[tg] = cnt["dve"]
                carry.setdefault("_agx_st", {})[tg] = cnt["s_agx"]
                continue
            evk_all = None
            prev_tile_ev = None
            for t in range(NT):
                ncol = sst[:, t * 8 + 5:t * 8 + 6]
                rcol = sst[:, t * 8 + 4:t * 8 + 5]
                vck, trk, evk = {}, {}, {}
                for c in range(NCI):
                    slot = c % 2
                    b3 = (c % 3) * 512
                    vck[c] = post("dve")

                    def f_vc(V, c=c, t=t, slot=slot, ncol=ncol, rcol=rcol, trk=trk,
                             negm_done=negm_done, ptv=prev_tile_ev):
                        if c == 0:
                            V.wait_ge(SEM["dve"], negm_done)
                            if ptv is not None:
                                V.wait_ge(SEM["dve"], ptv)
                        V.tensor_scalar(xf32[:, slot * 128:(slot + 1) * 128],
                                        xt[:, t * D + c * 128:t * D + (c + 1) * 128],
                                        ncol, rcol, ALU.add, ALU.mult).then_inc(SEM["dve"], 1)
                    em("vector", f_vc)
                    trk[c] = post("pe")

                    def f_tc(T, c=c, slot=slot, b3=b3, vck=vck, evk=evk, t=t, tg=tg):
                        T.wait_ge(SEM["dve"], vck[c])
                        if c >= 3:
                            T.wait_ge(SEM["dve"], evk[c - 3])
                        elif c < 3 and t == 0:
                            pass
                        T.transpose(psD[:, b3:b3 + 128], xf32[:, slot * 128:(slot + 1) * 128],
                                    ident_f[:, :]).then_inc(SEM["pe"], 1)
                    em("tensor", f_tc)
                    evk[c] = post("dve")

                    def f_ec(V, c=c, t=t, slot=slot, b3=b3, trk=trk):
                        V.wait_ge(SEM["pe"], trk[c])
                        V.tensor_copy(xl2T_v[:, c, t * 128:(t + 1) * 128],
                                      psD[:, b3:b3 + 128]).then_inc(SEM["dve"], 1)
                    em("vector", f_ec)
                evk_all = evk[NCI - 1]
                prev_tile_ev = evk[NCI - 1]
            carry.setdefault("_ln2_done", {})[tg] = evk_all
            if ATTN_STOP == "b1nostore":
                carry.setdefault("_agx_st", {})[tg] = cnt["s_agx"]
                continue
            post("s_agx", 16)

            def f_st(S, tg=tg, evk_all=evk_all):
                S.wait_ge(SEM["dve"], evk_all)
                S.dma_start(out=agx_in.rearrange("(c p) n -> p c n", p=128)[:, :, tg * TPR:(tg + 1) * TPR],
                            in_=xl2T_v).then_inc(SEM["s_agx"], 16)
            em("sync", f_st)
            carry.setdefault("_agx_st", {})[tg] = cnt["s_agx"]

        if ATTN_STOP in ("b1", "b1stats", "b1nostore"):
            carry["rs_done_cc"] = cnt["cc"]
            carry["rs_out"] = x1s
            carry["last_store_sq"] = [("s_agx", cnt["s_agx"])]
            carry["last_xev_dve"] = cnt["dve"]
            carry["prev_sd"] = None
            carry["prev2_sd"] = None
            return carry
        post("cc", 1)
        ag_done = cnt["cc"]
        agx_st_last = carry["_agx_st"][3]

        def f_ag(G, agx_st_last=agx_st_last):
            G.wait_ge(SEM["s_agx"], agx_st_last)
            G.collective_compute("AllGather", ALU.bypass,
                                 replica_groups=[[0, 1], [2, 3], [4, 5], [6, 7]],
                                 ins=[agx_in.opt()], outs=[agx_out.opt()]).then_inc(SEM["cc"], 1)
        em("gpsimd", f_ag)

        if ATTN_STOP == "b1ag":
            carry["rs_done_cc"] = ag_done
            carry["rs_out"] = x1s
            carry["rs_ready"] = ("cc", 0)
            carry["last_store_sq"] = [("s_agx", cnt["s_agx"])]
            carry["last_xev_dve"] = cnt["dve"]
            carry["prev_sd"] = None
            carry["prev2_sd"] = None
            return carry
        wnat = wgu_flat[:, 0:16384].rearrange("p (c n) -> p c n", c=8)
        wonat = wgu_flat[:, 16384:32768].rearrange("p (c n) -> p c n", c=16)
        wstage = wdb_flat[:, 0:1024]
        wl = {}
        for mi, (wsrc, rows) in enumerate([(wq, 1024), (wk, 1024), (wv, 1024), (wo, D)]):
            post("g_attw", 16)

            def f_wl(G, mi=mi, wsrc=wsrc, rows=rows):
                if mi > 0:
                    G.wait_ge(SEM["pe"], wl[mi - 1][1])
                if mi == 0 and carry.get("_wT_wait_pe") is not None:
                    G.wait_ge(SEM["pe"], carry["_wT_wait_pe"])
                if rows == 1024:
                    G.dma_start(out=wnat, in_=wsrc.rearrange("(c p) n -> p c n", p=128)
                                ).then_inc(SEM["g_attw"], 16)
                else:
                    G.dma_start(out=wonat, in_=wsrc.rearrange("(c p) n -> p c n", p=128)
                                ).then_inc(SEM["g_attw"], 16)
            em("gpsimd", f_wl)
            ld = cnt["g_attw"]
            if mi < 3:
                dst = [wqT_s, wkT_s, wvT_s][mi]
                last_pe = None
                for c in range(NCI):
                    tpe = post("pe")

                    def f_tr(T, mi=mi, c=c, ld=ld, tpe=tpe,
                             pv=wl.get((mi - 1, 15)) if mi >= 1 else None):
                        if c == 0:
                            T.wait_ge(SEM["g_attw"], ld)
                            if pv is not None:
                                T.wait_ge(SEM["dve"], pv)
                        if c >= 1:
                            T.wait_ge(SEM["dve"], wl[(mi, c - 1)])
                        last = None
                        for hc in range(8):
                            last = T.transpose(psTr[:, hc * 128:(hc + 1) * 128],
                                               wnat[:, hc, c * 128:(c + 1) * 128], ident_bf[:, :])
                        last.then_inc(SEM["pe"], 1)
                    em("tensor", f_tr)
                    wl[(mi, c)] = post("dve")

                    def f_ev(V, tpe=tpe, pst=wl.get(("st", mi, c - 1)) if c > 0 else
                             (wl.get(("st", mi - 1, 15)) if mi > 0 else None)):
                        V.wait_ge(SEM["pe"], tpe)
                        if pst is not None:
                            V.wait_ge(SEM["s_wT"], pst)
                        V.tensor_copy(wstage, psTr[:, 0:1024]).then_inc(SEM["dve"], 1)
                    em("vector", f_ev)
                    post("s_wT", 16)

                    def f_sd(S, dst=dst, c=c, ev=wl[(mi, c)]):
                        S.wait_ge(SEM["dve"], ev)
                        S.dma_start(out=dst[c * 128:(c + 1) * 128, :], in_=wstage
                                    ).then_inc(SEM["s_wT"], 16)
                    em("sync", f_sd)
                    wl[("st", mi, c)] = cnt["s_wT"]
                    last_pe = tpe
                wl[mi] = (ld, last_pe)
            else:
                last_pe = None
                wost = wdb_flat[:, 1024:3072]
                for jc in range(8):
                    tpe = post("pe")

                    def f_tr(T, jc=jc, ld=ld, tpe=tpe,
                             pv=wl.get((2, 15))):
                        if jc == 0:
                            T.wait_ge(SEM["g_attw"], ld)
                            if pv is not None:
                                T.wait_ge(SEM["dve"], pv)
                        if jc >= 1:
                            T.wait_ge(SEM["dve"], wl[(3, jc - 1, 1)])
                        last = None
                        for oc in range(8):
                            last = T.transpose(psTr[:, oc * 128:(oc + 1) * 128],
                                               wonat[:, oc, jc * 128:(jc + 1) * 128], ident_bf[:, :])
                        last.then_inc(SEM["pe"], 1)
                    em("tensor", f_tr)
                    e1 = post("dve")
                    wl[(3, jc, 0)] = e1

                    def f_ev(V, tpe=tpe, jc=jc, pst=wl.get(("sto", jc - 1))):
                        V.wait_ge(SEM["pe"], tpe)
                        if pst is not None:
                            V.wait_ge(SEM["s_wT"], pst)
                        V.tensor_copy(wost[:, 0:1024], psTr[:, 0:1024]).then_inc(SEM["dve"], 1)
                    em("vector", f_ev)
                    tpe2 = post("pe")

                    def f_tr2(T, jc=jc, tpe2=tpe2, e1=e1):
                        T.wait_ge(SEM["dve"], e1)
                        last = None
                        for oc in range(8, 16):
                            last = T.transpose(psTr[:, (oc - 8) * 128:(oc - 7) * 128],
                                               wonat[:, oc, jc * 128:(jc + 1) * 128], ident_bf[:, :])
                        last.then_inc(SEM["pe"], 1)
                    em("tensor", f_tr2)
                    e2 = post("dve")
                    wl[(3, jc, 1)] = e2

                    def f_ev2(V, tpe2=tpe2):
                        V.wait_ge(SEM["pe"], tpe2)
                        V.tensor_copy(wost[:, 1024:2048], psTr[:, 0:1024]).then_inc(SEM["dve"], 1)
                    em("vector", f_ev2)
                    post("s_wT", 16)

                    def f_sd(S, jc=jc, e2=e2):
                        S.wait_ge(SEM["dve"], e2)
                        S.dma_start(out=woT_s[jc * 128:(jc + 1) * 128, :], in_=wost
                                    ).then_inc(SEM["s_wT"], 16)
                    em("sync", f_sd)
                    wl[("sto", jc)] = cnt["s_wT"]
                    last_pe = tpe2
                wl[3] = (ld, last_pe)
        wT_done_sq = cnt["s_wT"]

        slab = wgu_flat[:, 0:32768].rearrange("p (c n) -> p c n", c=NCI)
        wsl = wdb_flat[:, 0:4096].rearrange("p (s n) -> p s n", s=2)
        qstg = wdb_flat[:, 4096:8192].rearrange("p (s n) -> p s n", s=2)
        b3_store = {}
        prev_ev = None
        prev_mm = None
        for tb in range(2):
            post("s_slab", 16)

            def f_sl(S, tb=tb, prev_mm=prev_mm):
                S.wait_ge(SEM["cc"], ag_done)
                S.wait_ge(SEM["s_wT"], wT_done_sq)
                if prev_mm is not None:
                    S.wait_ge(SEM["pe"], prev_mm)
                S.dma_start(out=slab,
                            in_=agx_out[tb * D:(tb + 1) * D, :].rearrange("(c p) n -> p c n", p=128)
                            ).then_inc(SEM["s_slab"], 16)
            em("sync", f_sl)
            slab_ld = cnt["s_slab"]
            for h in range(8):
                for xi, (wTs, dstq) in enumerate([(wqT_s, qT_sd), (wkT_s, kT_sd), (wvT_s, vT_sd)]):
                    si = (h * 3 + xi) % 2
                    ldsem = "s_ld3a" if si == 0 else "s_ld3b"
                    post(ldsem, 16)

                    def f_wv(S, wTs=wTs, h=h, si=si, ldsem=ldsem,
                             g=b3_store.get(("mmslot", si))):
                        if g is not None:
                            S.wait_ge(SEM["pe"], g)
                        S.dma_start(out=wsl[:, si, :].rearrange("p (c n) -> p c n", c=NCI),
                                    in_=wTs[:, h * 128:(h + 1) * 128]
                                    .rearrange("(c p) n -> p c n", p=128)
                                    ).then_inc(SEM[ldsem], 16)
                    em("sync", f_wv)
                    wld = (ldsem, cnt[ldsem])
                    mm = post("pe")

                    def f_mm(T, si=si, wld=wld, slab_ld=slab_ld, mm=mm, prev_ev=prev_ev):
                        T.wait_ge(SEM[wld[0]], wld[1])
                        T.wait_ge(SEM["s_slab"], slab_ld)
                        if prev_ev is not None:
                            T.wait_ge(SEM["dve"], prev_ev)
                        last = None
                        for c in range(NCI):
                            ws = wsl[:, si, c * 128:(c + 1) * 128]
                            for qq in range(4):
                                last = T.matmul(psD[:, qq * 512:(qq + 1) * 512], ws,
                                                slab[:, c, qq * 512:(qq + 1) * 512],
                                                start=(c == 0), stop=(c == NCI - 1))
                        last.then_inc(SEM["pe"], 1)
                    em("tensor", f_mm)
                    b3_store[("mmslot", si)] = mm
                    ev = post("dve")

                    def f_ev(V, mm=mm, si=si, pst=b3_store.get(("stslot", si))):
                        V.wait_ge(SEM["pe"], mm)
                        if pst is not None:
                            V.wait_ge(SEM["s_qkva" if si == 0 else "s_qkvb"], pst)
                        V.tensor_copy(qstg[:, si, :], psD[:, :]).then_inc(SEM["dve"], 1)
                    em("vector", f_ev)
                    prev_ev = ev
                    prev_mm = mm
                    qsem = "s_qkva" if si == 0 else "s_qkvb"
                    post(qsem, 16)

                    def f_qs(S, dstq=dstq, h=h, tb=tb, ev=ev, si=si, qsem=qsem):
                        S.wait_ge(SEM["dve"], ev)
                        S.dma_start(out=dstq[h * 128:(h + 1) * 128, tb * D:(tb + 1) * D],
                                    in_=qstg[:, si, :]).then_inc(SEM[qsem], 16)
                    em("sync", f_qs)
                    b3_store[(tb, h, xi)] = cnt[qsem]
                    b3_store[("stslot", si)] = cnt[qsem]
        qkv_done_sq = (cnt["s_qkva"], cnt["s_qkvb"])
        if ATTN_STOP == "b3":
            post("g_acc", 16)

            def f_dummy(G, q=qkv_done_sq):
                G.wait_ge(SEM["s_qkva"], q[0])
                G.wait_ge(SEM["s_qkvb"], q[1])
                G.dma_start(out=rs_out, in_=x1s).then_inc(SEM["g_acc"], 16)
            em("gpsimd", f_dummy)
            carry["rs_done_cc"] = cnt["cc"]
            carry["rs_ready"] = ("g_acc", cnt["g_acc"])
            carry["rs_out"] = rs_out
            carry["last_store_sq"] = [("s_agx", cnt["s_agx"])]
            carry["last_xev_dve"] = cnt["dve"]
            carry["prev_sd"] = None
            carry["prev2_sd"] = None
            return carry

        last_zT_ev_holder = {"v": None, "zst": None}
        qTb = wdb_flat[:, 0:4096]
        kTb = wdb_flat[:, 4096:8192]
        vTb = wdb_flat[:, 8192:12288]
        vb = wgu_flat[:, 0:4096].rearrange("p (c n) -> p c n", c=32)
        PTb = wgu_flat[:, 4096:6144]
        zstg = wgu_flat[:, 38912:39936]
        P_pool = [[wgu_flat[:, 6144 + (par * 4 + i) * 4096: 6144 + (par * 4 + i + 1) * 4096]
                   for i in range(4)] for par in range(2)]
        psS = [psGU[:, 0:1024], psD[:, 0:1024]]
        psO = psGU[:, 1024:1536]
        C = lambda i: sst[:, i:i + 1]

        def emit_av(h, qg, p_done, P_bufs):
            nk = 4 * qg + 4
            npair = nk // 2
            av_mm = {}
            tr_t = {}
            ev_t = {}
            for pj in range(npair):
                sl = pj % 2
                tr_t[pj] = post("pe")

                def f_tr(T, pj=pj, p_done=p_done, P_bufs=P_bufs, ev_t=ev_t):
                    if pj == 0:
                        for pf in p_done:
                            T.wait_ge(SEM["dve"], pf)
                    if pj >= 1:
                        T.wait_ge(SEM["dve"], ev_t[pj - 1])
                    last = None
                    for half in range(2):
                        kc = pj * 2 + half
                        for i in range(4):
                            last = T.transpose(
                                psTr[:, half * 512 + i * 128:half * 512 + (i + 1) * 128],
                                P_bufs[i][:, kc * 128:(kc + 1) * 128], ident_bf[:, :])
                    last.then_inc(SEM["pe"], 1)
                em("tensor", f_tr)
                ev_t[pj] = post("dve")

                def f_ev(V, pj=pj, sl=sl, tr=tr_t[pj], av_mm=av_mm):
                    V.wait_ge(SEM["pe"], tr)
                    pv = av_mm.get(pj - 2)
                    if pv is not None:
                        V.wait_ge(SEM["pe"], pv)
                    V.tensor_copy(PTb[:, sl * 1024:(sl + 1) * 1024],
                                  psTr[:, 0:1024]).then_inc(SEM["dve"], 1)
                em("vector", f_ev)
                av_mm[pj] = post("pe")

                def f_av(T, pj=pj, sl=sl, npair=npair, ev=ev_t[pj],
                         lzv=last_zT_ev_holder["v"]):
                    T.wait_ge(SEM["dve"], ev)
                    if pj == 0 and lzv is not None:
                        T.wait_ge(SEM["dve"], lzv)
                    last = None
                    for half in range(2):
                        kc = pj * 2 + half
                        last = T.matmul(psO, vb[:, kc, :],
                                        PTb[:, sl * 1024 + half * 512:sl * 1024 + (half + 1) * 512],
                                        start=(kc == 0), stop=(kc == nk - 1))
                    last.then_inc(SEM["pe"], 1)
                em("tensor", f_av)
            zev = post("dve")
            zsl = qg % 2

            def f_z(V, qg=qg, zsl=zsl, mm=av_mm[npair - 1], pzs=last_zT_ev_holder["zst"]):
                V.wait_ge(SEM["pe"], mm)
                if pzs is not None:
                    V.wait_ge(SEM["s_zt"], pzs)
                V.tensor_copy(zstg[:, zsl * 512:(zsl + 1) * 512], psO).then_inc(SEM["dve"], 1)
            em("vector", f_z)
            post("s_zt", 16)

            def f_zs(S, h=h, qg=qg, zsl=zsl, zev=zev):
                S.wait_ge(SEM["dve"], zev)
                S.dma_start(out=zT_sd[h * 128:(h + 1) * 128, qg * 512:(qg + 1) * 512],
                            in_=zstg[:, zsl * 512:(zsl + 1) * 512]).then_inc(SEM["s_zt"], 16)
            em("sync", f_zs)
            last_zT_ev_holder["v"] = zev
            last_zT_ev_holder["zst"] = cnt["s_zt"]
            return tr_t[npair - 1]

        last_head_S_pe = None
        prev_head_vdve = None
        gk_hist = {}
        gk = 0
        P_free = {0: None, 1: None}
        final_zst = None
        for h in range(8):
            post("s_hld", 16 * 3)
            prev_S = last_head_S_pe

            def f_ld(S, h=h, prev_S=prev_S, pvd=prev_head_vdve):
                if h == 0:
                    S.wait_ge(SEM["s_qkva"], qkv_done_sq[0])
                    S.wait_ge(SEM["s_qkvb"], qkv_done_sq[1])
                else:
                    S.wait_ge(SEM["pe"], prev_S)
                    S.wait_ge(SEM["dve"], pvd)
                S.dma_start(out=qTb, in_=qT_sd[h * 128:(h + 1) * 128, :]).then_inc(SEM["s_hld"], 16)
                S.dma_start(out=kTb, in_=kT_sd[h * 128:(h + 1) * 128, :]).then_inc(SEM["s_hld"], 16)
                S.dma_start(out=vTb, in_=vT_sd[h * 128:(h + 1) * 128, :]).then_inc(SEM["s_hld"], 16)
            em("sync", f_ld)
            ld_t = cnt["s_hld"]
            vtr = {}
            for b4 in range(4):
                tpe = post("pe")

                def f_vt(T, b4=b4, ld_t=ld_t, h=h, vtr=vtr, lz=last_zT_ev_holder["v"]):
                    if b4 == 0:
                        T.wait_ge(SEM["s_hld"], ld_t)
                        if lz is not None:
                            T.wait_ge(SEM["dve"], lz)
                    if b4 >= 1:
                        T.wait_ge(SEM["dve"], vtr[b4 - 1])
                    last = None
                    for j in range(8):
                        kc = b4 * 8 + j
                        last = T.transpose(psTr[:, j * 128:(j + 1) * 128],
                                           vTb[:, kc * 128:(kc + 1) * 128], ident_bf[:, :])
                    last.then_inc(SEM["pe"], 1)
                em("tensor", f_vt)
                vtr[b4] = post("dve")

                def f_ve(V, b4=b4, tpe=tpe):
                    V.wait_ge(SEM["pe"], tpe)
                    V.tensor_copy(wgu_flat[:, b4 * 1024:(b4 + 1) * 1024],
                                  psTr[:, 0:1024]).then_inc(SEM["dve"], 1)
                em("vector", f_ve)
            v_ready = vtr[3]
            prev_head_vdve = v_ready

            pend = None
            for qg in range(8):
                par = qg % 2
                P_bufs = P_pool[par]
                p_done = []
                for qt4 in range(4):
                    qt = qg * 4 + qt4
                    klen = 128 * (qt + 1)
                    kgs = (klen + 1023) // 1024
                    Pb = P_bufs[qt4]
                    smms = {}
                    exp_t = {}
                    for kg in range(kgs):
                        w = min(1024, klen - kg * 1024)
                        sset = gk % 2
                        prev_exp = gk_hist.get(gk - 2)
                        smms[kg] = post("pe")

                        def f_s(T, qt=qt, kg=kg, w=w, sset=sset, v_ready=v_ready,
                                prev_exp=prev_exp):
                            if qt == 0 and kg == 0:
                                T.wait_ge(SEM["dve"], v_ready)
                            if prev_exp is not None:
                                T.wait_ge(SEM["act"], prev_exp)
                            last = None
                            for off in range(0, w, 512):
                                ww = min(512, w - off)
                                last = T.matmul(psS[sset][:, off:off + ww],
                                                qTb[:, qt * 128:(qt + 1) * 128],
                                                kTb[:, kg * 1024 + off:kg * 1024 + off + ww],
                                                start=True, stop=True)
                            last.then_inc(SEM["pe"], 1)
                        em("tensor", f_s)
                        mx_d = post("dve")

                        def f_mx(V, kg=kg, w=w, sset=sset, smm=smms[kg]):
                            V.wait_ge(SEM["pe"], smm)
                            V.reduce_max(C(kg), psS[sset][:, 0:w], axis=AX
                                         ).then_inc(SEM["dve"], 1)
                        em("vector", f_mx)
                        mg_d = post("dve")

                        def f_ng(V, kg=kg, mx_d=mx_d):
                            V.wait_ge(SEM["dve"], mx_d)
                            V.tensor_scalar_mul(C(4 + kg), C(kg), -0.08838834764831845
                                                ).then_inc(SEM["dve"], 1)
                        em("vector", f_ng)
                        exp_t[kg] = post("act")
                        pfree = P_free[par]

                        def f_ex(S, kg=kg, w=w, sset=sset, mg_d=mg_d, Pb=Pb, pfree=pfree):
                            S.wait_ge(SEM["dve"], mg_d)
                            if kg == 0 and pfree is not None:
                                S.wait_ge(SEM["pe"], pfree)
                            S.activation(Pb[:, kg * 1024:kg * 1024 + w], psS[sset][:, 0:w],
                                         AF.Exp, bias=C(4 + kg), scale=0.08838834764831845,
                                         accum_out=C(8 + kg)).then_inc(SEM["act"], 1)
                        em("scalar", f_ex)
                        gk_hist[gk] = exp_t[kg]
                        gk += 1
                    dk = kgs - 1
                    dw = klen - dk * 1024
                    mk_d = post("dve")

                    def f_msk(V, qt=qt, Pb=Pb, et=exp_t[dk]):
                        V.wait_ge(SEM["act"], et)
                        V.tensor_tensor(Pb[:, qt * 128:qt * 128 + 128],
                                        Pb[:, qt * 128:qt * 128 + 128], tri01[:, :], ALU.mult
                                        ).then_inc(SEM["dve"], 1)
                    em("vector", f_msk)
                    post("dve")

                    def f_dsum(V, dk=dk, dw=dw, Pb=Pb, mk_d=mk_d):
                        V.wait_ge(SEM["dve"], mk_d)
                        V.reduce_sum(C(8 + dk), Pb[:, dk * 1024:dk * 1024 + dw], axis=AX
                                     ).then_inc(SEM["dve"], 1)
                    em("vector", f_dsum)

                    gmax = 128 * (4 * qg + 4)
                    if klen < gmax:
                        post("dve")

                        def f_zz(V, Pb=Pb, klen=klen, gmax=gmax, prev_tgt=cnt["dve"] - 1):
                            V.wait_ge(SEM["dve"], prev_tgt)
                            V.memset(Pb[:, klen:gmax], 0.0).then_inc(SEM["dve"], 1)
                        em("vector", f_zz)

                    def em_v2(fn, wait_act=None):
                        tgt = post("dve")

                        def f(V, fn=fn, wait_act=wait_act, prev_tgt=tgt - 1):
                            V.wait_ge(SEM["dve"], prev_tgt)
                            if wait_act is not None:
                                V.wait_ge(SEM["act"], wait_act)
                            fn(V).then_inc(SEM["dve"], 1)
                        em("vector", f)
                        return tgt

                    if kgs == 1:
                        em_v2(lambda V: V.reciprocal(C(22), C(8)))
                        pf = em_v2(lambda V, Pb=Pb, klen=klen:
                                   V.tensor_scalar(Pb[:, 0:klen], Pb[:, 0:klen], C(22), None,
                                                   ALU.mult))
                    else:
                        em_v2(lambda V, kgs=kgs: V.reduce_max(C(20), sst[:, 0:kgs], axis=AX))
                        for kg in range(kgs):
                            em_v2(lambda V, kg=kg: V.tensor_tensor(C(12 + kg), C(kg), C(20),
                                                                   ALU.subtract))
                        am_t = post("act")
                        amw = cnt["dve"]

                        def f_am(S, kgs=kgs, amw=amw):
                            S.wait_ge(SEM["dve"], amw)
                            last = None
                            for kg in range(kgs):
                                last = S.activation(C(12 + kg), C(12 + kg), AF.Exp,
                                                    scale=0.08838834764831845)
                            last.then_inc(SEM["act"], 1)
                        em("scalar", f_am)
                        em_v2(lambda V: V.tensor_tensor(C(21), C(12), C(8), ALU.mult),
                              wait_act=am_t)
                        for kg in range(1, kgs):
                            em_v2(lambda V, kg=kg: V.tensor_tensor(C(23), C(12 + kg),
                                                                   C(8 + kg), ALU.mult))
                            em_v2(lambda V: V.tensor_tensor(C(21), C(21), C(23), ALU.add))
                        em_v2(lambda V: V.reciprocal(C(22), C(21)))
                        pf = None
                        for kg in range(kgs):
                            w = min(1024, klen - kg * 1024)
                            em_v2(lambda V, kg=kg: V.tensor_tensor(C(16 + kg), C(12 + kg),
                                                                   C(22), ALU.mult))
                            pf = em_v2(lambda V, kg=kg, w=w, Pb=Pb:
                                       V.tensor_scalar(Pb[:, kg * 1024:kg * 1024 + w],
                                                       Pb[:, kg * 1024:kg * 1024 + w],
                                                       C(16 + kg), None, ALU.mult))
                    p_done.append(pf)
                if pend is not None:
                    P_free[pend[0] % 2] = emit_av(h, pend[0], pend[1], P_pool[pend[0] % 2])
                pend = (qg, p_done)
            P_free[pend[0] % 2] = emit_av(h, pend[0], pend[1], P_pool[pend[0] % 2])
            pend = None
            last_head_S_pe = smms[kgs - 1]
        attn_done_dve = last_zT_ev_holder["v"]
        zT_done_sq = last_zT_ev_holder["zst"]

        if ATTN_STOP == "b4":
            post("g_acc", 16)

            def f_dummy(G, q=last_zT_ev_holder["zst"]):
                G.wait_ge(SEM["s_zt"], q)
                G.dma_start(out=rs_out, in_=x1s).then_inc(SEM["g_acc"], 16)
            em("gpsimd", f_dummy)
            carry["rs_done_cc"] = cnt["cc"]
            carry["rs_ready"] = ("g_acc", cnt["g_acc"])
            carry["rs_out"] = rs_out
            carry["last_store_sq"] = [("s_agx", cnt["s_agx"])]
            carry["last_xev_dve"] = cnt["dve"]
            carry["prev_sd"] = None
            carry["prev2_sd"] = None
            return carry

        woT_res = wdb_flat[:, 0:16384].rearrange("p (z n) -> p z n", z=8)
        zT_v = wgu_flat[:, 6144:38912].rearrange("p (h n) -> p h n", h=8)
        post("s_hld", 32)

        def f_wo(S, attn_done_dve=attn_done_dve, lsp=last_head_S_pe, zds=zT_done_sq):
            S.wait_ge(SEM["dve"], attn_done_dve)
            S.wait_ge(SEM["pe"], lsp)
            S.wait_ge(SEM["s_zt"], zds)
            S.dma_start(out=woT_res,
                        in_=woT_s.rearrange("(z p) n -> p z n", p=128)).then_inc(SEM["s_hld"], 16)
            S.dma_start(out=zT_v,
                        in_=zT_sd.rearrange("(h p) n -> p h n", p=128)).then_inc(SEM["s_hld"], 16)
        em("sync", f_wo)
        wo_ld = cnt["s_hld"]
        rs_stage = xt[:, :].rearrange("p (s n) -> p s n", s=NT)
        prev_ev5 = {}
        last_rs_sq = None
        for tt in range(32):
            sl = tt % 2
            mm5 = post("pe")

            def f_m5(T, tt=tt, sl=sl, wo_ld=wo_ld, prev_ev5=prev_ev5):
                if tt == 0:
                    T.wait_ge(SEM["s_hld"], wo_ld)
                pv = prev_ev5.get(tt - 1)
                if pv is not None:
                    T.wait_ge(SEM["dve"], pv)
                last = None
                for zc in range(8):
                    zs = zT_v[:, zc, tt * 128:(tt + 1) * 128]
                    for qq in range(4):
                        last = T.matmul(psD[:, qq * 512:(qq + 1) * 512], zs,
                                        woT_res[:, zc, qq * 512:(qq + 1) * 512],
                                        start=(zc == 0), stop=(zc == 7))
                last.then_inc(SEM["pe"], 1)
            em("tensor", f_m5)
            ev5 = post("dve")

            def f_e5(V, tt=tt, sl=sl, mm5=mm5):
                V.wait_ge(SEM["pe"], mm5)
                if tt >= 2:
                    V.wait_ge(SEM[f"s_rs{sl}"], carry["_rs_st"][tt - 2])
                V.tensor_copy(rs_stage[:, sl, :], psD[:, :]).then_inc(SEM["dve"], 1)
            em("vector", f_e5)
            prev_ev5[tt] = ev5
            post(f"s_rs{sl}", 16)

            def f_s5(S, tt=tt, sl=sl, ev5=ev5):
                S.wait_ge(SEM["dve"], ev5)
                S.dma_start(out=rs_in[tt * 128:(tt + 1) * 128, :], in_=rs_stage[:, sl, :]
                            ).then_inc(SEM[f"s_rs{sl}"], 16)
            em("sync", f_s5)
            carry.setdefault("_rs_st", {})[tt] = cnt[f"s_rs{sl}"]
            last_rs_sq = (cnt["s_rs0"], cnt["s_rs1"])

        post("cc", 1)
        rs_done = cnt["cc"]

        def f_rs(G, last_rs_sq=last_rs_sq):
            G.wait_ge(SEM["s_rs0"], last_rs_sq[0])
            G.wait_ge(SEM["s_rs1"], last_rs_sq[1])
            G.collective_compute("ReduceScatter", ALU.add,
                                 replica_groups=[[0, 1], [2, 3], [4, 5], [6, 7]],
                                 ins=[rs_in.opt()], outs=[rs_out.opt()]).then_inc(SEM["cc"], 1)
        em("gpsimd", f_rs)
        carry["rs_done_cc"] = rs_done
        carry["rs_ready"] = ("cc", rs_done)
        carry["rs_out"] = rs_out
        carry["last_xev_dve"] = prev_ev5[31]
        carry["last_store_sq"] = [("s_rs0", cnt["s_rs0"]), ("s_rs1", cnt["s_rs1"])]
        carry["prev_sd"] = None
        carry["prev2_sd"] = None
        return carry

    def emit_ffn(ffn_idx, dst, carry):
        for r in range(NR):
            use_cache = not (ffn_idx == 0 and r == 0)
            post("s_x", 16)
            ls = carry.get("last_store_sq")

            gwpe = gwtr[NCI - 1] if (ffn_idx == 0 and r == 0) else None

            def f_x(S, r=r, ls=ls, ffn_idx=ffn_idx, gwpe=gwpe):
                if ls is not None:
                    for sname, v in ls:
                        S.wait_ge(SEM[sname], v)
                if gwpe is not None:
                    S.wait_ge(SEM["pe"], gwpe)
                src = x_in if ffn_idx == 0 else x1s
                S.dma_start(out=xt[:, :].rearrange("p (t n) -> p t n", t=NT),
                            in_=src[r * TPR:(r + 1) * TPR, :].rearrange("(t p) n -> p t n", p=128)
                            ).then_inc(SEM["s_x"], 16)
            em("sync", f_x)
            x_ready = ("s_x", cnt["s_x"])
            if ffn_idx == 1:
                post("g_acc", 16)
                xr = x_ready

                def f_x2(G, r=r, xr=xr, rsr=carry.get("rs_ready")):
                    G.wait_ge(SEM[xr[0]], xr[1])
                    if rsr is not None and rsr[1]:
                        G.wait_ge(SEM[rsr[0]], rsr[1])
                    G.dma_start(out=xt[:, :].rearrange("p (t n) -> p t n", t=NT),
                                in_=carry["rs_out"][r * TPR:(r + 1) * TPR, :]
                                .rearrange("(t p) n -> p t n", p=128),
                                accum_op=ALU.add).then_inc(SEM["g_acc"], 16)
                em("gpsimd", f_x2)
                x_ready = ("g_acc", cnt["g_acc"])

            if ffn_idx == 0 and r == 0:
                pf_dve, pf_pe = gwT_ready_dve, None
            else:
                pf_dve, pf_pe = carry["last_xev_dve"], None
            cmb_ready, lg_free = emit_ln_round(ffn_idx, r, x_ready, pf_dve, pf_pe)
            if DEBUG and ffn_idx == 0 and r == 0:
                emit_dbg_snapshot("dve", cmb_ready, dbg_xlnT, xlnT[:, :])
                emit_dbg_snapshot("dve", cmb_ready, dbg_lgb, lgb[:, :])
                emit_dbg_snapshot("dve", cmb_ready, dbg_cmb, cmb[:, :])

            for e in range(NEXP):
                key = (ffn_idx, r, e)
                wgu_tgts = carry["wgu_tgts"][key]
                for t in range(NT):
                    step = carry["step"]
                    sd = dict(e=e, t=t, slot=step % 2, wgu_tgts=wgu_tgts)
                    prev_sd = carry.get("prev_sd")
                    prev2_sd = carry.get("prev2_sd")
                    flushed = prev_sd is None or prev_sd.get("_emitted_lag", False)

                    emit_g(sd, prev_sd["h"] if prev_sd else None, cmb_ready)
                    emit_silu(sd, prev2_sd["tr2"] if prev2_sd and "tr2" in prev2_sd else None)
                    if DEBUG and ffn_idx == 0 and r == 0 and e == 0 and t == 0:
                        sd["_dbg_gs"] = emit_dbg_snapshot("act", sd["silu"], dbg_gs,
                                                          gs[:, sd["slot"] * I:(sd["slot"] + 1) * I])
                    if not flushed:
                        emit_lagged(prev_sd, carry)
                    emit_u(sd)
                    if not flushed:
                        emit_dnB(prev_sd)
                        emit_xevac(prev_sd)
                        carry["last_xev_dve"] = prev_sd["xev"]
                        prev_sd["_emitted_lag"] = True
                        if DEBUG and prev_sd.get("_dbg_h"):
                            emit_dbg_snapshot("dve", prev_sd["ev2"], dbg_hT,
                                              hT[:, prev_sd["slot"] * NIC * 128:
                                                 (prev_sd["slot"] + 1) * NIC * 128])
                            emit_dbg_snapshot("dve", prev_sd["xev"], dbg_xt,
                                              xt[:, 0:D])
                    if DEBUG and sd.get("_dbg_gs"):
                        sd["_h_extra_wait"] = sd["_dbg_gs"]
                    emit_h(sd, cmb_ready)
                    if DEBUG and ffn_idx == 0 and r == 0 and e == 0 and t == 0:
                        sd["_dbg_h"] = emit_dbg_snapshot("dve", sd["h"], dbg_h,
                                                         gs[:, sd["slot"] * I:(sd["slot"] + 1) * I])

                    if t == 0:
                        wd_key = carry["wd_tgts"].get(key)
                        if wd_key is None:
                            wp = prev_sd["dnB"] if (prev_sd and "dnB" in prev_sd) else None
                            ws = carry.get("cache_sq")
                            carry["wd_tgts"][key] = emit_wd_dma(use_cache, e, wp, ws)
                    sd["wd_tgts"] = carry["wd_tgts"][key]
                    sd["_psTr_free"] = prev_sd["ev2"] if (prev_sd and "ev2" in prev_sd) else None
                    sd["_hT_free"] = (prev2_sd["dnB"] if prev2_sd and "dnB" in prev2_sd else None)
                    sd["_psD_free"] = (prev_sd["xev"] if (prev_sd and "xev" in prev_sd
                                       and not prev_sd.get("_flush_boundary")) else lg_free)
                    carry["prev2_sd"] = prev_sd
                    carry["prev_sd"] = sd
                    carry["step"] = step + 1
                nf, nr_, ne = ffn_idx, r, e + 1
                if ne == NEXP:
                    ne, nr_ = 0, r + 1
                    if nr_ == NR:
                        nr_, nf = 0, ffn_idx + 1
                if not use_cache:
                    carry["cache_sq"] = emit_cache_write(
                        e, wgu_tgts, carry["wd_tgts"][key])
                if nf == ffn_idx:
                    nuc = not (nf == 0 and nr_ == 0)
                    carry["wgu_tgts"][(nf, nr_, ne)] = emit_wgu_dma(
                        nuc, ne, cnt["pe"], carry.get("cache_sq"))

            psd_ = carry["prev_sd"]
            emit_lagged(psd_, carry)
            emit_dnB(psd_)
            emit_xevac(psd_)
            psd_["_emitted_lag"] = True
            psd_["_flush_boundary"] = True
            carry["last_xev_dve"] = psd_["xev"]

            post("s_stx", 16)
            lxv = carry["last_xev_dve"]

            def f_s(S, r=r, dst=dst, lxv=lxv):
                S.wait_ge(SEM["dve"], lxv)
                S.dma_start(out=dst[r * TPR:(r + 1) * TPR, :].rearrange("(t p) n -> p t n", p=128),
                            in_=xt[:, :].rearrange("p (t n) -> p t n", t=NT)
                            ).then_inc(SEM["s_stx"], 16)
            em("sync", f_s)
            carry["last_store_sq"] = [("s_stx", cnt["s_stx"])]
        return carry

    carry = {"step": 0, "wgu_tgts": {}, "wd_tgts": {}, "cache_sq": None,
             "prev_sd": None, "prev2_sd": None, "last_xev_dve": None,
             "last_store_sq": None}
    carry["wgu_tgts"][(0, 0, 0)] = emit_wgu_dma(False, 0, None, None)
    carry["wd_tgts"][(0, 0, 0)] = emit_wd_dma(False, 0, None, None)

    dst1 = y_out if PHASES == "ffn1" else x1s
    carry = emit_ffn(0, dst1, carry)

    if PHASES in ("attn", "full"):
        carry["_wT_wait_pe"] = cnt["pe"]
        carry = emit_attention(carry)
        if PHASES == "attn":
            for r in range(NR):
                post("s_x", 16)
                ls = carry.get("last_store_sq")

                def f_xa(S, r=r, ls=ls, lxd=carry.get("last_xev_dve")):
                    if ls is not None:
                        for sname, v in ls:
                            S.wait_ge(SEM[sname], v)
                    if r == 0 and lxd is not None:
                        S.wait_ge(SEM["dve"], lxd)
                    if r > 0:
                        S.wait_ge(SEM["s_stx"], carry["_at_st"][r - 1])
                    S.dma_start(out=xt[:, :].rearrange("p (t n) -> p t n", t=NT),
                                in_=x1s[r * TPR:(r + 1) * TPR, :]
                                .rearrange("(t p) n -> p t n", p=128)).then_inc(SEM["s_x"], 16)
                em("sync", f_xa)
                xl = cnt["s_x"]
                post("g_acc", 16)

                def f_xb(G, r=r, xl=xl, rsr=carry["rs_ready"]):
                    G.wait_ge(SEM["s_x"], xl)
                    if rsr[1]:
                        G.wait_ge(SEM[rsr[0]], rsr[1])
                    G.dma_start(out=xt[:, :].rearrange("p (t n) -> p t n", t=NT),
                                in_=carry["rs_out"][r * TPR:(r + 1) * TPR, :]
                                .rearrange("(t p) n -> p t n", p=128),
                                accum_op=ALU.add).then_inc(SEM["g_acc"], 16)
                em("gpsimd", f_xb)
                xg = cnt["g_acc"]
                post("s_stx", 16)

                def f_xs(S, r=r, xg=xg):
                    S.wait_ge(SEM["g_acc"], xg)
                    S.dma_start(out=y_out[r * TPR:(r + 1) * TPR, :]
                                .rearrange("(t p) n -> p t n", p=128),
                                in_=xt[:, :].rearrange("p (t n) -> p t n", t=NT)
                                ).then_inc(SEM["s_stx"], 16)
                em("sync", f_xs)
                carry.setdefault("_at_st", {})[r] = cnt["s_stx"]
                carry["last_store_sq"] = [("s_stx", cnt["s_stx"])]
        else:
            carry["wgu_tgts"][(1, 0, 0)] = emit_wgu_dma(True, 0, cnt["pe"], None)
            carry["wd_tgts"][(1, 0, 0)] = emit_wd_dma(True, 0, cnt["pe"], None)
            carry = emit_ffn(1, y_out, carry)

    fins = {k: v for k, v in cnt.items()
            if v > 0 and k not in ("pe", "act", "dve", "cc")}
    fin_cc = cnt["cc"]

    def f_fin(G, fins=fins, fin_cc=fin_cc):
        for k, v in fins.items():
            G.wait_ge(SEM[k], v)
        if fin_cc:
            G.wait_ge(SEM["cc"], fin_cc)
    em("gpsimd", f_fin)

    with nc.Block() as blk:
        blk.sync(lambda E: [f(E) for f in streams["sync"]])
        blk.gpsimd(lambda E: [f(E) for f in streams["gpsimd"]])
        blk.tensor(lambda E: [f(E) for f in streams["tensor"]])
        blk.vector(lambda E: [f(E) for f in streams["vector"]])
        blk.scalar(lambda E: [f(E) for f in streams["scalar"]])

    fes.close()
    es.close()
    return nc




from concourse.bass_utils import run_bass_kernel_spmd


def kernel(L0_attn_x, L0_attn_Wq, L0_attn_Wk, L0_attn_Wv, L0_attn_Wo,
           L0_ffn_gate_w, L0_ffn_w_gate, L0_ffn_w_up, L0_ffn_w_down,
           L0_ffn_sw_gate, L0_ffn_sw_up, L0_ffn_sw_down, L0_ffn_shared_gate_w):
    xs = np.ascontiguousarray(np.asarray(L0_attn_x, dtype=np.float32))
    wq = np.ascontiguousarray(np.asarray(L0_attn_Wq, dtype=np.float32))
    wk = np.ascontiguousarray(np.asarray(L0_attn_Wk, dtype=np.float32))
    wv = np.ascontiguousarray(np.asarray(L0_attn_Wv, dtype=np.float32))
    wo = np.ascontiguousarray(np.asarray(L0_attn_Wo, dtype=np.float32))
    gate_w = np.ascontiguousarray(np.asarray(L0_ffn_gate_w, dtype=np.float32))
    w_gate = np.ascontiguousarray(np.asarray(L0_ffn_w_gate, dtype=np.float32))
    w_up = np.ascontiguousarray(np.asarray(L0_ffn_w_up, dtype=np.float32))
    w_down = np.ascontiguousarray(np.asarray(L0_ffn_w_down, dtype=np.float32))
    sw_gate = np.ascontiguousarray(np.asarray(L0_ffn_sw_gate, dtype=np.float32))
    sw_up = np.ascontiguousarray(np.asarray(L0_ffn_sw_up, dtype=np.float32))
    sw_down = np.ascontiguousarray(np.asarray(L0_ffn_sw_down, dtype=np.float32))
    sgw = np.ascontiguousarray(np.asarray(L0_ffn_shared_gate_w, dtype=np.float32))

    nc = build_program()
    in_maps = []
    for core in range(8):
        b, hf = core // 2, core % 2
        in_maps.append({
            "x": np.ascontiguousarray(xs[b, hf * 2048:(hf + 1) * 2048, :]),
            "gate_w": gate_w, "sgw": sgw,
            "w_gate": w_gate, "w_up": w_up, "w_down": w_down,
            "sw_gate": sw_gate, "sw_up": sw_up, "sw_down": sw_down,
            "wq": np.ascontiguousarray(wq[hf * 1024:(hf + 1) * 1024, :]),
            "wk": np.ascontiguousarray(wk[hf * 1024:(hf + 1) * 1024, :]),
            "wv": np.ascontiguousarray(wv[hf * 1024:(hf + 1) * 1024, :]),
            "wo": np.ascontiguousarray(wo[:, hf * 1024:(hf + 1) * 1024]),
        })
    res = run_bass_kernel_spmd(nc, in_maps, core_ids=list(range(8)))
    out = np.zeros((4, 4096, 2048), np.float32)
    for core in range(8):
        b, hf = core // 2, core % 2
        out[b, hf * 2048:(hf + 1) * 2048, :] = res.results[core]["y"]
    return out



# version 03
# speedup vs baseline: 1.0632x; 1.0632x over previous
import numpy as np
import ml_dtypes
from contextlib import ExitStack

import concourse.bass as bass
import concourse.mybir as mybir

F32 = mybir.dt.float32
BF16 = mybir.dt.bfloat16
AF = mybir.ActivationFunctionType
ALU = mybir.AluOpType
AX = mybir.AxisListType.X
BF = ml_dtypes.bfloat16

D = 2048
I = 1408
NEXP = 12
TOK = 2048
TPR = 512
NT = 4
NR = TOK // TPR
NCI = 16
NIC = 11
EPS = 1e-5

PHASES = "full"
DEBUG = False
import os
ATTN_STOP = os.environ.get("ATTN_STOP", "")


def build_program():
    nc = bass.Bass("TRN2", target_bir_lowering=False)
    es = ExitStack()
    streams = {"sync": [], "gpsimd": [], "tensor": [], "vector": [], "scalar": []}
    cnt = {}
    SEM = {}
    for name in ["pe", "act", "dve", "gq", "sq", "cc",
                 "s_const", "s_x", "s_stx", "s_cache",
                 "g_wgu0", "g_wgu1", "g_wgu2", "g_wgu3",
                 "g_wd0", "g_wd1", "g_wd2", "g_acc", "g_attw",
                 "s_agx", "s_wT", "s_qkv", "s_zt", "s_rs0", "s_rs1",
                 "s_ld3a", "s_ld3b", "s_slab", "s_hld", "s_qkva", "s_qkvb"]:
        SEM[name] = es.enter_context(nc.semaphore(name))
        cnt[name] = 0

    def post(name, k=1):
        cnt[name] += k
        return cnt[name]

    def em(eng, fn):
        streams[eng].append(fn)

    x_in = nc.dram_tensor("x", [TOK, D], F32, kind="ExternalInput").ap()
    gate_w = nc.dram_tensor("gate_w", [8, D], F32, kind="ExternalInput").ap()
    sgw = nc.dram_tensor("sgw", [D, 1], F32, kind="ExternalInput").ap()
    w_gate = nc.dram_tensor("w_gate", [8, D, I], F32, kind="ExternalInput").ap()
    w_up = nc.dram_tensor("w_up", [8, D, I], F32, kind="ExternalInput").ap()
    w_down = nc.dram_tensor("w_down", [8, I, D], F32, kind="ExternalInput").ap()
    sw_gate = nc.dram_tensor("sw_gate", [D, 4 * I], F32, kind="ExternalInput").ap()
    sw_up = nc.dram_tensor("sw_up", [D, 4 * I], F32, kind="ExternalInput").ap()
    sw_down = nc.dram_tensor("sw_down", [4 * I, D], F32, kind="ExternalInput").ap()
    wq = nc.dram_tensor("wq", [1024, D], F32, kind="ExternalInput").ap()
    wk = nc.dram_tensor("wk", [1024, D], F32, kind="ExternalInput").ap()
    wv = nc.dram_tensor("wv", [1024, D], F32, kind="ExternalInput").ap()
    wo = nc.dram_tensor("wo", [D, 1024], F32, kind="ExternalInput").ap()
    y_out = nc.dram_tensor("y", [TOK, D], F32, kind="ExternalOutput").ap()

    if DEBUG:
        dbg_xlnT = nc.dram_tensor("dbg_xlnT", [128, NCI * TPR], BF16, kind="ExternalOutput").ap()
        dbg_lgb = nc.dram_tensor("dbg_lgb", [128, NT * 16], F32, kind="ExternalOutput").ap()
        dbg_cmb = nc.dram_tensor("dbg_cmb", [128, NT * 16], F32, kind="ExternalOutput").ap()
        dbg_gs = nc.dram_tensor("dbg_gs", [128, I], BF16, kind="ExternalOutput").ap()
        dbg_h = nc.dram_tensor("dbg_h", [128, I], BF16, kind="ExternalOutput").ap()
        dbg_hT = nc.dram_tensor("dbg_hT", [128, NIC * 128], BF16, kind="ExternalOutput").ap()
        dbg_xt = nc.dram_tensor("dbg_xt", [128, D], F32, kind="ExternalOutput").ap()
    x1s = nc.dram_tensor("x1s", [TOK, D], F32).ap()
    wgu_c = nc.dram_tensor("wgu_c", [NEXP, D, 2 * I], BF16).ap()
    wd_c = nc.dram_tensor("wd_c", [NEXP, I, D], BF16).ap()
    agx_in = nc.dram_tensor("agx_in", [D, TOK], BF16).ap()
    agx_out = nc.dram_tensor("agx_out", [2 * D, TOK], BF16).ap()
    wqT_s = nc.dram_tensor("wqT_s", [D, 1024], BF16).ap()
    wkT_s = nc.dram_tensor("wkT_s", [D, 1024], BF16).ap()
    wvT_s = nc.dram_tensor("wvT_s", [D, 1024], BF16).ap()
    woT_s = nc.dram_tensor("woT_s", [1024, D], BF16).ap()
    qT_sd = nc.dram_tensor("qT_sd", [1024, 4096], BF16).ap()
    kT_sd = nc.dram_tensor("kT_sd", [1024, 4096], BF16).ap()
    vT_sd = nc.dram_tensor("vT_sd", [1024, 4096], BF16).ap()
    zT_sd = nc.dram_tensor("zT_sd", [1024, 4096], BF16).ap()
    rs_in = nc.dram_tensor("rs_in", [4096, D], F32).ap()
    rs_out = nc.dram_tensor("rs_out", [TOK, D], F32).ap()

    ident_bf_d = nc.inline_tensor(np.eye(128, dtype=BF), "ident_bf_d")
    ident_f_d = nc.inline_tensor(np.eye(128, dtype=np.float32), "ident_f_d")
    tri_d = nc.inline_tensor(np.tril(np.ones((128, 128), dtype=np.float32)).astype(BF), "tri01_d")
    kconst_d = nc.inline_tensor(np.full((128, 2), EPS, dtype=np.float32), "kconst_d")

    ident_bf = es.enter_context(nc.sbuf_tensor("ident_bf", [128, 128], BF16))
    ident_f = es.enter_context(nc.sbuf_tensor("ident_f", [128, 128], F32))
    tri01 = es.enter_context(nc.sbuf_tensor("tri01", [128, 128], BF16))
    gwT = es.enter_context(nc.sbuf_tensor("gwT", [128, NCI * 16], F32))
    kconst = es.enter_context(nc.sbuf_tensor("kconst", [128, 2], F32))

    c_tgt = post("s_const", 16 * 4)

    def f_const(S):
        S.dma_start(out=kconst[:, :], in_=kconst_d.ap()).then_inc(SEM["s_const"], 16)
        S.dma_start(out=ident_bf[:, :], in_=ident_bf_d.ap()).then_inc(SEM["s_const"], 16)
        S.dma_start(out=ident_f[:, :], in_=ident_f_d.ap()).then_inc(SEM["s_const"], 16)
        S.dma_start(out=tri01[:, :], in_=tri_d.ap()).then_inc(SEM["s_const"], 16)
    em("sync", f_const)

    fes = ExitStack()
    xt = fes.enter_context(nc.sbuf_tensor("xt", [128, NT * D], F32))
    xlnT = fes.enter_context(nc.sbuf_tensor("xlnT", [128, NCI * TPR], BF16))
    wgu = fes.enter_context(nc.sbuf_tensor("wgu", [128, NCI * 2 * I], BF16))
    wdb = fes.enter_context(nc.sbuf_tensor("wdb", [128, NIC * D], BF16))
    gs = fes.enter_context(nc.sbuf_tensor("gs", [128, 2 * I], BF16))
    hT = fes.enter_context(nc.sbuf_tensor("hT", [128, 2 * NIC * 128], BF16))
    cmb = fes.enter_context(nc.sbuf_tensor("cmb", [128, NT * 16], F32))
    lgb = fes.enter_context(nc.sbuf_tensor("lgb", [128, NT * 16], F32))
    xf32 = fes.enter_context(nc.sbuf_tensor("xf32", [128, 2 * 128], F32))
    xT32 = fes.enter_context(nc.sbuf_tensor("xT32", [128, 2 * 128], F32))
    sst = fes.enter_context(nc.sbuf_tensor("sst", [128, NT * 8], F32))
    t2s = fes.enter_context(nc.sbuf_tensor("t2s", [128, NT * 48], F32))

    psGU = fes.enter_context(nc.psum_tensor("psGU", [128, 1536], F32))
    psD = fes.enter_context(nc.psum_tensor("psD", [128, 2048], F32))
    psTr = fes.enter_context(nc.psum_tensor("psTr", [128, 1024], BF16))

    wgu_v = wgu[:, :].rearrange("p (c n) -> p c n", c=NCI)
    wdb_v = wdb[:, :].rearrange("p (c n) -> p c n", c=NIC)
    xlnT_v = xlnT[:, :].rearrange("p (c n) -> p c n", c=NCI)
    lgps = psD[:, 1536:1552]

    gw_nat_t = fes.enter_context(nc.sbuf_tensor("gw_nat", [8, D], F32))
    gw_nat = gw_nat_t[:, :]
    sg_nat = fes.enter_context(nc.sbuf_tensor("sg_nat", [128, NCI], F32))
    gl = post("s_const", 16 * 2)

    def f_gwl(S):
        S.dma_start(out=gw_nat, in_=gate_w).then_inc(SEM["s_const"], 16)
        S.dma_start(out=sg_nat[:, :].rearrange("p (c one) -> p c one", one=1),
                    in_=sgw.rearrange("(c p) one -> p c one", p=128),
                    allow_slow_non_contiguous=True).then_inc(SEM["s_const"], 16)
    em("sync", f_gwl)

    gz = post("dve")

    def f_gz(V):
        V.memset(gwT[:, :], 0.0).then_inc(SEM["dve"], 1)
    em("vector", f_gz)

    gwtr, gwev = {}, {}
    for c in range(NCI):
        gwtr[c] = post("pe")

        def f_t(T, c=c):
            if c == 0:
                T.wait_ge(SEM["s_const"], gl)
                T.wait_ge(SEM["dve"], gz)
            if c >= 3:
                T.wait_ge(SEM["dve"], gwev[c - 3])
            T.transpose(psD[:, (c % 3) * 512:(c % 3) * 512 + 8],
                        gw_nat_t[0:8, c * 128:(c + 1) * 128], ident_f[0:8, 0:8]).then_inc(SEM["pe"], 1)
        em("tensor", f_t)
        gwev[c] = post("dve")

        def f_v(V, c=c):
            V.wait_ge(SEM["pe"], gwtr[c])
            V.tensor_copy(gwT[:, c * 16:c * 16 + 8],
                          psD[:, (c % 3) * 512:(c % 3) * 512 + 8]).then_inc(SEM["dve"], 1)
        em("vector", f_v)
    gws = post("dve")

    def f_sg(V):
        V.wait_ge(SEM["s_const"], gl)
        V.tensor_copy(gwT[:, :].rearrange("p (c n) -> p c n", c=NCI)[:, :, 8:9],
                      sg_nat[:, :].rearrange("p (c one) -> p c one", one=1)).then_inc(SEM["dve"], 1)
    em("vector", f_sg)
    gwT_ready_dve = gws
    gwT_psD_free_dve = gwev[NCI - 1]

    def src_gu(e, half):
        if e < 8:
            t = w_gate if half == 0 else w_up
            return t[e].rearrange("(c p) i -> p c i", p=128)
        j = e - 8
        t = sw_gate if half == 0 else sw_up
        return t.rearrange("(c p) i -> p c i", p=128)[:, :, j * I:(j + 1) * I]

    def src_d(e):
        if e < 8:
            return w_down[e].rearrange("(c p) d -> p c d", p=128)
        j = e - 8
        return sw_down[j * I:(j + 1) * I, :].rearrange("(c p) d -> p c d", p=128)

    def emit_wgu_dma(use_cache, e, wait_pe, wait_sq):
        tgts = []
        for g in range(4):
            cs = slice(g * 4, (g + 1) * 4)
            sname = f"g_wgu{g}"
            post(sname, 16 if use_cache else 32)

            def f(G, g=g, e=e, cs=cs, wp=wait_pe, ws=wait_sq, uc=use_cache, sname=sname):
                if g == 0:
                    if wp is not None:
                        G.wait_ge(SEM["pe"], wp)
                    if ws is not None:
                        G.wait_ge(SEM["s_cache"], ws)
                if uc:
                    G.dma_start(out=wgu_v[:, cs, :],
                                in_=wgu_c[e].rearrange("(c p) n -> p c n", p=128)[:, cs, :]
                                ).then_inc(SEM[sname], 16)
                else:
                    G.dma_start(out=wgu_v[:, cs, 0:I], in_=src_gu(e, 0)[:, cs, :]
                                ).then_inc(SEM[sname], 16)
                    G.dma_start(out=wgu_v[:, cs, I:2 * I], in_=src_gu(e, 1)[:, cs, :]
                                ).then_inc(SEM[sname], 16)
            em("gpsimd", f)
            tgts.append((sname, cnt[sname]))
        return tgts

    def emit_wd_dma(use_cache, e, wait_pe, wait_sq):
        tgts = []
        for gi, (c0, c1) in enumerate([(0, 4), (4, 8), (8, 11)]):
            sname = f"g_wd{gi}"
            post(sname, 16)

            def f(G, gi=gi, c0=c0, c1=c1, e=e, wp=wait_pe, ws=wait_sq, uc=use_cache,
                  sname=sname):
                if gi == 0:
                    if wp is not None:
                        G.wait_ge(SEM["pe"], wp)
                    if ws is not None:
                        G.wait_ge(SEM["s_cache"], ws)
                if uc:
                    G.dma_start(out=wdb_v[:, c0:c1, :],
                                in_=wd_c[e].rearrange("(c p) n -> p c n", p=128)[:, c0:c1, :]
                                ).then_inc(SEM[sname], 16)
                else:
                    G.dma_start(out=wdb_v[:, c0:c1, :],
                                in_=src_d(e)[:, c0:c1, :]).then_inc(SEM[sname], 16)
            em("gpsimd", f)
            tgts.append((sname, cnt[sname]))
        return tgts

    def emit_cache_write(e, wgu_tgts, wd_tgts):
        post("s_cache", 32)

        def f(S, e=e, wgu_tgts=wgu_tgts, wd_tgts=wd_tgts):
            for sname, v in wgu_tgts:
                S.wait_ge(SEM[sname], v)
            S.dma_start(out=wgu_c[e].rearrange("(c p) n -> p c n", p=128),
                        in_=wgu_v[:, :, :]).then_inc(SEM["s_cache"], 16)
            for sname, v in wd_tgts:
                S.wait_ge(SEM[sname], v)
            S.dma_start(out=wd_c[e].rearrange("(c p) n -> p c n", p=128),
                        in_=wdb_v[:, :, :]).then_inc(SEM["s_cache"], 16)
        em("sync", f)
        return cnt["s_cache"]

    def emit_ln_round(ffn_idx, r, x_ready, psD_free_dve, psD_free_pe):
        stats1 = {}
        for t in range(NT):
            xt_t = xt[:, t * D:(t + 1) * D]
            mcol = sst[:, t * 8 + 0:t * 8 + 1]
            qcol = sst[:, t * 8 + 1:t * 8 + 2]
            q2col = sst[:, t * 8 + 6:t * 8 + 7]
            tcol = sst[:, t * 8 + 2:t * 8 + 3]
            vcol = sst[:, t * 8 + 3:t * 8 + 4]
            stats1[t] = post("dve", 2)

            def f(V, t=t, xt_t=xt_t, mcol=mcol, qcol=qcol, q2col=q2col,
                  x_ready=x_ready, pfree=psD_free_dve, ppe=psD_free_pe):
                if t == 0:
                    V.wait_ge(SEM[x_ready[0]], x_ready[1])
                    if pfree is not None:
                        V.wait_ge(SEM["dve"], pfree)
                    if ppe is not None:
                        V.wait_ge(SEM["pe"], ppe)
                V.reduce_sum(mcol, xt_t, axis=AX)
                V.scalar_tensor_tensor(psGU[:, 0:1536], xt_t[:, 0:1536], 1.0, xt_t[:, 0:1536],
                                       ALU.mult, ALU.mult, accum_out=qcol)
                V.scalar_tensor_tensor(gs[:, 0:512], xt_t[:, 1536:2048], 1.0,
                                       xt_t[:, 1536:2048], ALU.mult, ALU.mult,
                                       accum_out=q2col).then_inc(SEM["dve"], 2)
            em("vector", f)
            m2t = post("dve")

            def f_m2(V, mcol=mcol, tcol=tcol, qcol=qcol, q2col=q2col, s1=stats1[t]):
                V.wait_ge(SEM["dve"], s1)
                V.tensor_tensor(qcol, qcol, q2col, ALU.add)
                V.scalar_tensor_tensor(tcol, mcol, 1.0 / (D * D), mcol, ALU.mult, ALU.mult
                                       ).then_inc(SEM["dve"], 1)
            em("vector", f_m2)
            stats1[t] = m2t
        var_done = {}
        for t in range(NT):
            qcol = sst[:, t * 8 + 1:t * 8 + 2]
            tcol = sst[:, t * 8 + 2:t * 8 + 3]
            vcol = sst[:, t * 8 + 3:t * 8 + 4]
            var_done[t] = post("dve")

            def f(V, t=t, qcol=qcol, tcol=tcol, vcol=vcol):
                V.wait_ge(SEM["dve"], stats1[t])
                V.scalar_tensor_tensor(vcol, qcol, 1.0 / D, tcol, ALU.mult, ALU.subtract
                                       ).then_inc(SEM["dve"], 1)
            em("vector", f)
        ln4 = post("act")
        rstd_done = post("act")

        def f_ln4(S):
            S.wait_ge(SEM["dve"], var_done[NT - 1])
            last = None
            for t in range(NT):
                vcol = sst[:, t * 8 + 3:t * 8 + 4]
                tcol = sst[:, t * 8 + 2:t * 8 + 3]
                last = S.activation(tcol, vcol, AF.Ln, bias=kconst[:, 0:1])
            last.then_inc(SEM["act"], 1)
        em("scalar", f_ln4)

        def f_rstd(S, ln4=ln4):
            S.wait_ge(SEM["act"], ln4)
            last = None
            for t in range(NT):
                tcol = sst[:, t * 8 + 2:t * 8 + 3]
                rcol = sst[:, t * 8 + 4:t * 8 + 5]
                last = S.activation(rcol, tcol, AF.Exp, scale=-0.5)
            last.then_inc(SEM["act"], 1)
        em("scalar", f_rstd)
        negm_done = post("dve")

        def f_negm(V):
            V.wait_ge(SEM["act"], rstd_done)
            last = None
            for t in range(NT):
                mcol = sst[:, t * 8 + 0:t * 8 + 1]
                ncol = sst[:, t * 8 + 5:t * 8 + 6]
                last = V.tensor_scalar_mul(ncol, mcol, -1.0 / D)
            last.then_inc(SEM["dve"], 1)
        em("vector", f_negm)

        lg_mm_last = {}
        prev_tile_lg_dve = None
        for t in range(NT):
            xt_t = xt[:, t * D:(t + 1) * D]
            ncol = sst[:, t * 8 + 5:t * 8 + 6]
            rcol = sst[:, t * 8 + 4:t * 8 + 5]
            vck, trk, evk, mmk = {}, {}, {}, {}
            for c in range(NCI):
                slot = c % 2
                b3 = (c % 3) * 512
                vck[c] = post("dve")

                def f_vc(V, c=c, t=t, slot=slot, ptl=prev_tile_lg_dve, trk=trk,
                         ncol=ncol, rcol=rcol):
                    if c == 0:
                        V.wait_ge(SEM["dve"], negm_done)
                        if ptl is not None:
                            V.wait_ge(SEM["dve"], ptl)
                    if c >= 2:
                        V.wait_ge(SEM["pe"], trk[c - 2])
                    V.tensor_scalar(xf32[:, slot * 128:(slot + 1) * 128],
                                    xt[:, t * D + c * 128:t * D + (c + 1) * 128],
                                    ncol, rcol, ALU.add, ALU.mult).then_inc(SEM["dve"], 1)
                em("vector", f_vc)
                trk[c] = post("pe")

                def f_tc(T, c=c, t=t, slot=slot, b3=b3, vck=vck, evk=evk):
                    T.wait_ge(SEM["dve"], vck[c])
                    if c >= 3:
                        T.wait_ge(SEM["dve"], evk[c - 3])
                    T.transpose(psD[:, b3:b3 + 128], xf32[:, slot * 128:(slot + 1) * 128],
                                ident_f[:, :]).then_inc(SEM["pe"], 1)
                em("tensor", f_tc)
                evk[c] = post("dve", 2)

                def f_ec(V, c=c, t=t, slot=slot, b3=b3, trk=trk, mmk=mmk):
                    V.wait_ge(SEM["pe"], trk[c])
                    if c >= 2:
                        V.wait_ge(SEM["pe"], mmk[c - 2])
                    V.tensor_copy(xT32[:, slot * 128:(slot + 1) * 128], psD[:, b3:b3 + 128])
                    V.tensor_copy(xlnT_v[:, c, t * 128:(t + 1) * 128],
                                  psD[:, b3:b3 + 128]).then_inc(SEM["dve"], 2)
                em("vector", f_ec)
                mmk[c] = post("pe")

                def f_mc(T, c=c, t=t, slot=slot, evk=evk,
                         need_gwT=(ffn_idx == 0 and r == 0)):
                    T.wait_ge(SEM["dve"], evk[c])
                    if need_gwT and c == 0 and t == 0:
                        T.wait_ge(SEM["dve"], gwT_ready_dve)
                    T.matmul(lgps, xT32[:, slot * 128:(slot + 1) * 128],
                             gwT[:, c * 16:c * 16 + 16],
                             start=(c == 0), stop=(c == NCI - 1)).then_inc(SEM["pe"], 1)
                em("tensor", f_mc)
            lg_mm_last[t] = mmk[NCI - 1]
            prev_tile_lg_dve = post("dve")

            def f_lg(V, t=t, mm_last=lg_mm_last[t]):
                V.wait_ge(SEM["pe"], mm_last)
                V.tensor_copy(lgb[:, t * 16:(t + 1) * 16], lgps).then_inc(SEM["dve"], 1)
            em("vector", f_lg)

        lg_all_dve = prev_tile_lg_dve
        ddp = {}

        def em_v(fn, wait_act=None, extra_dve=None):
            tgt = post("dve")

            def f(V, fn=fn, wait_act=wait_act, prev_tgt=tgt - 1, extra_dve=extra_dve):
                V.wait_ge(SEM["dve"], prev_tgt)
                if extra_dve is not None:
                    V.wait_ge(SEM["dve"], extra_dve)
                if wait_act is not None:
                    V.wait_ge(SEM["act"], wait_act)
                fn(V).then_inc(SEM["dve"], 1)
            em("vector", f)
            return tgt

        for t in range(NT):
            base = t * 48
            m1 = t2s[:, base + 0:base + 1]; m2 = t2s[:, base + 1:base + 2]
            ddc = t2s[:, base + 2:base + 3]
            eq1 = t2s[:, base + 8:base + 16]; lgm = t2s[:, base + 16:base + 24]
            eq2 = t2s[:, base + 24:base + 32]
            l8 = lgb[:, t * 16:t * 16 + 8]
            em_v(lambda V, m1=m1, l8=l8: V.reduce_max(m1, l8, axis=AX),
                 extra_dve=lg_all_dve)
            em_v(lambda V, eq1=eq1, l8=l8, m1=m1: V.tensor_scalar(eq1, l8, m1, None, ALU.is_equal))
            em_v(lambda V, lgm=lgm, eq1=eq1, l8=l8:
                 V.scalar_tensor_tensor(lgm, eq1, -1e30, l8, ALU.mult, ALU.add))
            em_v(lambda V, m2=m2, lgm=lgm: V.reduce_max(m2, lgm, axis=AX))
            em_v(lambda V, eq2=eq2, lgm=lgm, m2=m2: V.tensor_scalar(eq2, lgm, m2, None, ALU.is_equal))
            ddp[t] = em_v(lambda V, ddc=ddc, m2=m2, m1=m1: V.tensor_tensor(ddc, m2, m1, ALU.subtract))
        exps_done = post("act")

        def f_exps(S):
            S.wait_ge(SEM["dve"], ddp[NT - 1])
            last = None
            for t in range(NT):
                base = t * 48
                last = S.activation(t2s[:, base + 3:base + 4], t2s[:, base + 2:base + 3], AF.Exp)
            for t in range(NT):
                base = t * 48
                last = S.activation(t2s[:, base + 6:base + 7], lgb[:, t * 16 + 8:t * 16 + 9],
                                    AF.Exp, scale=-1.0)
            last.then_inc(SEM["act"], 1)
        em("scalar", f_exps)
        cmb_done = None
        for t in range(NT):
            base = t * 48
            eec = t2s[:, base + 3:base + 4]; ssc = t2s[:, base + 4:base + 5]
            rrc = t2s[:, base + 5:base + 6]; egc = t2s[:, base + 6:base + 7]
            w1c = t2s[:, base + 7:base + 8]
            eq1 = t2s[:, base + 8:base + 16]; eq2 = t2s[:, base + 24:base + 32]
            t1c = t2s[:, base + 32:base + 40]
            w2c = t2s[:, base + 40:base + 41]
            sgc = t2s[:, base + 41:base + 42]
            hgc = t2s[:, base + 42:base + 43]
            em_v(lambda V, ssc=ssc, eec=eec: V.tensor_scalar_add(ssc, eec, 1.0),
                 wait_act=exps_done)
            em_v(lambda V, rrc=rrc, ssc=ssc: V.reciprocal(rrc, ssc))
            em_v(lambda V, w1c=w1c, rrc=rrc: V.tensor_scalar_mul(w1c, rrc, 0.5))
            em_v(lambda V, w2c=w2c, eec=eec, rrc=rrc: V.tensor_tensor(w2c, eec, rrc, ALU.mult))
            em_v(lambda V, w2c=w2c: V.tensor_scalar_mul(w2c, w2c, 0.5))
            em_v(lambda V, t1c=t1c, eq1=eq1, w1c=w1c: V.tensor_scalar(t1c, eq1, w1c, None, ALU.mult))
            em_v(lambda V, t=t, eq2=eq2, w2c=w2c, t1c=t1c:
                 V.scalar_tensor_tensor(cmb[:, t * 16:t * 16 + 8], eq2, w2c, t1c,
                                        ALU.mult, ALU.add))
            em_v(lambda V, sgc=sgc, egc=egc: V.tensor_scalar_add(sgc, egc, 1.0))
            em_v(lambda V, hgc=hgc, sgc=sgc: V.reciprocal(hgc, sgc))
            cmb_done = em_v(lambda V, t=t, hgc=hgc:
                            V.tensor_scalar_mul(cmb[:, t * 16 + 8:t * 16 + 9], hgc, 0.5))
        return cmb_done, lg_all_dve

    dbg_state = {"done": False}

    def emit_dbg_snapshot(wait_sem, wait_val, out_ap, in_ap):
        if not DEBUG:
            return None
        post("gq", 16)

        def f(G, wait_sem=wait_sem, wait_val=wait_val, out_ap=out_ap, in_ap=in_ap):
            G.wait_ge(SEM[wait_sem], wait_val)
            G.dma_start(out=out_ap, in_=in_ap).then_inc(SEM["gq"], 16)
        em("gpsimd", f)
        return cnt["gq"]

    def emit_g(sd, psGU_free_dve, xln_ready_dve):
        tgt = post("pe")

        def f(T, sd=sd, pf=psGU_free_dve, xr=xln_ready_dve):
            if pf is not None:
                T.wait_ge(SEM["dve"], pf)
            T.wait_ge(SEM["dve"], xr)
            last = None
            for c in range(NCI):
                if sd["t"] == 0 and c % 4 == 0:
                    sname, v = sd["wgu_tgts"][c // 4]
                    T.wait_ge(SEM[sname], v)
                elif sd["t"] != 0 and c == 0:
                    pass
                xs = xlnT_v[:, c, sd["t"] * 128:(sd["t"] + 1) * 128]
                ws = wgu_v[:, c, 0:I]
                T.matmul(psGU[:, 0:512], xs, ws[:, 0:512], start=(c == 0), stop=(c == NCI - 1))
                T.matmul(psGU[:, 512:1024], xs, ws[:, 512:1024], start=(c == 0), stop=(c == NCI - 1))
                last = T.matmul(psGU[:, 1024:1408], xs, ws[:, 1024:1408],
                                start=(c == 0), stop=(c == NCI - 1))
            last.then_inc(SEM["pe"], 1)
        em("tensor", f)
        sd["g"] = tgt

    def emit_silu(sd, gs_free_pe):
        tgt = post("act")

        def f(S, sd=sd, gf=gs_free_pe):
            S.wait_ge(SEM["pe"], sd["g"])
            if gf is not None:
                S.wait_ge(SEM["pe"], gf)
            S.activation(gs[:, sd["slot"] * I:(sd["slot"] + 1) * I], psGU[:, 0:1408],
                         AF.Silu).then_inc(SEM["act"], 1)
        em("scalar", f)
        sd["silu"] = tgt

    def emit_u(sd):
        tgt = post("pe")

        def f(T, sd=sd):
            T.wait_ge(SEM["act"], sd["silu"])
            last = None
            for c in range(NCI):
                xs = xlnT_v[:, c, sd["t"] * 128:(sd["t"] + 1) * 128]
                ws = wgu_v[:, c, I:2 * I]
                T.matmul(psGU[:, 0:512], xs, ws[:, 0:512], start=(c == 0), stop=(c == NCI - 1))
                T.matmul(psGU[:, 512:1024], xs, ws[:, 512:1024], start=(c == 0), stop=(c == NCI - 1))
                last = T.matmul(psGU[:, 1024:1408], xs, ws[:, 1024:1408],
                                start=(c == 0), stop=(c == NCI - 1))
            last.then_inc(SEM["pe"], 1)
        em("tensor", f)
        sd["u"] = tgt

    def emit_h(sd, cmb_ready_dve):
        tgt = post("dve")
        e, t = sd["e"], sd["t"]
        col = t * 16 + (e if e < 8 else 8)

        def f(V, sd=sd, col=col, cr=cmb_ready_dve):
            V.wait_ge(SEM["pe"], sd["u"])
            V.wait_ge(SEM["dve"], cr)
            if sd.get("_h_extra_wait"):
                V.wait_ge(SEM["gq"], sd["_h_extra_wait"])
            V.scalar_tensor_tensor(gs[:, sd["slot"] * I:(sd["slot"] + 1) * I],
                                   psGU[:, 0:1408], cmb[:, col:col + 1],
                                   gs[:, sd["slot"] * I:(sd["slot"] + 1) * I],
                                   ALU.mult, ALU.mult).then_inc(SEM["dve"], 1)
        em("vector", f)
        sd["h"] = tgt

    def emit_tr1(sd, psTr_free_dve):
        tgt = post("pe")

        def f(T, sd=sd, pf=psTr_free_dve):
            T.wait_ge(SEM["dve"], sd["h"])
            if sd.get("_dbg_h"):
                T.wait_ge(SEM["gq"], sd["_dbg_h"])
            if pf is not None:
                T.wait_ge(SEM["dve"], pf)
            last = None
            for c in range(8):
                last = T.transpose(psTr[:, c * 128:(c + 1) * 128],
                                   gs[:, sd["slot"] * I + c * 128:sd["slot"] * I + (c + 1) * 128],
                                   ident_bf[:, :])
            last.then_inc(SEM["pe"], 1)
        em("tensor", f)
        sd["tr1"] = tgt

    def emit_ev1(sd, hT_free_pe):
        tgt = post("dve")

        def f(V, sd=sd, hf=hT_free_pe):
            V.wait_ge(SEM["pe"], sd["tr1"])
            if hf is not None:
                V.wait_ge(SEM["pe"], hf)
            V.tensor_copy(hT[:, sd["slot"] * NIC * 128:sd["slot"] * NIC * 128 + 1024],
                          psTr[:, 0:1024]).then_inc(SEM["dve"], 1)
        em("vector", f)
        sd["ev1"] = tgt

    def emit_dnA(sd, psD_free_dve, psD_free_pe):
        tgt = post("pe")

        def f(T, sd=sd, pf=psD_free_dve, pp=psD_free_pe):
            T.wait_ge(SEM["dve"], sd["ev1"])
            if pf is not None:
                T.wait_ge(SEM["dve"], pf)
            for sname, v in sd["wd_tgts"][0:2]:
                T.wait_ge(SEM[sname], v)
            last = None
            base = sd["slot"] * NIC * 128
            for ic in range(6):
                hs = hT[:, base + ic * 128:base + (ic + 1) * 128]
                for qq in range(4):
                    last = T.matmul(psD[:, qq * 512:(qq + 1) * 512], hs,
                                    wdb_v[:, ic, qq * 512:(qq + 1) * 512],
                                    start=(ic == 0), stop=False)
            last.then_inc(SEM["pe"], 1)
        em("tensor", f)
        sd["dnA"] = tgt

    def emit_tr2(sd):
        tgt = post("pe")

        def f(T, sd=sd):
            T.wait_ge(SEM["dve"], sd["ev1"])
            last = None
            for c in range(8, NIC):
                last = T.transpose(psTr[:, (c - 8) * 128:(c - 7) * 128],
                                   gs[:, sd["slot"] * I + c * 128:sd["slot"] * I + (c + 1) * 128],
                                   ident_bf[:, :])
            last.then_inc(SEM["pe"], 1)
        em("tensor", f)
        sd["tr2"] = tgt

    def emit_ev2(sd):
        tgt = post("dve")

        def f(V, sd=sd):
            V.wait_ge(SEM["pe"], sd["tr2"])
            V.tensor_copy(hT[:, sd["slot"] * NIC * 128 + 1024:sd["slot"] * NIC * 128 + 1408],
                          psTr[:, 0:384]).then_inc(SEM["dve"], 1)
        em("vector", f)
        sd["ev2"] = tgt

    def emit_dnB(sd):
        tgt = post("pe")

        def f(T, sd=sd):
            T.wait_ge(SEM["dve"], sd["ev2"])
            sname, v = sd["wd_tgts"][2]
            T.wait_ge(SEM[sname], v)
            last = None
            base = sd["slot"] * NIC * 128
            for ic in range(6, NIC):
                hs = hT[:, base + ic * 128:base + (ic + 1) * 128]
                for qq in range(4):
                    last = T.matmul(psD[:, qq * 512:(qq + 1) * 512], hs,
                                    wdb_v[:, ic, qq * 512:(qq + 1) * 512],
                                    start=False, stop=(ic == NIC - 1))
            last.then_inc(SEM["pe"], 1)
        em("tensor", f)
        sd["dnB"] = tgt

    def emit_xevac(sd):
        tgt = post("dve")

        def f(V, sd=sd):
            V.wait_ge(SEM["pe"], sd["dnB"])
            V.tensor_tensor(xt[:, sd["t"] * D:(sd["t"] + 1) * D],
                            xt[:, sd["t"] * D:(sd["t"] + 1) * D],
                            psD[:, :], ALU.add).then_inc(SEM["dve"], 1)
        em("vector", f)
        sd["xev"] = tgt

    def emit_lagged(prev_sd, carry):
        emit_tr1(prev_sd, prev_sd["_psTr_free"])
        emit_ev1(prev_sd, prev_sd["_hT_free"])
        emit_dnA(prev_sd, prev_sd["_psD_free"], None)
        emit_tr2(prev_sd)
        emit_ev2(prev_sd)


    def emit_attention(carry):
        SQ = "sq"; GQ = "gq"
        if ATTN_STOP == "none":
            carry["rs_done_cc"] = cnt["cc"]
            carry["rs_out"] = x1s
            carry["rs_ready"] = ("cc", 0)
            carry["prev_sd"] = None
            carry["prev2_sd"] = None
            return carry
        wgu_flat = wgu[:, :]
        wdb_flat = wdb[:, :]
        xlnT_flat = xlnT[:, :]
        xl2T_v = xlnT_flat.rearrange("p (c n) -> p c n", c=NCI)
        for tg in range(4):
            post("s_x", 16)
            ls = carry.get("last_store_sq")
            lx = carry.get("last_xev_dve")

            def f_x(S, tg=tg, ls=ls):
                if ls is not None:
                    for sname, v in ls:
                        S.wait_ge(SEM[sname], v)
                if tg > 0:
                    S.wait_ge(SEM["dve"], carry["_ln2_done"][tg - 1])
                S.dma_start(out=xt[:, :].rearrange("p (t n) -> p t n", t=NT),
                            in_=x1s[tg * TPR:(tg + 1) * TPR, :]
                            .rearrange("(t p) n -> p t n", p=128)).then_inc(SEM["s_x"], 16)
            em("sync", f_x)
            x_ready = ("s_x", cnt["s_x"])
            stats1 = {}
            for t in range(NT):
                xt_t = xt[:, t * D:(t + 1) * D]
                mcol = sst[:, t * 8 + 0:t * 8 + 1]
                qcol = sst[:, t * 8 + 1:t * 8 + 2]
                tcol = sst[:, t * 8 + 2:t * 8 + 3]
                vcol = sst[:, t * 8 + 3:t * 8 + 4]
                q2col = sst[:, t * 8 + 6:t * 8 + 7]
                stats1[t] = post("dve", 2)

                def f(V, t=t, xt_t=xt_t, mcol=mcol, qcol=qcol, q2col=q2col,
                      x_ready=x_ready, lx=lx, tg=tg):
                    if t == 0:
                        V.wait_ge(SEM[x_ready[0]], x_ready[1])
                        if tg == 0 and lx is not None:
                            V.wait_ge(SEM["dve"], lx)
                        if tg > 0:
                            V.wait_ge(SEM["dve"], carry["_ln2_done"][tg - 1])
                            V.wait_ge(SEM["s_agx"], carry["_agx_st"][tg - 1])
                    V.reduce_sum(mcol, xt_t, axis=AX)
                    V.scalar_tensor_tensor(psGU[:, 0:1536], xt_t[:, 0:1536], 1.0, xt_t[:, 0:1536],
                                           ALU.mult, ALU.mult, accum_out=qcol)
                    V.scalar_tensor_tensor(gs[:, 0:512], xt_t[:, 1536:2048], 1.0,
                                           xt_t[:, 1536:2048], ALU.mult, ALU.mult,
                                           accum_out=q2col).then_inc(SEM["dve"], 2)
                em("vector", f)
                m2t = post("dve")

                def f_m2(V, mcol=mcol, tcol=tcol, qcol=qcol, q2col=q2col, s1=stats1[t]):
                    V.wait_ge(SEM["dve"], s1)
                    V.tensor_tensor(qcol, qcol, q2col, ALU.add)
                    V.scalar_tensor_tensor(tcol, mcol, 1.0 / (D * D), mcol, ALU.mult, ALU.mult
                                           ).then_inc(SEM["dve"], 1)
                em("vector", f_m2)
                stats1[t] = m2t
            var_done = {}
            for t in range(NT):
                qcol = sst[:, t * 8 + 1:t * 8 + 2]
                tcol = sst[:, t * 8 + 2:t * 8 + 3]
                vcol = sst[:, t * 8 + 3:t * 8 + 4]
                var_done[t] = post("dve")

                def f(V, t=t, qcol=qcol, tcol=tcol, vcol=vcol, stats1=stats1):
                    V.wait_ge(SEM["dve"], stats1[t])
                    V.scalar_tensor_tensor(vcol, qcol, 1.0 / D, tcol, ALU.mult, ALU.subtract
                                           ).then_inc(SEM["dve"], 1)
                em("vector", f)
            ln4b = post("act")
            rstd_done = post("act")

            def f_ln4b(S, var_done=var_done):
                S.wait_ge(SEM["dve"], var_done[NT - 1])
                last = None
                for t in range(NT):
                    last = S.activation(sst[:, t * 8 + 2:t * 8 + 3], sst[:, t * 8 + 3:t * 8 + 4],
                                        AF.Ln, bias=kconst[:, 0:1])
                last.then_inc(SEM["act"], 1)
            em("scalar", f_ln4b)

            def f_rstd(S, ln4b=ln4b):
                S.wait_ge(SEM["act"], ln4b)
                last = None
                for t in range(NT):
                    last = S.activation(sst[:, t * 8 + 4:t * 8 + 5], sst[:, t * 8 + 2:t * 8 + 3],
                                        AF.Exp, scale=-0.5)
                last.then_inc(SEM["act"], 1)
            em("scalar", f_rstd)
            negm_done = post("dve")

            def f_negm(V, rstd_done=rstd_done):
                V.wait_ge(SEM["act"], rstd_done)
                last = None
                for t in range(NT):
                    last = V.tensor_scalar_mul(sst[:, t * 8 + 5:t * 8 + 6],
                                               sst[:, t * 8 + 0:t * 8 + 1], -1.0 / D)
                last.then_inc(SEM["dve"], 1)
            em("vector", f_negm)
            if ATTN_STOP == "b1stats":
                carry.setdefault("_ln2_done", {})[tg] = cnt["dve"]
                carry.setdefault("_agx_st", {})[tg] = cnt["s_agx"]
                continue
            evk_all = None
            prev_tile_ev = None
            for t in range(NT):
                ncol = sst[:, t * 8 + 5:t * 8 + 6]
                rcol = sst[:, t * 8 + 4:t * 8 + 5]
                vck, trk, evk = {}, {}, {}
                for c in range(NCI):
                    slot = c % 2
                    b3 = (c % 3) * 512
                    vck[c] = post("dve")

                    def f_vc(V, c=c, t=t, slot=slot, ncol=ncol, rcol=rcol, trk=trk,
                             negm_done=negm_done, ptv=prev_tile_ev):
                        if c == 0:
                            V.wait_ge(SEM["dve"], negm_done)
                            if ptv is not None:
                                V.wait_ge(SEM["dve"], ptv)
                        V.tensor_scalar(xf32[:, slot * 128:(slot + 1) * 128],
                                        xt[:, t * D + c * 128:t * D + (c + 1) * 128],
                                        ncol, rcol, ALU.add, ALU.mult).then_inc(SEM["dve"], 1)
                    em("vector", f_vc)
                    trk[c] = post("pe")

                    def f_tc(T, c=c, slot=slot, b3=b3, vck=vck, evk=evk, t=t, tg=tg):
                        T.wait_ge(SEM["dve"], vck[c])
                        if c >= 3:
                            T.wait_ge(SEM["dve"], evk[c - 3])
                        elif c < 3 and t == 0:
                            pass
                        T.transpose(psD[:, b3:b3 + 128], xf32[:, slot * 128:(slot + 1) * 128],
                                    ident_f[:, :]).then_inc(SEM["pe"], 1)
                    em("tensor", f_tc)
                    evk[c] = post("dve")

                    def f_ec(V, c=c, t=t, slot=slot, b3=b3, trk=trk):
                        V.wait_ge(SEM["pe"], trk[c])
                        V.tensor_copy(xl2T_v[:, c, t * 128:(t + 1) * 128],
                                      psD[:, b3:b3 + 128]).then_inc(SEM["dve"], 1)
                    em("vector", f_ec)
                evk_all = evk[NCI - 1]
                prev_tile_ev = evk[NCI - 1]
            carry.setdefault("_ln2_done", {})[tg] = evk_all
            if ATTN_STOP == "b1nostore":
                carry.setdefault("_agx_st", {})[tg] = cnt["s_agx"]
                continue
            post("s_agx", 16)

            def f_st(S, tg=tg, evk_all=evk_all):
                S.wait_ge(SEM["dve"], evk_all)
                S.dma_start(out=agx_in.rearrange("(c p) n -> p c n", p=128)[:, :, tg * TPR:(tg + 1) * TPR],
                            in_=xl2T_v).then_inc(SEM["s_agx"], 16)
            em("sync", f_st)
            carry.setdefault("_agx_st", {})[tg] = cnt["s_agx"]

        if ATTN_STOP in ("b1", "b1stats", "b1nostore"):
            carry["rs_done_cc"] = cnt["cc"]
            carry["rs_out"] = x1s
            carry["last_store_sq"] = [("s_agx", cnt["s_agx"])]
            carry["last_xev_dve"] = cnt["dve"]
            carry["prev_sd"] = None
            carry["prev2_sd"] = None
            return carry
        post("cc", 1)
        ag_done = cnt["cc"]
        agx_st_last = carry["_agx_st"][3]

        def f_ag(G, agx_st_last=agx_st_last):
            G.wait_ge(SEM["s_agx"], agx_st_last)
            G.collective_compute("AllGather", ALU.bypass,
                                 replica_groups=[[0, 1], [2, 3], [4, 5], [6, 7]],
                                 ins=[agx_in.opt()], outs=[agx_out.opt()]).then_inc(SEM["cc"], 1)
        em("gpsimd", f_ag)

        if ATTN_STOP == "b1ag":
            carry["rs_done_cc"] = ag_done
            carry["rs_out"] = x1s
            carry["rs_ready"] = ("cc", 0)
            carry["last_store_sq"] = [("s_agx", cnt["s_agx"])]
            carry["last_xev_dve"] = cnt["dve"]
            carry["prev_sd"] = None
            carry["prev2_sd"] = None
            return carry
        wnat = wgu_flat[:, 0:16384].rearrange("p (c n) -> p c n", c=8)
        wonat = wgu_flat[:, 16384:32768].rearrange("p (c n) -> p c n", c=16)
        wstage = wdb_flat[:, 0:1024]
        wl = {}
        for mi, (wsrc, rows) in enumerate([(wq, 1024), (wk, 1024), (wv, 1024), (wo, D)]):
            post("g_attw", 16)

            def f_wl(G, mi=mi, wsrc=wsrc, rows=rows):
                if mi > 0:
                    G.wait_ge(SEM["pe"], wl[mi - 1][1])
                if mi == 0 and carry.get("_wT_wait_pe") is not None:
                    G.wait_ge(SEM["pe"], carry["_wT_wait_pe"])
                if rows == 1024:
                    G.dma_start(out=wnat, in_=wsrc.rearrange("(c p) n -> p c n", p=128)
                                ).then_inc(SEM["g_attw"], 16)
                else:
                    G.dma_start(out=wonat, in_=wsrc.rearrange("(c p) n -> p c n", p=128)
                                ).then_inc(SEM["g_attw"], 16)
            em("gpsimd", f_wl)
            ld = cnt["g_attw"]
            if mi < 3:
                dst = [wqT_s, wkT_s, wvT_s][mi]
                last_pe = None
                for c in range(NCI):
                    tpe = post("pe")

                    def f_tr(T, mi=mi, c=c, ld=ld, tpe=tpe,
                             pv=wl.get((mi - 1, 15)) if mi >= 1 else None):
                        if c == 0:
                            T.wait_ge(SEM["g_attw"], ld)
                            if pv is not None:
                                T.wait_ge(SEM["dve"], pv)
                        if c >= 1:
                            T.wait_ge(SEM["dve"], wl[(mi, c - 1)])
                        last = None
                        for hc in range(8):
                            last = T.transpose(psTr[:, hc * 128:(hc + 1) * 128],
                                               wnat[:, hc, c * 128:(c + 1) * 128], ident_bf[:, :])
                        last.then_inc(SEM["pe"], 1)
                    em("tensor", f_tr)
                    wl[(mi, c)] = post("dve")

                    def f_ev(V, tpe=tpe, pst=wl.get(("st", mi, c - 1)) if c > 0 else
                             (wl.get(("st", mi - 1, 15)) if mi > 0 else None)):
                        V.wait_ge(SEM["pe"], tpe)
                        if pst is not None:
                            V.wait_ge(SEM["s_wT"], pst)
                        V.tensor_copy(wstage, psTr[:, 0:1024]).then_inc(SEM["dve"], 1)
                    em("vector", f_ev)
                    post("s_wT", 16)

                    def f_sd(S, dst=dst, c=c, ev=wl[(mi, c)]):
                        S.wait_ge(SEM["dve"], ev)
                        S.dma_start(out=dst[c * 128:(c + 1) * 128, :], in_=wstage
                                    ).then_inc(SEM["s_wT"], 16)
                    em("sync", f_sd)
                    wl[("st", mi, c)] = cnt["s_wT"]
                    last_pe = tpe
                wl[mi] = (ld, last_pe)
            else:
                last_pe = None
                wost = wdb_flat[:, 1024:3072]
                for jc in range(8):
                    tpe = post("pe")

                    def f_tr(T, jc=jc, ld=ld, tpe=tpe,
                             pv=wl.get((2, 15))):
                        if jc == 0:
                            T.wait_ge(SEM["g_attw"], ld)
                            if pv is not None:
                                T.wait_ge(SEM["dve"], pv)
                        if jc >= 1:
                            T.wait_ge(SEM["dve"], wl[(3, jc - 1, 1)])
                        last = None
                        for oc in range(8):
                            last = T.transpose(psTr[:, oc * 128:(oc + 1) * 128],
                                               wonat[:, oc, jc * 128:(jc + 1) * 128], ident_bf[:, :])
                        last.then_inc(SEM["pe"], 1)
                    em("tensor", f_tr)
                    e1 = post("dve")
                    wl[(3, jc, 0)] = e1

                    def f_ev(V, tpe=tpe, jc=jc, pst=wl.get(("sto", jc - 1))):
                        V.wait_ge(SEM["pe"], tpe)
                        if pst is not None:
                            V.wait_ge(SEM["s_wT"], pst)
                        V.tensor_copy(wost[:, 0:1024], psTr[:, 0:1024]).then_inc(SEM["dve"], 1)
                    em("vector", f_ev)
                    tpe2 = post("pe")

                    def f_tr2(T, jc=jc, tpe2=tpe2, e1=e1):
                        T.wait_ge(SEM["dve"], e1)
                        last = None
                        for oc in range(8, 16):
                            last = T.transpose(psTr[:, (oc - 8) * 128:(oc - 7) * 128],
                                               wonat[:, oc, jc * 128:(jc + 1) * 128], ident_bf[:, :])
                        last.then_inc(SEM["pe"], 1)
                    em("tensor", f_tr2)
                    e2 = post("dve")
                    wl[(3, jc, 1)] = e2

                    def f_ev2(V, tpe2=tpe2):
                        V.wait_ge(SEM["pe"], tpe2)
                        V.tensor_copy(wost[:, 1024:2048], psTr[:, 0:1024]).then_inc(SEM["dve"], 1)
                    em("vector", f_ev2)
                    post("s_wT", 16)

                    def f_sd(S, jc=jc, e2=e2):
                        S.wait_ge(SEM["dve"], e2)
                        S.dma_start(out=woT_s[jc * 128:(jc + 1) * 128, :], in_=wost
                                    ).then_inc(SEM["s_wT"], 16)
                    em("sync", f_sd)
                    wl[("sto", jc)] = cnt["s_wT"]
                    last_pe = tpe2
                wl[3] = (ld, last_pe)
        wT_done_sq = cnt["s_wT"]

        slab = wgu_flat[:, 0:32768].rearrange("p (c n) -> p c n", c=NCI)
        wsl = wdb_flat[:, 0:4096].rearrange("p (s n) -> p s n", s=2)
        qstg = wdb_flat[:, 4096:8192].rearrange("p (s n) -> p s n", s=2)
        b3_store = {}
        prev_ev = None
        prev_mm = None
        for tb in range(2):
            post("s_slab", 16)

            def f_sl(S, tb=tb, prev_mm=prev_mm):
                S.wait_ge(SEM["cc"], ag_done)
                S.wait_ge(SEM["s_wT"], wT_done_sq)
                if prev_mm is not None:
                    S.wait_ge(SEM["pe"], prev_mm)
                S.dma_start(out=slab,
                            in_=agx_out[tb * D:(tb + 1) * D, :].rearrange("(c p) n -> p c n", p=128)
                            ).then_inc(SEM["s_slab"], 16)
            em("sync", f_sl)
            slab_ld = cnt["s_slab"]
            for h in range(8):
                for xi, (wTs, dstq) in enumerate([(wqT_s, qT_sd), (wkT_s, kT_sd), (wvT_s, vT_sd)]):
                    si = (h * 3 + xi) % 2
                    ldsem = "s_ld3a" if si == 0 else "s_ld3b"
                    post(ldsem, 16)

                    def f_wv(S, wTs=wTs, h=h, si=si, ldsem=ldsem,
                             g=b3_store.get(("mmslot", si))):
                        if g is not None:
                            S.wait_ge(SEM["pe"], g)
                        S.dma_start(out=wsl[:, si, :].rearrange("p (c n) -> p c n", c=NCI),
                                    in_=wTs[:, h * 128:(h + 1) * 128]
                                    .rearrange("(c p) n -> p c n", p=128)
                                    ).then_inc(SEM[ldsem], 16)
                    em("sync", f_wv)
                    wld = (ldsem, cnt[ldsem])
                    mm = post("pe")

                    def f_mm(T, si=si, wld=wld, slab_ld=slab_ld, mm=mm, prev_ev=prev_ev):
                        T.wait_ge(SEM[wld[0]], wld[1])
                        T.wait_ge(SEM["s_slab"], slab_ld)
                        if prev_ev is not None:
                            T.wait_ge(SEM["dve"], prev_ev)
                        last = None
                        for c in range(NCI):
                            ws = wsl[:, si, c * 128:(c + 1) * 128]
                            for qq in range(4):
                                last = T.matmul(psD[:, qq * 512:(qq + 1) * 512], ws,
                                                slab[:, c, qq * 512:(qq + 1) * 512],
                                                start=(c == 0), stop=(c == NCI - 1))
                        last.then_inc(SEM["pe"], 1)
                    em("tensor", f_mm)
                    b3_store[("mmslot", si)] = mm
                    ev = post("dve")

                    def f_ev(V, mm=mm, si=si, pst=b3_store.get(("stslot", si))):
                        V.wait_ge(SEM["pe"], mm)
                        if pst is not None:
                            V.wait_ge(SEM["s_qkva" if si == 0 else "s_qkvb"], pst)
                        V.tensor_copy(qstg[:, si, :], psD[:, :]).then_inc(SEM["dve"], 1)
                    em("vector", f_ev)
                    prev_ev = ev
                    prev_mm = mm
                    qsem = "s_qkva" if si == 0 else "s_qkvb"
                    post(qsem, 16)

                    def f_qs(S, dstq=dstq, h=h, tb=tb, ev=ev, si=si, qsem=qsem):
                        S.wait_ge(SEM["dve"], ev)
                        S.dma_start(out=dstq[h * 128:(h + 1) * 128, tb * D:(tb + 1) * D],
                                    in_=qstg[:, si, :]).then_inc(SEM[qsem], 16)
                    em("sync", f_qs)
                    b3_store[(tb, h, xi)] = cnt[qsem]
                    b3_store[("stslot", si)] = cnt[qsem]
        qkv_done_sq = (cnt["s_qkva"], cnt["s_qkvb"])
        if ATTN_STOP == "b3":
            post("g_acc", 16)

            def f_dummy(G, q=qkv_done_sq):
                G.wait_ge(SEM["s_qkva"], q[0])
                G.wait_ge(SEM["s_qkvb"], q[1])
                G.dma_start(out=rs_out, in_=x1s).then_inc(SEM["g_acc"], 16)
            em("gpsimd", f_dummy)
            carry["rs_done_cc"] = cnt["cc"]
            carry["rs_ready"] = ("g_acc", cnt["g_acc"])
            carry["rs_out"] = rs_out
            carry["last_store_sq"] = [("s_agx", cnt["s_agx"])]
            carry["last_xev_dve"] = cnt["dve"]
            carry["prev_sd"] = None
            carry["prev2_sd"] = None
            return carry

        last_zT_ev_holder = {"v": None, "zst": None}
        qTb = wdb_flat[:, 0:4096]
        kTb = wdb_flat[:, 4096:8192]
        vTb = wdb_flat[:, 8192:12288]
        vb = wgu_flat[:, 0:4096].rearrange("p (c n) -> p c n", c=32)
        PTb = wgu_flat[:, 4096:6144]
        zstg = wgu_flat[:, 38912:39936]
        P_pool = [[wgu_flat[:, 6144 + (par * 4 + i) * 4096: 6144 + (par * 4 + i + 1) * 4096]
                   for i in range(4)] for par in range(2)]
        psS = [psGU[:, 0:1024], psD[:, 0:1024], psD[:, 1024:2048]]
        psO = psGU[:, 1024:1536]
        C = lambda i: sst[:, i:i + 1]

        def emit_av(h, qg, p_done, P_bufs):
            nk = 4 * qg + 4
            npair = nk // 2
            av_mm = {}
            tr_t = {}
            ev_t = {}
            for pj in range(npair):
                sl = pj % 2
                tr_t[pj] = post("pe")

                def f_tr(T, pj=pj, p_done=p_done, P_bufs=P_bufs, ev_t=ev_t):
                    if pj == 0:
                        for pf in p_done:
                            T.wait_ge(SEM["dve"], pf)
                    if pj >= 1:
                        T.wait_ge(SEM["act"], ev_t[pj - 1])
                    last = None
                    for half in range(2):
                        kc = pj * 2 + half
                        for i in range(4):
                            last = T.transpose(
                                psTr[:, half * 512 + i * 128:half * 512 + (i + 1) * 128],
                                P_bufs[i][:, kc * 128:(kc + 1) * 128], ident_bf[:, :])
                    last.then_inc(SEM["pe"], 1)
                em("tensor", f_tr)
                ev_t[pj] = post("act")

                def f_ev(S, pj=pj, sl=sl, tr=tr_t[pj], av_mm=av_mm):
                    S.wait_ge(SEM["pe"], tr)
                    pv = av_mm.get(pj - 2)
                    if pv is not None:
                        S.wait_ge(SEM["pe"], pv)
                    S.copy(PTb[:, sl * 1024:(sl + 1) * 1024],
                           psTr[:, 0:1024]).then_inc(SEM["act"], 1)
                em("scalar", f_ev)
                av_mm[pj] = post("pe")

                def f_av(T, pj=pj, sl=sl, npair=npair, ev=ev_t[pj],
                         lzv=last_zT_ev_holder["v"]):
                    T.wait_ge(SEM["act"], ev)
                    if pj == 0 and lzv is not None:
                        T.wait_ge(SEM["dve"], lzv)
                    last = None
                    for half in range(2):
                        kc = pj * 2 + half
                        last = T.matmul(psO, vb[:, kc, :],
                                        PTb[:, sl * 1024 + half * 512:sl * 1024 + (half + 1) * 512],
                                        start=(kc == 0), stop=(kc == nk - 1))
                    last.then_inc(SEM["pe"], 1)
                em("tensor", f_av)
            zev = post("dve")
            zsl = qg % 2

            def f_z(V, qg=qg, zsl=zsl, mm=av_mm[npair - 1], pzs=last_zT_ev_holder["zst"]):
                V.wait_ge(SEM["pe"], mm)
                if pzs is not None:
                    V.wait_ge(SEM["s_zt"], pzs)
                V.tensor_copy(zstg[:, zsl * 512:(zsl + 1) * 512], psO).then_inc(SEM["dve"], 1)
            em("vector", f_z)
            post("s_zt", 16)

            def f_zs(S, h=h, qg=qg, zsl=zsl, zev=zev):
                S.wait_ge(SEM["dve"], zev)
                S.dma_start(out=zT_sd[h * 128:(h + 1) * 128, qg * 512:(qg + 1) * 512],
                            in_=zstg[:, zsl * 512:(zsl + 1) * 512]).then_inc(SEM["s_zt"], 16)
            em("sync", f_zs)
            last_zT_ev_holder["v"] = zev
            last_zT_ev_holder["zst"] = cnt["s_zt"]
            return tr_t[npair - 1]

        last_head_S_pe = None
        prev_head_vdve = None
        gk_hist = {}
        gk = 0
        P_free = {0: None, 1: None}
        final_zst = None
        for h in range(8):
            post("s_hld", 16 * 3)
            prev_S = last_head_S_pe

            def f_ld(S, h=h, prev_S=prev_S, pvd=prev_head_vdve):
                if h == 0:
                    S.wait_ge(SEM["s_qkva"], qkv_done_sq[0])
                    S.wait_ge(SEM["s_qkvb"], qkv_done_sq[1])
                else:
                    S.wait_ge(SEM["pe"], prev_S)
                    S.wait_ge(SEM["dve"], pvd)
                S.dma_start(out=qTb, in_=qT_sd[h * 128:(h + 1) * 128, :]).then_inc(SEM["s_hld"], 16)
                S.dma_start(out=kTb, in_=kT_sd[h * 128:(h + 1) * 128, :]).then_inc(SEM["s_hld"], 16)
                S.dma_start(out=vTb, in_=vT_sd[h * 128:(h + 1) * 128, :]).then_inc(SEM["s_hld"], 16)
            em("sync", f_ld)
            ld_t = cnt["s_hld"]
            vtr = {}
            for b4 in range(4):
                tpe = post("pe")

                def f_vt(T, b4=b4, ld_t=ld_t, h=h, vtr=vtr, lz=last_zT_ev_holder["v"]):
                    if b4 == 0:
                        T.wait_ge(SEM["s_hld"], ld_t)
                        if lz is not None:
                            T.wait_ge(SEM["dve"], lz)
                    if b4 >= 1:
                        T.wait_ge(SEM["dve"], vtr[b4 - 1])
                    last = None
                    for j in range(8):
                        kc = b4 * 8 + j
                        last = T.transpose(psTr[:, j * 128:(j + 1) * 128],
                                           vTb[:, kc * 128:(kc + 1) * 128], ident_bf[:, :])
                    last.then_inc(SEM["pe"], 1)
                em("tensor", f_vt)
                vtr[b4] = post("dve")

                def f_ve(V, b4=b4, tpe=tpe):
                    V.wait_ge(SEM["pe"], tpe)
                    V.tensor_copy(wgu_flat[:, b4 * 1024:(b4 + 1) * 1024],
                                  psTr[:, 0:1024]).then_inc(SEM["dve"], 1)
                em("vector", f_ve)
            v_ready = vtr[3]
            prev_head_vdve = v_ready

            pend = None
            for qg in range(8):
                par = qg % 2
                P_bufs = P_pool[par]
                p_done = []
                for qt4 in range(4):
                    qt = qg * 4 + qt4
                    klen = 128 * (qt + 1)
                    kgs = (klen + 1023) // 1024
                    Pb = P_bufs[qt4]
                    smms = {}
                    exp_t = {}
                    for kg in range(kgs):
                        w = min(1024, klen - kg * 1024)
                        sset = gk % 3
                        prev_exp = gk_hist.get(gk - 3)
                        smms[kg] = post("pe")

                        def f_s(T, qt=qt, kg=kg, w=w, sset=sset, v_ready=v_ready,
                                prev_exp=prev_exp):
                            if qt == 0 and kg == 0:
                                T.wait_ge(SEM["dve"], v_ready)
                            if prev_exp is not None:
                                T.wait_ge(SEM["act"], prev_exp)
                            last = None
                            for off in range(0, w, 512):
                                ww = min(512, w - off)
                                last = T.matmul(psS[sset][:, off:off + ww],
                                                qTb[:, qt * 128:(qt + 1) * 128],
                                                kTb[:, kg * 1024 + off:kg * 1024 + off + ww],
                                                start=True, stop=True)
                            last.then_inc(SEM["pe"], 1)
                        em("tensor", f_s)
                        mx_d = post("dve")

                        def f_mx(V, kg=kg, w=w, sset=sset, smm=smms[kg]):
                            V.wait_ge(SEM["pe"], smm)
                            V.reduce_max(C(kg), psS[sset][:, 0:w], axis=AX
                                         ).then_inc(SEM["dve"], 1)
                        em("vector", f_mx)
                        mg_d = post("dve")

                        def f_ng(V, kg=kg, mx_d=mx_d):
                            V.wait_ge(SEM["dve"], mx_d)
                            V.tensor_scalar_mul(C(4 + kg), C(kg), -0.08838834764831845
                                                ).then_inc(SEM["dve"], 1)
                        em("vector", f_ng)
                        exp_t[kg] = post("act")
                        pfree = P_free[par]

                        def f_ex(S, kg=kg, w=w, sset=sset, mg_d=mg_d, Pb=Pb, pfree=pfree):
                            S.wait_ge(SEM["dve"], mg_d)
                            if kg == 0 and pfree is not None:
                                S.wait_ge(SEM["pe"], pfree)
                            S.activation(Pb[:, kg * 1024:kg * 1024 + w], psS[sset][:, 0:w],
                                         AF.Exp, bias=C(4 + kg), scale=0.08838834764831845,
                                         accum_out=C(8 + kg)).then_inc(SEM["act"], 1)
                        em("scalar", f_ex)
                        gk_hist[gk] = exp_t[kg]
                        gk += 1
                    dk = kgs - 1
                    dw = klen - dk * 1024
                    mk_d = post("dve")

                    def f_msk(V, qt=qt, Pb=Pb, et=exp_t[dk]):
                        V.wait_ge(SEM["act"], et)
                        V.tensor_tensor(Pb[:, qt * 128:qt * 128 + 128],
                                        Pb[:, qt * 128:qt * 128 + 128], tri01[:, :], ALU.mult
                                        ).then_inc(SEM["dve"], 1)
                    em("vector", f_msk)
                    post("dve")

                    def f_dsum(V, dk=dk, dw=dw, Pb=Pb, mk_d=mk_d):
                        V.wait_ge(SEM["dve"], mk_d)
                        V.reduce_sum(C(8 + dk), Pb[:, dk * 1024:dk * 1024 + dw], axis=AX
                                     ).then_inc(SEM["dve"], 1)
                    em("vector", f_dsum)

                    gmax = 128 * (4 * qg + 4)
                    if klen < gmax:
                        post("dve")

                        def f_zz(V, Pb=Pb, klen=klen, gmax=gmax, prev_tgt=cnt["dve"] - 1):
                            V.wait_ge(SEM["dve"], prev_tgt)
                            V.memset(Pb[:, klen:gmax], 0.0).then_inc(SEM["dve"], 1)
                        em("vector", f_zz)

                    def em_v2(fn, wait_act=None):
                        tgt = post("dve")

                        def f(V, fn=fn, wait_act=wait_act, prev_tgt=tgt - 1):
                            V.wait_ge(SEM["dve"], prev_tgt)
                            if wait_act is not None:
                                V.wait_ge(SEM["act"], wait_act)
                            fn(V).then_inc(SEM["dve"], 1)
                        em("vector", f)
                        return tgt

                    if kgs == 1:
                        em_v2(lambda V: V.reciprocal(C(22), C(8)))
                        pf = em_v2(lambda V, Pb=Pb, klen=klen:
                                   V.tensor_scalar(Pb[:, 0:klen], Pb[:, 0:klen], C(22), None,
                                                   ALU.mult))
                    else:
                        em_v2(lambda V, kgs=kgs: V.reduce_max(C(20), sst[:, 0:kgs], axis=AX))
                        for kg in range(kgs):
                            em_v2(lambda V, kg=kg: V.tensor_tensor(C(12 + kg), C(kg), C(20),
                                                                   ALU.subtract))
                        am_t = post("act")
                        amw = cnt["dve"]

                        def f_am(S, kgs=kgs, amw=amw):
                            S.wait_ge(SEM["dve"], amw)
                            last = None
                            for kg in range(kgs):
                                last = S.activation(C(12 + kg), C(12 + kg), AF.Exp,
                                                    scale=0.08838834764831845)
                            last.then_inc(SEM["act"], 1)
                        em("scalar", f_am)
                        em_v2(lambda V: V.tensor_tensor(C(21), C(12), C(8), ALU.mult),
                              wait_act=am_t)
                        for kg in range(1, kgs):
                            em_v2(lambda V, kg=kg: V.tensor_tensor(C(23), C(12 + kg),
                                                                   C(8 + kg), ALU.mult))
                            em_v2(lambda V: V.tensor_tensor(C(21), C(21), C(23), ALU.add))
                        em_v2(lambda V: V.reciprocal(C(22), C(21)))
                        pf = None
                        for kg in range(kgs):
                            w = min(1024, klen - kg * 1024)
                            em_v2(lambda V, kg=kg: V.tensor_tensor(C(16 + kg), C(12 + kg),
                                                                   C(22), ALU.mult))
                            pf = em_v2(lambda V, kg=kg, w=w, Pb=Pb:
                                       V.tensor_scalar(Pb[:, kg * 1024:kg * 1024 + w],
                                                       Pb[:, kg * 1024:kg * 1024 + w],
                                                       C(16 + kg), None, ALU.mult))
                    p_done.append(pf)
                if pend is not None:
                    P_free[pend[0] % 2] = emit_av(h, pend[0], pend[1], P_pool[pend[0] % 2])
                pend = (qg, p_done)
            P_free[pend[0] % 2] = emit_av(h, pend[0], pend[1], P_pool[pend[0] % 2])
            pend = None
            last_head_S_pe = smms[kgs - 1]
        attn_done_dve = last_zT_ev_holder["v"]
        zT_done_sq = last_zT_ev_holder["zst"]

        if ATTN_STOP == "b4":
            post("g_acc", 16)

            def f_dummy(G, q=last_zT_ev_holder["zst"]):
                G.wait_ge(SEM["s_zt"], q)
                G.dma_start(out=rs_out, in_=x1s).then_inc(SEM["g_acc"], 16)
            em("gpsimd", f_dummy)
            carry["rs_done_cc"] = cnt["cc"]
            carry["rs_ready"] = ("g_acc", cnt["g_acc"])
            carry["rs_out"] = rs_out
            carry["last_store_sq"] = [("s_agx", cnt["s_agx"])]
            carry["last_xev_dve"] = cnt["dve"]
            carry["prev_sd"] = None
            carry["prev2_sd"] = None
            return carry

        woT_res = wdb_flat[:, 0:16384].rearrange("p (z n) -> p z n", z=8)
        zT_v = wgu_flat[:, 6144:38912].rearrange("p (h n) -> p h n", h=8)
        post("s_hld", 32)

        def f_wo(S, attn_done_dve=attn_done_dve, lsp=last_head_S_pe, zds=zT_done_sq):
            S.wait_ge(SEM["dve"], attn_done_dve)
            S.wait_ge(SEM["pe"], lsp)
            S.wait_ge(SEM["s_zt"], zds)
            S.dma_start(out=woT_res,
                        in_=woT_s.rearrange("(z p) n -> p z n", p=128)).then_inc(SEM["s_hld"], 16)
            S.dma_start(out=zT_v,
                        in_=zT_sd.rearrange("(h p) n -> p h n", p=128)).then_inc(SEM["s_hld"], 16)
        em("sync", f_wo)
        wo_ld = cnt["s_hld"]
        rs_stage = xt[:, :].rearrange("p (s n) -> p s n", s=NT)
        prev_ev5 = {}
        last_rs_sq = None
        for tt in range(32):
            sl = tt % 2
            mm5 = post("pe")

            def f_m5(T, tt=tt, sl=sl, wo_ld=wo_ld, prev_ev5=prev_ev5):
                if tt == 0:
                    T.wait_ge(SEM["s_hld"], wo_ld)
                pv = prev_ev5.get(tt - 1)
                if pv is not None:
                    T.wait_ge(SEM["dve"], pv)
                last = None
                for zc in range(8):
                    zs = zT_v[:, zc, tt * 128:(tt + 1) * 128]
                    for qq in range(4):
                        last = T.matmul(psD[:, qq * 512:(qq + 1) * 512], zs,
                                        woT_res[:, zc, qq * 512:(qq + 1) * 512],
                                        start=(zc == 0), stop=(zc == 7))
                last.then_inc(SEM["pe"], 1)
            em("tensor", f_m5)
            ev5 = post("dve")

            def f_e5(V, tt=tt, sl=sl, mm5=mm5):
                V.wait_ge(SEM["pe"], mm5)
                if tt >= 2:
                    V.wait_ge(SEM[f"s_rs{sl}"], carry["_rs_st"][tt - 2])
                V.tensor_copy(rs_stage[:, sl, :], psD[:, :]).then_inc(SEM["dve"], 1)
            em("vector", f_e5)
            prev_ev5[tt] = ev5
            post(f"s_rs{sl}", 16)

            def f_s5(S, tt=tt, sl=sl, ev5=ev5):
                S.wait_ge(SEM["dve"], ev5)
                S.dma_start(out=rs_in[tt * 128:(tt + 1) * 128, :], in_=rs_stage[:, sl, :]
                            ).then_inc(SEM[f"s_rs{sl}"], 16)
            em("sync", f_s5)
            carry.setdefault("_rs_st", {})[tt] = cnt[f"s_rs{sl}"]
            last_rs_sq = (cnt["s_rs0"], cnt["s_rs1"])

        post("cc", 1)
        rs_done = cnt["cc"]

        def f_rs(G, last_rs_sq=last_rs_sq):
            G.wait_ge(SEM["s_rs0"], last_rs_sq[0])
            G.wait_ge(SEM["s_rs1"], last_rs_sq[1])
            G.collective_compute("ReduceScatter", ALU.add,
                                 replica_groups=[[0, 1], [2, 3], [4, 5], [6, 7]],
                                 ins=[rs_in.opt()], outs=[rs_out.opt()]).then_inc(SEM["cc"], 1)
        em("gpsimd", f_rs)
        carry["rs_done_cc"] = rs_done
        carry["rs_ready"] = ("cc", rs_done)
        carry["rs_out"] = rs_out
        carry["last_xev_dve"] = prev_ev5[31]
        carry["last_store_sq"] = [("s_rs0", cnt["s_rs0"]), ("s_rs1", cnt["s_rs1"])]
        carry["prev_sd"] = None
        carry["prev2_sd"] = None
        return carry

    def emit_ffn(ffn_idx, dst, carry):
        for r in range(NR):
            use_cache = not (ffn_idx == 0 and r == 0)
            post("s_x", 16)
            ls = carry.get("last_store_sq")

            def f_x(S, r=r, ls=ls, ffn_idx=ffn_idx):
                if ls is not None:
                    for sname, v in ls:
                        S.wait_ge(SEM[sname], v)
                src = x_in if ffn_idx == 0 else x1s
                S.dma_start(out=xt[:, :].rearrange("p (t n) -> p t n", t=NT),
                            in_=src[r * TPR:(r + 1) * TPR, :].rearrange("(t p) n -> p t n", p=128)
                            ).then_inc(SEM["s_x"], 16)
            em("sync", f_x)
            x_ready = ("s_x", cnt["s_x"])
            if ffn_idx == 1:
                post("g_acc", 16)
                xr = x_ready

                def f_x2(G, r=r, xr=xr, rsr=carry.get("rs_ready")):
                    G.wait_ge(SEM[xr[0]], xr[1])
                    if rsr is not None and rsr[1]:
                        G.wait_ge(SEM[rsr[0]], rsr[1])
                    G.dma_start(out=xt[:, :].rearrange("p (t n) -> p t n", t=NT),
                                in_=carry["rs_out"][r * TPR:(r + 1) * TPR, :]
                                .rearrange("(t p) n -> p t n", p=128),
                                accum_op=ALU.add).then_inc(SEM["g_acc"], 16)
                em("gpsimd", f_x2)
                x_ready = ("g_acc", cnt["g_acc"])

            if ffn_idx == 0 and r == 0:
                pf_dve, pf_pe = gwT_ready_dve, None
            else:
                pf_dve, pf_pe = carry["last_xev_dve"], None
            cmb_ready, lg_free = emit_ln_round(ffn_idx, r, x_ready, pf_dve, pf_pe)
            if DEBUG and ffn_idx == 0 and r == 0:
                emit_dbg_snapshot("dve", cmb_ready, dbg_xlnT, xlnT[:, :])
                emit_dbg_snapshot("dve", cmb_ready, dbg_lgb, lgb[:, :])
                emit_dbg_snapshot("dve", cmb_ready, dbg_cmb, cmb[:, :])

            for e in range(NEXP):
                key = (ffn_idx, r, e)
                wgu_tgts = carry["wgu_tgts"][key]
                for t in range(NT):
                    step = carry["step"]
                    sd = dict(e=e, t=t, slot=step % 2, wgu_tgts=wgu_tgts)
                    prev_sd = carry.get("prev_sd")
                    prev2_sd = carry.get("prev2_sd")
                    flushed = prev_sd is None or prev_sd.get("_emitted_lag", False)

                    emit_g(sd, prev_sd["h"] if prev_sd else None, cmb_ready)
                    emit_silu(sd, prev2_sd["tr2"] if prev2_sd and "tr2" in prev2_sd else None)
                    if DEBUG and ffn_idx == 0 and r == 0 and e == 0 and t == 0:
                        sd["_dbg_gs"] = emit_dbg_snapshot("act", sd["silu"], dbg_gs,
                                                          gs[:, sd["slot"] * I:(sd["slot"] + 1) * I])
                    if not flushed:
                        emit_lagged(prev_sd, carry)
                    emit_u(sd)
                    if DEBUG and sd.get("_dbg_gs"):
                        sd["_h_extra_wait"] = sd["_dbg_gs"]
                    emit_h(sd, cmb_ready)
                    if not flushed:
                        emit_dnB(prev_sd)
                        emit_xevac(prev_sd)
                        carry["last_xev_dve"] = prev_sd["xev"]
                        prev_sd["_emitted_lag"] = True
                        if DEBUG and prev_sd.get("_dbg_h"):
                            emit_dbg_snapshot("dve", prev_sd["ev2"], dbg_hT,
                                              hT[:, prev_sd["slot"] * NIC * 128:
                                                 (prev_sd["slot"] + 1) * NIC * 128])
                            emit_dbg_snapshot("dve", prev_sd["xev"], dbg_xt,
                                              xt[:, 0:D])
                    if t == 0:
                        wd_key = carry["wd_tgts"].get(key)
                        if wd_key is None:
                            wp = prev_sd["dnB"] if (prev_sd and "dnB" in prev_sd) else None
                            ws = carry.get("cache_sq")
                            carry["wd_tgts"][key] = emit_wd_dma(use_cache, e, wp, ws)
                    sd["wd_tgts"] = carry["wd_tgts"][key]
                    sd["_psTr_free"] = prev_sd["ev2"] if (prev_sd and "ev2" in prev_sd) else None
                    sd["_hT_free"] = (prev2_sd["dnB"] if prev2_sd and "dnB" in prev2_sd else None)
                    sd["_psD_free"] = (prev_sd["xev"] if (prev_sd and "xev" in prev_sd
                                       and not prev_sd.get("_flush_boundary")) else lg_free)
                    carry["prev2_sd"] = prev_sd
                    carry["prev_sd"] = sd
                    carry["step"] = step + 1
                nf, nr_, ne = ffn_idx, r, e + 1
                if ne == NEXP:
                    ne, nr_ = 0, r + 1
                    if nr_ == NR:
                        nr_, nf = 0, ffn_idx + 1
                if not use_cache:
                    carry["cache_sq"] = emit_cache_write(
                        e, wgu_tgts, carry["wd_tgts"][key])
                if nf == ffn_idx:
                    nuc = not (nf == 0 and nr_ == 0)
                    carry["wgu_tgts"][(nf, nr_, ne)] = emit_wgu_dma(
                        nuc, ne, cnt["pe"], carry.get("cache_sq"))

            psd_ = carry["prev_sd"]
            emit_lagged(psd_, carry)
            emit_dnB(psd_)
            emit_xevac(psd_)
            psd_["_emitted_lag"] = True
            psd_["_flush_boundary"] = True
            carry["last_xev_dve"] = psd_["xev"]

            post("s_stx", 16)
            lxv = carry["last_xev_dve"]

            def f_s(S, r=r, dst=dst, lxv=lxv):
                S.wait_ge(SEM["dve"], lxv)
                S.dma_start(out=dst[r * TPR:(r + 1) * TPR, :].rearrange("(t p) n -> p t n", p=128),
                            in_=xt[:, :].rearrange("p (t n) -> p t n", t=NT)
                            ).then_inc(SEM["s_stx"], 16)
            em("sync", f_s)
            carry["last_store_sq"] = [("s_stx", cnt["s_stx"])]
        return carry

    carry = {"step": 0, "wgu_tgts": {}, "wd_tgts": {}, "cache_sq": None,
             "prev_sd": None, "prev2_sd": None, "last_xev_dve": None,
             "last_store_sq": None}
    carry["wgu_tgts"][(0, 0, 0)] = emit_wgu_dma(False, 0, None, None)
    carry["wd_tgts"][(0, 0, 0)] = emit_wd_dma(False, 0, None, None)

    dst1 = y_out if PHASES == "ffn1" else x1s
    carry = emit_ffn(0, dst1, carry)

    if PHASES in ("attn", "full"):
        carry["_wT_wait_pe"] = cnt["pe"]
        carry = emit_attention(carry)
        if PHASES == "attn":
            for r in range(NR):
                post("s_x", 16)
                ls = carry.get("last_store_sq")

                def f_xa(S, r=r, ls=ls, lxd=carry.get("last_xev_dve")):
                    if ls is not None:
                        for sname, v in ls:
                            S.wait_ge(SEM[sname], v)
                    if r == 0 and lxd is not None:
                        S.wait_ge(SEM["dve"], lxd)
                    if r > 0:
                        S.wait_ge(SEM["s_stx"], carry["_at_st"][r - 1])
                    S.dma_start(out=xt[:, :].rearrange("p (t n) -> p t n", t=NT),
                                in_=x1s[r * TPR:(r + 1) * TPR, :]
                                .rearrange("(t p) n -> p t n", p=128)).then_inc(SEM["s_x"], 16)
                em("sync", f_xa)
                xl = cnt["s_x"]
                post("g_acc", 16)

                def f_xb(G, r=r, xl=xl, rsr=carry["rs_ready"]):
                    G.wait_ge(SEM["s_x"], xl)
                    if rsr[1]:
                        G.wait_ge(SEM[rsr[0]], rsr[1])
                    G.dma_start(out=xt[:, :].rearrange("p (t n) -> p t n", t=NT),
                                in_=carry["rs_out"][r * TPR:(r + 1) * TPR, :]
                                .rearrange("(t p) n -> p t n", p=128),
                                accum_op=ALU.add).then_inc(SEM["g_acc"], 16)
                em("gpsimd", f_xb)
                xg = cnt["g_acc"]
                post("s_stx", 16)

                def f_xs(S, r=r, xg=xg):
                    S.wait_ge(SEM["g_acc"], xg)
                    S.dma_start(out=y_out[r * TPR:(r + 1) * TPR, :]
                                .rearrange("(t p) n -> p t n", p=128),
                                in_=xt[:, :].rearrange("p (t n) -> p t n", t=NT)
                                ).then_inc(SEM["s_stx"], 16)
                em("sync", f_xs)
                carry.setdefault("_at_st", {})[r] = cnt["s_stx"]
                carry["last_store_sq"] = [("s_stx", cnt["s_stx"])]
        else:
            carry["wgu_tgts"][(1, 0, 0)] = emit_wgu_dma(True, 0, cnt["pe"], None)
            carry["wd_tgts"][(1, 0, 0)] = emit_wd_dma(True, 0, cnt["pe"], None)
            carry = emit_ffn(1, y_out, carry)

    fins = {k: v for k, v in cnt.items()
            if v > 0 and k not in ("pe", "act", "dve", "cc")}
    fin_cc = cnt["cc"]

    def f_fin(G, fins=fins, fin_cc=fin_cc):
        for k, v in fins.items():
            G.wait_ge(SEM[k], v)
        if fin_cc:
            G.wait_ge(SEM["cc"], fin_cc)
    em("gpsimd", f_fin)

    with nc.Block() as blk:
        blk.sync(lambda E: [f(E) for f in streams["sync"]])
        blk.gpsimd(lambda E: [f(E) for f in streams["gpsimd"]])
        blk.tensor(lambda E: [f(E) for f in streams["tensor"]])
        blk.vector(lambda E: [f(E) for f in streams["vector"]])
        blk.scalar(lambda E: [f(E) for f in streams["scalar"]])

    fes.close()
    es.close()
    return nc




from concourse.bass_utils import run_bass_kernel_spmd


def kernel(L0_attn_x, L0_attn_Wq, L0_attn_Wk, L0_attn_Wv, L0_attn_Wo,
           L0_ffn_gate_w, L0_ffn_w_gate, L0_ffn_w_up, L0_ffn_w_down,
           L0_ffn_sw_gate, L0_ffn_sw_up, L0_ffn_sw_down, L0_ffn_shared_gate_w):
    xs = np.ascontiguousarray(np.asarray(L0_attn_x, dtype=np.float32))
    wq = np.ascontiguousarray(np.asarray(L0_attn_Wq, dtype=np.float32))
    wk = np.ascontiguousarray(np.asarray(L0_attn_Wk, dtype=np.float32))
    wv = np.ascontiguousarray(np.asarray(L0_attn_Wv, dtype=np.float32))
    wo = np.ascontiguousarray(np.asarray(L0_attn_Wo, dtype=np.float32))
    gate_w = np.ascontiguousarray(np.asarray(L0_ffn_gate_w, dtype=np.float32))
    w_gate = np.ascontiguousarray(np.asarray(L0_ffn_w_gate, dtype=np.float32))
    w_up = np.ascontiguousarray(np.asarray(L0_ffn_w_up, dtype=np.float32))
    w_down = np.ascontiguousarray(np.asarray(L0_ffn_w_down, dtype=np.float32))
    sw_gate = np.ascontiguousarray(np.asarray(L0_ffn_sw_gate, dtype=np.float32))
    sw_up = np.ascontiguousarray(np.asarray(L0_ffn_sw_up, dtype=np.float32))
    sw_down = np.ascontiguousarray(np.asarray(L0_ffn_sw_down, dtype=np.float32))
    sgw = np.ascontiguousarray(np.asarray(L0_ffn_shared_gate_w, dtype=np.float32))

    nc = build_program()
    in_maps = []
    for core in range(8):
        b, hf = core // 2, core % 2
        in_maps.append({
            "x": np.ascontiguousarray(xs[b, hf * 2048:(hf + 1) * 2048, :]),
            "gate_w": gate_w, "sgw": sgw,
            "w_gate": w_gate, "w_up": w_up, "w_down": w_down,
            "sw_gate": sw_gate, "sw_up": sw_up, "sw_down": sw_down,
            "wq": np.ascontiguousarray(wq[hf * 1024:(hf + 1) * 1024, :]),
            "wk": np.ascontiguousarray(wk[hf * 1024:(hf + 1) * 1024, :]),
            "wv": np.ascontiguousarray(wv[hf * 1024:(hf + 1) * 1024, :]),
            "wo": np.ascontiguousarray(wo[:, hf * 1024:(hf + 1) * 1024]),
        })
    res = run_bass_kernel_spmd(nc, in_maps, core_ids=list(range(8)))
    out = np.zeros((4, 4096, 2048), np.float32)
    for core in range(8):
        b, hf = core // 2, core % 2
        out[b, hf * 2048:(hf + 1) * 2048, :] = res.results[core]["y"]
    return out



# version 04
# speedup vs baseline: 1.1349x; 1.0674x over previous
import numpy as np
import ml_dtypes
from contextlib import ExitStack

import concourse.bass as bass
import concourse.mybir as mybir

F32 = mybir.dt.float32
BF16 = mybir.dt.bfloat16
AF = mybir.ActivationFunctionType
ALU = mybir.AluOpType
AX = mybir.AxisListType.X
BF = ml_dtypes.bfloat16

D = 2048
I = 1408
NEXP = 12
TOK = 2048
TPR = 512
NT = 4
NR = TOK // TPR
NCI = 16
NIC = 11
EPS = 1e-5

PHASES = "full"
DEBUG = False
import os
ATTN_STOP = os.environ.get("ATTN_STOP", "")


def build_program():
    nc = bass.Bass("TRN2", target_bir_lowering=False)
    es = ExitStack()
    streams = {"sync": [], "gpsimd": [], "tensor": [], "vector": [], "scalar": []}
    cnt = {}
    SEM = {}
    for name in ["pe", "act", "dve", "gq", "sq", "cc",
                 "s_const", "s_x", "s_stx", "s_cache",
                 "g_wgu0", "g_wgu1", "g_wgu2", "g_wgu3",
                 "g_wd0", "g_wd1", "g_wd2", "g_acc", "g_attw",
                 "s_agx", "s_wT", "s_qkv", "s_zt", "s_rs0", "s_rs1",
                 "s_ld3a", "s_ld3b", "s_slab", "s_hld", "s_qkva", "s_qkvb",
                 "g_wuu0", "g_wuu1", "g_wuu2", "g_wuu3"]:
        SEM[name] = es.enter_context(nc.semaphore(name))
        cnt[name] = 0

    def post(name, k=1):
        cnt[name] += k
        return cnt[name]

    def em(eng, fn):
        streams[eng].append(fn)

    x_in = nc.dram_tensor("x", [TOK, D], F32, kind="ExternalInput").ap()
    gate_w = nc.dram_tensor("gate_w", [8, D], F32, kind="ExternalInput").ap()
    sgw = nc.dram_tensor("sgw", [D, 1], F32, kind="ExternalInput").ap()
    w_gate = nc.dram_tensor("w_gate", [8, D, I], F32, kind="ExternalInput").ap()
    w_up = nc.dram_tensor("w_up", [8, D, I], F32, kind="ExternalInput").ap()
    w_down = nc.dram_tensor("w_down", [8, I, D], F32, kind="ExternalInput").ap()
    sw_gate = nc.dram_tensor("sw_gate", [D, 4 * I], F32, kind="ExternalInput").ap()
    sw_up = nc.dram_tensor("sw_up", [D, 4 * I], F32, kind="ExternalInput").ap()
    sw_down = nc.dram_tensor("sw_down", [4 * I, D], F32, kind="ExternalInput").ap()
    wq = nc.dram_tensor("wq", [1024, D], F32, kind="ExternalInput").ap()
    wk = nc.dram_tensor("wk", [1024, D], F32, kind="ExternalInput").ap()
    wv = nc.dram_tensor("wv", [1024, D], F32, kind="ExternalInput").ap()
    wo = nc.dram_tensor("wo", [D, 1024], F32, kind="ExternalInput").ap()
    y_out = nc.dram_tensor("y", [TOK, D], F32, kind="ExternalOutput").ap()

    if DEBUG:
        dbg_xlnT = nc.dram_tensor("dbg_xlnT", [128, NCI * TPR], BF16, kind="ExternalOutput").ap()
        dbg_lgb = nc.dram_tensor("dbg_lgb", [128, NT * 16], F32, kind="ExternalOutput").ap()
        dbg_cmb = nc.dram_tensor("dbg_cmb", [128, NT * 16], F32, kind="ExternalOutput").ap()
        dbg_gs = nc.dram_tensor("dbg_gs", [128, I], BF16, kind="ExternalOutput").ap()
        dbg_h = nc.dram_tensor("dbg_h", [128, I], BF16, kind="ExternalOutput").ap()
        dbg_hT = nc.dram_tensor("dbg_hT", [128, NIC * 128], BF16, kind="ExternalOutput").ap()
        dbg_xt = nc.dram_tensor("dbg_xt", [128, D], F32, kind="ExternalOutput").ap()
    x1s = nc.dram_tensor("x1s", [TOK, D], F32).ap()
    wgu_c = nc.dram_tensor("wgu_c", [NEXP, D, 2 * I], BF16).ap()
    wd_c = nc.dram_tensor("wd_c", [NEXP, I, D], BF16).ap()
    agx_in = nc.dram_tensor("agx_in", [D, TOK], BF16).ap()
    agx_out = nc.dram_tensor("agx_out", [2 * D, TOK], BF16).ap()
    wqT_s = nc.dram_tensor("wqT_s", [D, 1024], BF16).ap()
    wkT_s = nc.dram_tensor("wkT_s", [D, 1024], BF16).ap()
    wvT_s = nc.dram_tensor("wvT_s", [D, 1024], BF16).ap()
    woT_s = nc.dram_tensor("woT_s", [1024, D], BF16).ap()
    qT_sd = nc.dram_tensor("qT_sd", [1024, 4096], BF16).ap()
    kT_sd = nc.dram_tensor("kT_sd", [1024, 4096], BF16).ap()
    vT_sd = nc.dram_tensor("vT_sd", [1024, 4096], BF16).ap()
    zT_sd = nc.dram_tensor("zT_sd", [1024, 4096], BF16).ap()
    rs_in = nc.dram_tensor("rs_in", [4096, D], F32).ap()
    rs_out = nc.dram_tensor("rs_out", [TOK, D], F32).ap()

    ident_bf_d = nc.inline_tensor(np.eye(128, dtype=BF), "ident_bf_d")
    ident_f_d = nc.inline_tensor(np.eye(128, dtype=np.float32), "ident_f_d")
    tri_d = nc.inline_tensor(np.tril(np.ones((128, 128), dtype=np.float32)).astype(BF), "tri01_d")
    kconst_d = nc.inline_tensor(np.full((128, 2), EPS, dtype=np.float32), "kconst_d")

    ident_bf = es.enter_context(nc.sbuf_tensor("ident_bf", [128, 128], BF16))
    ident_f = es.enter_context(nc.sbuf_tensor("ident_f", [128, 128], F32))
    tri01 = es.enter_context(nc.sbuf_tensor("tri01", [128, 128], BF16))
    gwT = es.enter_context(nc.sbuf_tensor("gwT", [128, NCI * 16], F32))
    kconst = es.enter_context(nc.sbuf_tensor("kconst", [128, 2], F32))

    c_tgt = post("s_const", 16 * 4)

    def f_const(S):
        S.dma_start(out=kconst[:, :], in_=kconst_d.ap()).then_inc(SEM["s_const"], 16)
        S.dma_start(out=ident_bf[:, :], in_=ident_bf_d.ap()).then_inc(SEM["s_const"], 16)
        S.dma_start(out=ident_f[:, :], in_=ident_f_d.ap()).then_inc(SEM["s_const"], 16)
        S.dma_start(out=tri01[:, :], in_=tri_d.ap()).then_inc(SEM["s_const"], 16)
    em("sync", f_const)

    fes = ExitStack()
    xt = fes.enter_context(nc.sbuf_tensor("xt", [128, NT * D], F32))
    xlnT = fes.enter_context(nc.sbuf_tensor("xlnT", [128, NCI * TPR], BF16))
    wgu = fes.enter_context(nc.sbuf_tensor("wgu", [128, NCI * 2 * I], BF16))
    wdb = fes.enter_context(nc.sbuf_tensor("wdb", [128, NIC * D], BF16))
    gs = fes.enter_context(nc.sbuf_tensor("gs", [128, 2 * I], BF16))
    hT = fes.enter_context(nc.sbuf_tensor("hT", [128, 2 * NIC * 128], BF16))
    cmb = fes.enter_context(nc.sbuf_tensor("cmb", [128, NT * 16], F32))
    lgb = fes.enter_context(nc.sbuf_tensor("lgb", [128, NT * 16], F32))
    xf32 = fes.enter_context(nc.sbuf_tensor("xf32", [128, 2 * 128], F32))
    xT32 = fes.enter_context(nc.sbuf_tensor("xT32", [128, 2 * 128], F32))
    sst = fes.enter_context(nc.sbuf_tensor("sst", [128, NT * 8], F32))
    t2s = fes.enter_context(nc.sbuf_tensor("t2s", [128, NT * 48], F32))

    psGU = fes.enter_context(nc.psum_tensor("psGU", [128, 1536], F32))
    psD = fes.enter_context(nc.psum_tensor("psD", [128, 2048], F32))
    psTr = fes.enter_context(nc.psum_tensor("psTr", [128, 1024], BF16))

    wgu_v = wgu[:, :].rearrange("p (c n) -> p c n", c=NCI)
    wdb_v = wdb[:, :].rearrange("p (c n) -> p c n", c=NIC)
    xlnT_v = xlnT[:, :].rearrange("p (c n) -> p c n", c=NCI)
    lgps = psD[:, 1536:1552]

    gw_nat_t = fes.enter_context(nc.sbuf_tensor("gw_nat", [8, D], F32))
    gw_nat = gw_nat_t[:, :]
    sg_nat = fes.enter_context(nc.sbuf_tensor("sg_nat", [128, NCI], F32))
    gl = post("s_const", 16 * 2)

    def f_gwl(S):
        S.dma_start(out=gw_nat, in_=gate_w).then_inc(SEM["s_const"], 16)
        S.dma_start(out=sg_nat[:, :].rearrange("p (c one) -> p c one", one=1),
                    in_=sgw.rearrange("(c p) one -> p c one", p=128),
                    allow_slow_non_contiguous=True).then_inc(SEM["s_const"], 16)
    em("sync", f_gwl)

    gz = post("dve")

    def f_gz(V):
        V.memset(gwT[:, :], 0.0).then_inc(SEM["dve"], 1)
    em("vector", f_gz)

    gwtr, gwev = {}, {}
    for c in range(NCI):
        gwtr[c] = post("pe")

        def f_t(T, c=c):
            if c == 0:
                T.wait_ge(SEM["s_const"], gl)
                T.wait_ge(SEM["dve"], gz)
            if c >= 3:
                T.wait_ge(SEM["dve"], gwev[c - 3])
            T.transpose(psD[:, (c % 3) * 512:(c % 3) * 512 + 8],
                        gw_nat_t[0:8, c * 128:(c + 1) * 128], ident_f[0:8, 0:8]).then_inc(SEM["pe"], 1)
        em("tensor", f_t)
        gwev[c] = post("dve")

        def f_v(V, c=c):
            V.wait_ge(SEM["pe"], gwtr[c])
            V.tensor_copy(gwT[:, c * 16:c * 16 + 8],
                          psD[:, (c % 3) * 512:(c % 3) * 512 + 8]).then_inc(SEM["dve"], 1)
        em("vector", f_v)
    gws = post("dve")

    def f_sg(V):
        V.wait_ge(SEM["s_const"], gl)
        V.tensor_copy(gwT[:, :].rearrange("p (c n) -> p c n", c=NCI)[:, :, 8:9],
                      sg_nat[:, :].rearrange("p (c one) -> p c one", one=1)).then_inc(SEM["dve"], 1)
    em("vector", f_sg)
    gwT_ready_dve = gws
    gwT_psD_free_dve = gwev[NCI - 1]

    def src_gu(e, half):
        if e < 8:
            t = w_gate if half == 0 else w_up
            return t[e].rearrange("(c p) i -> p c i", p=128)
        j = e - 8
        t = sw_gate if half == 0 else sw_up
        return t.rearrange("(c p) i -> p c i", p=128)[:, :, j * I:(j + 1) * I]

    def src_d(e):
        if e < 8:
            return w_down[e].rearrange("(c p) d -> p c d", p=128)
        j = e - 8
        return sw_down[j * I:(j + 1) * I, :].rearrange("(c p) d -> p c d", p=128)

    def emit_wgu_dma(use_cache, e, wait_pe, wait_sq, half):
        tgts = []
        for g in range(4):
            cs = slice(g * 4, (g + 1) * 4)
            sname = f"g_wgu{g}" if half == 0 else f"g_wuu{g}"
            post(sname, 16)

            def f(G, g=g, e=e, cs=cs, wp=wait_pe, ws=wait_sq, uc=use_cache,
                  sname=sname, half=half):
                if g == 0:
                    if wp is not None:
                        G.wait_ge(SEM["pe"], wp)
                    if ws is not None:
                        G.wait_ge(SEM["s_cache"], ws)
                if uc:
                    G.dma_start(out=wgu_v[:, cs, half * I:(half + 1) * I],
                                in_=wgu_c[e].rearrange("(c p) n -> p c n", p=128)
                                [:, cs, half * I:(half + 1) * I]).then_inc(SEM[sname], 16)
                else:
                    G.dma_start(out=wgu_v[:, cs, half * I:(half + 1) * I],
                                in_=src_gu(e, half)[:, cs, :]).then_inc(SEM[sname], 16)
            em("gpsimd", f)
            tgts.append((sname, cnt[sname]))
        return tgts

    def emit_wd_dma(use_cache, e, wait_pe, wait_sq):
        tgts = []
        for gi, (c0, c1) in enumerate([(0, 4), (4, 8), (8, 11)]):
            sname = f"g_wd{gi}"
            post(sname, 16)

            def f(G, gi=gi, c0=c0, c1=c1, e=e, wp=wait_pe, ws=wait_sq, uc=use_cache,
                  sname=sname):
                if gi == 0:
                    if wp is not None:
                        G.wait_ge(SEM["pe"], wp)
                    if ws is not None:
                        G.wait_ge(SEM["s_cache"], ws)
                if uc:
                    G.dma_start(out=wdb_v[:, c0:c1, :],
                                in_=wd_c[e].rearrange("(c p) n -> p c n", p=128)[:, c0:c1, :]
                                ).then_inc(SEM[sname], 16)
                else:
                    G.dma_start(out=wdb_v[:, c0:c1, :],
                                in_=src_d(e)[:, c0:c1, :]).then_inc(SEM[sname], 16)
            em("gpsimd", f)
            tgts.append((sname, cnt[sname]))
        return tgts

    def emit_cache_write(e, wgu_tgts, wd_tgts):
        post("s_cache", 32)

        def f(S, e=e, wgu_tgts=wgu_tgts, wd_tgts=wd_tgts):
            for part in wgu_tgts:
                for sname, v in part:
                    S.wait_ge(SEM[sname], v)
            S.dma_start(out=wgu_c[e].rearrange("(c p) n -> p c n", p=128),
                        in_=wgu_v[:, :, :]).then_inc(SEM["s_cache"], 16)
            for sname, v in wd_tgts:
                S.wait_ge(SEM[sname], v)
            S.dma_start(out=wd_c[e].rearrange("(c p) n -> p c n", p=128),
                        in_=wdb_v[:, :, :]).then_inc(SEM["s_cache"], 16)
        em("sync", f)
        return cnt["s_cache"]

    def emit_ln_round(ffn_idx, r, x_ready, psD_free_dve, psD_free_pe):
        stats1 = {}
        for t in range(NT):
            xt_t = xt[:, t * D:(t + 1) * D]
            mcol = sst[:, t * 8 + 0:t * 8 + 1]
            qcol = sst[:, t * 8 + 1:t * 8 + 2]
            q2col = sst[:, t * 8 + 6:t * 8 + 7]
            tcol = sst[:, t * 8 + 2:t * 8 + 3]
            vcol = sst[:, t * 8 + 3:t * 8 + 4]
            stats1[t] = post("dve", 2)

            def f(V, t=t, xt_t=xt_t, mcol=mcol, qcol=qcol, q2col=q2col,
                  x_ready=x_ready, pfree=psD_free_dve, ppe=psD_free_pe):
                if t == 0:
                    V.wait_ge(SEM[x_ready[0]], x_ready[1])
                    if pfree is not None:
                        V.wait_ge(SEM["dve"], pfree)
                    if ppe is not None:
                        V.wait_ge(SEM["pe"], ppe)
                V.reduce_sum(mcol, xt_t, axis=AX)
                V.scalar_tensor_tensor(psGU[:, 0:1536], xt_t[:, 0:1536], 1.0, xt_t[:, 0:1536],
                                       ALU.mult, ALU.mult, accum_out=qcol)
                V.scalar_tensor_tensor(gs[:, 0:512], xt_t[:, 1536:2048], 1.0,
                                       xt_t[:, 1536:2048], ALU.mult, ALU.mult,
                                       accum_out=q2col).then_inc(SEM["dve"], 2)
            em("vector", f)
            m2t = post("dve")

            def f_m2(V, mcol=mcol, tcol=tcol, qcol=qcol, q2col=q2col, s1=stats1[t]):
                V.wait_ge(SEM["dve"], s1)
                V.tensor_tensor(qcol, qcol, q2col, ALU.add)
                V.scalar_tensor_tensor(tcol, mcol, 1.0 / (D * D), mcol, ALU.mult, ALU.mult
                                       ).then_inc(SEM["dve"], 1)
            em("vector", f_m2)
            stats1[t] = m2t
        var_done = {}
        for t in range(NT):
            qcol = sst[:, t * 8 + 1:t * 8 + 2]
            tcol = sst[:, t * 8 + 2:t * 8 + 3]
            vcol = sst[:, t * 8 + 3:t * 8 + 4]
            var_done[t] = post("dve")

            def f(V, t=t, qcol=qcol, tcol=tcol, vcol=vcol):
                V.wait_ge(SEM["dve"], stats1[t])
                V.scalar_tensor_tensor(vcol, qcol, 1.0 / D, tcol, ALU.mult, ALU.subtract
                                       ).then_inc(SEM["dve"], 1)
            em("vector", f)
        ln4 = post("act")
        rstd_done = post("act")

        def f_ln4(S):
            S.wait_ge(SEM["dve"], var_done[NT - 1])
            last = None
            for t in range(NT):
                vcol = sst[:, t * 8 + 3:t * 8 + 4]
                tcol = sst[:, t * 8 + 2:t * 8 + 3]
                last = S.activation(tcol, vcol, AF.Ln, bias=kconst[:, 0:1])
            last.then_inc(SEM["act"], 1)
        em("scalar", f_ln4)

        def f_rstd(S, ln4=ln4):
            S.wait_ge(SEM["act"], ln4)
            last = None
            for t in range(NT):
                tcol = sst[:, t * 8 + 2:t * 8 + 3]
                rcol = sst[:, t * 8 + 4:t * 8 + 5]
                last = S.activation(rcol, tcol, AF.Exp, scale=-0.5)
            last.then_inc(SEM["act"], 1)
        em("scalar", f_rstd)
        negm_done = post("dve")

        def f_negm(V):
            V.wait_ge(SEM["act"], rstd_done)
            last = None
            for t in range(NT):
                mcol = sst[:, t * 8 + 0:t * 8 + 1]
                ncol = sst[:, t * 8 + 5:t * 8 + 6]
                last = V.tensor_scalar_mul(ncol, mcol, -1.0 / D)
            last.then_inc(SEM["dve"], 1)
        em("vector", f_negm)

        lg_mm_last = {}
        prev_tile_lg_dve = None
        ptrk, pevk, pmmk = None, None, None
        SLOT3 = {0: 15, 1: 13, 2: 14}
        SLOT2 = {0: 14, 1: 15}
        for t in range(NT):
            xt_t = xt[:, t * D:(t + 1) * D]
            ncol = sst[:, t * 8 + 5:t * 8 + 6]
            rcol = sst[:, t * 8 + 4:t * 8 + 5]
            vck, trk, evk, mmk = {}, {}, {}, {}
            for c in range(NCI):
                slot = c % 2
                b3 = (c % 3) * 512
                vck[c] = post("dve")

                def f_vc(V, c=c, t=t, slot=slot, trk=trk, ptrk=ptrk,
                         ncol=ncol, rcol=rcol):
                    if c == 0:
                        V.wait_ge(SEM["dve"], negm_done)
                    if c >= 2:
                        V.wait_ge(SEM["pe"], trk[c - 2])
                    elif ptrk is not None:
                        V.wait_ge(SEM["pe"], ptrk[SLOT2[c]])
                    V.tensor_scalar(xf32[:, slot * 128:(slot + 1) * 128],
                                    xt[:, t * D + c * 128:t * D + (c + 1) * 128],
                                    ncol, rcol, ALU.add, ALU.mult).then_inc(SEM["dve"], 1)
                em("vector", f_vc)
                trk[c] = post("pe")

                def f_tc(T, c=c, t=t, slot=slot, b3=b3, vck=vck, evk=evk, pevk=pevk):
                    T.wait_ge(SEM["dve"], vck[c])
                    if c >= 3:
                        T.wait_ge(SEM["dve"], evk[c - 3])
                    elif pevk is not None:
                        T.wait_ge(SEM["dve"], pevk[SLOT3[c]])
                    T.transpose(psD[:, b3:b3 + 128], xf32[:, slot * 128:(slot + 1) * 128],
                                ident_f[:, :]).then_inc(SEM["pe"], 1)
                em("tensor", f_tc)
                evk[c] = post("dve", 2)

                def f_ec(V, c=c, t=t, slot=slot, b3=b3, trk=trk, mmk=mmk, pmmk=pmmk):
                    V.wait_ge(SEM["pe"], trk[c])
                    if c >= 2:
                        V.wait_ge(SEM["pe"], mmk[c - 2])
                    elif pmmk is not None:
                        V.wait_ge(SEM["pe"], pmmk[SLOT2[c]])
                    V.tensor_copy(xT32[:, slot * 128:(slot + 1) * 128], psD[:, b3:b3 + 128])
                    V.tensor_copy(xlnT_v[:, c, t * 128:(t + 1) * 128],
                                  psD[:, b3:b3 + 128]).then_inc(SEM["dve"], 2)
                em("vector", f_ec)
                mmk[c] = post("pe")

                def f_mc(T, c=c, t=t, slot=slot, evk=evk, ptl=prev_tile_lg_dve,
                         need_gwT=(ffn_idx == 0 and r == 0)):
                    T.wait_ge(SEM["dve"], evk[c])
                    if c == 0 and ptl is not None:
                        T.wait_ge(SEM["dve"], ptl)
                    if need_gwT and c == 0 and t == 0:
                        T.wait_ge(SEM["dve"], gwT_ready_dve)
                    T.matmul(lgps, xT32[:, slot * 128:(slot + 1) * 128],
                             gwT[:, c * 16:c * 16 + 16],
                             start=(c == 0), stop=(c == NCI - 1)).then_inc(SEM["pe"], 1)
                em("tensor", f_mc)
            lg_mm_last[t] = mmk[NCI - 1]
            prev_tile_lg_dve = post("dve")

            def f_lg(V, t=t, mm_last=lg_mm_last[t]):
                V.wait_ge(SEM["pe"], mm_last)
                V.tensor_copy(lgb[:, t * 16:(t + 1) * 16], lgps).then_inc(SEM["dve"], 1)
            em("vector", f_lg)
            ptrk, pevk, pmmk = trk, evk, mmk

        lg_all_dve = prev_tile_lg_dve
        ddp = {}

        def em_v(fn, wait_act=None, extra_dve=None):
            tgt = post("dve")

            def f(V, fn=fn, wait_act=wait_act, prev_tgt=tgt - 1, extra_dve=extra_dve):
                V.wait_ge(SEM["dve"], prev_tgt)
                if extra_dve is not None:
                    V.wait_ge(SEM["dve"], extra_dve)
                if wait_act is not None:
                    V.wait_ge(SEM["act"], wait_act)
                fn(V).then_inc(SEM["dve"], 1)
            em("vector", f)
            return tgt

        for t in range(NT):
            base = t * 48
            m1 = t2s[:, base + 0:base + 1]; m2 = t2s[:, base + 1:base + 2]
            ddc = t2s[:, base + 2:base + 3]
            eq1 = t2s[:, base + 8:base + 16]; lgm = t2s[:, base + 16:base + 24]
            eq2 = t2s[:, base + 24:base + 32]
            l8 = lgb[:, t * 16:t * 16 + 8]
            em_v(lambda V, m1=m1, l8=l8: V.reduce_max(m1, l8, axis=AX),
                 extra_dve=lg_all_dve)
            em_v(lambda V, eq1=eq1, l8=l8, m1=m1: V.tensor_scalar(eq1, l8, m1, None, ALU.is_equal))
            em_v(lambda V, lgm=lgm, eq1=eq1, l8=l8:
                 V.scalar_tensor_tensor(lgm, eq1, -1e30, l8, ALU.mult, ALU.add))
            em_v(lambda V, m2=m2, lgm=lgm: V.reduce_max(m2, lgm, axis=AX))
            em_v(lambda V, eq2=eq2, lgm=lgm, m2=m2: V.tensor_scalar(eq2, lgm, m2, None, ALU.is_equal))
            ddp[t] = em_v(lambda V, ddc=ddc, m2=m2, m1=m1: V.tensor_tensor(ddc, m2, m1, ALU.subtract))
        exps_done = post("act")

        def f_exps(S):
            S.wait_ge(SEM["dve"], ddp[NT - 1])
            last = None
            for t in range(NT):
                base = t * 48
                last = S.activation(t2s[:, base + 3:base + 4], t2s[:, base + 2:base + 3], AF.Exp)
            for t in range(NT):
                base = t * 48
                last = S.activation(t2s[:, base + 6:base + 7], lgb[:, t * 16 + 8:t * 16 + 9],
                                    AF.Exp, scale=-1.0)
            last.then_inc(SEM["act"], 1)
        em("scalar", f_exps)
        cmb_done = None
        for t in range(NT):
            base = t * 48
            eec = t2s[:, base + 3:base + 4]; ssc = t2s[:, base + 4:base + 5]
            rrc = t2s[:, base + 5:base + 6]; egc = t2s[:, base + 6:base + 7]
            w1c = t2s[:, base + 7:base + 8]
            eq1 = t2s[:, base + 8:base + 16]; eq2 = t2s[:, base + 24:base + 32]
            t1c = t2s[:, base + 32:base + 40]
            w2c = t2s[:, base + 40:base + 41]
            sgc = t2s[:, base + 41:base + 42]
            hgc = t2s[:, base + 42:base + 43]
            em_v(lambda V, ssc=ssc, eec=eec: V.tensor_scalar_add(ssc, eec, 1.0),
                 wait_act=exps_done)
            em_v(lambda V, rrc=rrc, ssc=ssc: V.reciprocal(rrc, ssc))
            em_v(lambda V, w1c=w1c, rrc=rrc: V.tensor_scalar_mul(w1c, rrc, 0.5))
            em_v(lambda V, w2c=w2c, eec=eec, rrc=rrc: V.tensor_tensor(w2c, eec, rrc, ALU.mult))
            em_v(lambda V, w2c=w2c: V.tensor_scalar_mul(w2c, w2c, 0.5))
            em_v(lambda V, t1c=t1c, eq1=eq1, w1c=w1c: V.tensor_scalar(t1c, eq1, w1c, None, ALU.mult))
            em_v(lambda V, t=t, eq2=eq2, w2c=w2c, t1c=t1c:
                 V.scalar_tensor_tensor(cmb[:, t * 16:t * 16 + 8], eq2, w2c, t1c,
                                        ALU.mult, ALU.add))
            em_v(lambda V, sgc=sgc, egc=egc: V.tensor_scalar_add(sgc, egc, 1.0))
            em_v(lambda V, hgc=hgc, sgc=sgc: V.reciprocal(hgc, sgc))
            cmb_done = em_v(lambda V, t=t, hgc=hgc:
                            V.tensor_scalar_mul(cmb[:, t * 16 + 8:t * 16 + 9], hgc, 0.5))
        return cmb_done, lg_all_dve

    dbg_state = {"done": False}

    def emit_dbg_snapshot(wait_sem, wait_val, out_ap, in_ap):
        if not DEBUG:
            return None
        post("gq", 16)

        def f(G, wait_sem=wait_sem, wait_val=wait_val, out_ap=out_ap, in_ap=in_ap):
            G.wait_ge(SEM[wait_sem], wait_val)
            G.dma_start(out=out_ap, in_=in_ap).then_inc(SEM["gq"], 16)
        em("gpsimd", f)
        return cnt["gq"]

    def emit_g(sd, psGU_free_dve, xln_ready_dve):
        tgt = post("pe")

        def f(T, sd=sd, pf=psGU_free_dve, xr=xln_ready_dve):
            if pf is not None:
                T.wait_ge(SEM["dve"], pf)
            T.wait_ge(SEM["dve"], xr)
            last = None
            for c in range(NCI):
                if sd["t"] == 0 and c % 4 == 0:
                    sname, v = sd["wgu_tgts"][0][c // 4]
                    T.wait_ge(SEM[sname], v)
                elif sd["t"] != 0 and c == 0:
                    pass
                xs = xlnT_v[:, c, sd["t"] * 128:(sd["t"] + 1) * 128]
                ws = wgu_v[:, c, 0:I]
                T.matmul(psGU[:, 0:512], xs, ws[:, 0:512], start=(c == 0), stop=(c == NCI - 1))
                T.matmul(psGU[:, 512:1024], xs, ws[:, 512:1024], start=(c == 0), stop=(c == NCI - 1))
                last = T.matmul(psGU[:, 1024:1408], xs, ws[:, 1024:1408],
                                start=(c == 0), stop=(c == NCI - 1))
            last.then_inc(SEM["pe"], 1)
        em("tensor", f)
        sd["g"] = tgt

    def emit_silu(sd, gs_free_pe):
        tgt = post("act")

        def f(S, sd=sd, gf=gs_free_pe):
            S.wait_ge(SEM["pe"], sd["g"])
            if gf is not None:
                S.wait_ge(SEM["pe"], gf)
            S.activation(gs[:, sd["slot"] * I:(sd["slot"] + 1) * I], psGU[:, 0:1408],
                         AF.Silu).then_inc(SEM["act"], 1)
        em("scalar", f)
        sd["silu"] = tgt

    def emit_u(sd):
        tgt = post("pe")

        def f(T, sd=sd):
            T.wait_ge(SEM["act"], sd["silu"])
            last = None
            for c in range(NCI):
                if sd["t"] == 0 and c % 4 == 0:
                    sname, v = sd["wgu_tgts"][1][c // 4]
                    T.wait_ge(SEM[sname], v)
                xs = xlnT_v[:, c, sd["t"] * 128:(sd["t"] + 1) * 128]
                ws = wgu_v[:, c, I:2 * I]
                T.matmul(psGU[:, 0:512], xs, ws[:, 0:512], start=(c == 0), stop=(c == NCI - 1))
                T.matmul(psGU[:, 512:1024], xs, ws[:, 512:1024], start=(c == 0), stop=(c == NCI - 1))
                last = T.matmul(psGU[:, 1024:1408], xs, ws[:, 1024:1408],
                                start=(c == 0), stop=(c == NCI - 1))
            last.then_inc(SEM["pe"], 1)
        em("tensor", f)
        sd["u"] = tgt

    def emit_h(sd, cmb_ready_dve):
        tgt = post("dve")
        e, t = sd["e"], sd["t"]
        col = t * 16 + (e if e < 8 else 8)

        def f(V, sd=sd, col=col, cr=cmb_ready_dve):
            V.wait_ge(SEM["pe"], sd["u"])
            V.wait_ge(SEM["dve"], cr)
            if sd.get("_h_extra_wait"):
                V.wait_ge(SEM["gq"], sd["_h_extra_wait"])
            V.scalar_tensor_tensor(gs[:, sd["slot"] * I:(sd["slot"] + 1) * I],
                                   psGU[:, 0:1408], cmb[:, col:col + 1],
                                   gs[:, sd["slot"] * I:(sd["slot"] + 1) * I],
                                   ALU.mult, ALU.mult).then_inc(SEM["dve"], 1)
        em("vector", f)
        sd["h"] = tgt

    def emit_tr1(sd, psTr_free_dve):
        tgt = post("pe")

        def f(T, sd=sd, pf=psTr_free_dve):
            T.wait_ge(SEM["dve"], sd["h"])
            if sd.get("_dbg_h"):
                T.wait_ge(SEM["gq"], sd["_dbg_h"])
            if pf is not None:
                T.wait_ge(SEM["dve"], pf)
            last = None
            for c in range(8):
                last = T.transpose(psTr[:, c * 128:(c + 1) * 128],
                                   gs[:, sd["slot"] * I + c * 128:sd["slot"] * I + (c + 1) * 128],
                                   ident_bf[:, :])
            last.then_inc(SEM["pe"], 1)
        em("tensor", f)
        sd["tr1"] = tgt

    def emit_ev1(sd, hT_free_pe):
        tgt = post("dve")

        def f(V, sd=sd, hf=hT_free_pe):
            V.wait_ge(SEM["pe"], sd["tr1"])
            if hf is not None:
                V.wait_ge(SEM["pe"], hf)
            V.tensor_copy(hT[:, sd["slot"] * NIC * 128:sd["slot"] * NIC * 128 + 1024],
                          psTr[:, 0:1024]).then_inc(SEM["dve"], 1)
        em("vector", f)
        sd["ev1"] = tgt

    def emit_dnA(sd, psD_free_dve, psD_free_pe):
        tgt = post("pe")

        def f(T, sd=sd, pf=psD_free_dve, pp=psD_free_pe):
            T.wait_ge(SEM["dve"], sd["ev1"])
            if pf is not None:
                T.wait_ge(SEM["dve"], pf)
            for sname, v in sd["wd_tgts"][0:2]:
                T.wait_ge(SEM[sname], v)
            last = None
            base = sd["slot"] * NIC * 128
            for ic in range(6):
                hs = hT[:, base + ic * 128:base + (ic + 1) * 128]
                for qq in range(4):
                    last = T.matmul(psD[:, qq * 512:(qq + 1) * 512], hs,
                                    wdb_v[:, ic, qq * 512:(qq + 1) * 512],
                                    start=(ic == 0), stop=False)
            last.then_inc(SEM["pe"], 1)
        em("tensor", f)
        sd["dnA"] = tgt

    def emit_tr2(sd):
        tgt = post("pe")

        def f(T, sd=sd):
            T.wait_ge(SEM["dve"], sd["ev1"])
            last = None
            for c in range(8, NIC):
                last = T.transpose(psTr[:, (c - 8) * 128:(c - 7) * 128],
                                   gs[:, sd["slot"] * I + c * 128:sd["slot"] * I + (c + 1) * 128],
                                   ident_bf[:, :])
            last.then_inc(SEM["pe"], 1)
        em("tensor", f)
        sd["tr2"] = tgt

    def emit_ev2(sd):
        tgt = post("dve")

        def f(V, sd=sd):
            V.wait_ge(SEM["pe"], sd["tr2"])
            V.tensor_copy(hT[:, sd["slot"] * NIC * 128 + 1024:sd["slot"] * NIC * 128 + 1408],
                          psTr[:, 0:384]).then_inc(SEM["dve"], 1)
        em("vector", f)
        sd["ev2"] = tgt

    def emit_dnB(sd):
        tgt = post("pe")

        def f(T, sd=sd):
            T.wait_ge(SEM["dve"], sd["ev2"])
            sname, v = sd["wd_tgts"][2]
            T.wait_ge(SEM[sname], v)
            last = None
            base = sd["slot"] * NIC * 128
            for ic in range(6, NIC):
                hs = hT[:, base + ic * 128:base + (ic + 1) * 128]
                for qq in range(4):
                    last = T.matmul(psD[:, qq * 512:(qq + 1) * 512], hs,
                                    wdb_v[:, ic, qq * 512:(qq + 1) * 512],
                                    start=False, stop=(ic == NIC - 1))
            last.then_inc(SEM["pe"], 1)
        em("tensor", f)
        sd["dnB"] = tgt

    def emit_xevac(sd):
        tgt = post("dve")

        def f(V, sd=sd):
            V.wait_ge(SEM["pe"], sd["dnB"])
            V.tensor_tensor(xt[:, sd["t"] * D:(sd["t"] + 1) * D],
                            xt[:, sd["t"] * D:(sd["t"] + 1) * D],
                            psD[:, :], ALU.add).then_inc(SEM["dve"], 1)
        em("vector", f)
        sd["xev"] = tgt

    def emit_lagged(prev_sd, carry):
        emit_tr1(prev_sd, prev_sd["_psTr_free"])
        emit_ev1(prev_sd, prev_sd["_hT_free"])
        emit_dnA(prev_sd, prev_sd["_psD_free"], None)
        emit_tr2(prev_sd)
        emit_ev2(prev_sd)


    def emit_attention(carry):
        SQ = "sq"; GQ = "gq"
        if ATTN_STOP == "none":
            carry["rs_done_cc"] = cnt["cc"]
            carry["rs_out"] = x1s
            carry["rs_ready"] = ("cc", 0)
            carry["prev_sd"] = None
            carry["prev2_sd"] = None
            return carry
        wgu_flat = wgu[:, :]
        wdb_flat = wdb[:, :]
        xlnT_flat = xlnT[:, :]
        xl2T_v = xlnT_flat.rearrange("p (c n) -> p c n", c=NCI)
        for tg in range(4):
            post("s_x", 16)
            ls = carry.get("last_store_sq")
            lx = carry.get("last_xev_dve")

            def f_x(S, tg=tg, ls=ls):
                if ls is not None:
                    for sname, v in ls:
                        S.wait_ge(SEM[sname], v)
                if tg > 0:
                    S.wait_ge(SEM["dve"], carry["_ln2_done"][tg - 1])
                S.dma_start(out=xt[:, :].rearrange("p (t n) -> p t n", t=NT),
                            in_=x1s[tg * TPR:(tg + 1) * TPR, :]
                            .rearrange("(t p) n -> p t n", p=128)).then_inc(SEM["s_x"], 16)
            em("sync", f_x)
            x_ready = ("s_x", cnt["s_x"])
            stats1 = {}
            for t in range(NT):
                xt_t = xt[:, t * D:(t + 1) * D]
                mcol = sst[:, t * 8 + 0:t * 8 + 1]
                qcol = sst[:, t * 8 + 1:t * 8 + 2]
                tcol = sst[:, t * 8 + 2:t * 8 + 3]
                vcol = sst[:, t * 8 + 3:t * 8 + 4]
                q2col = sst[:, t * 8 + 6:t * 8 + 7]
                stats1[t] = post("dve", 2)

                def f(V, t=t, xt_t=xt_t, mcol=mcol, qcol=qcol, q2col=q2col,
                      x_ready=x_ready, lx=lx, tg=tg):
                    if t == 0:
                        V.wait_ge(SEM[x_ready[0]], x_ready[1])
                        if tg == 0 and lx is not None:
                            V.wait_ge(SEM["dve"], lx)
                        if tg > 0:
                            V.wait_ge(SEM["dve"], carry["_ln2_done"][tg - 1])
                            V.wait_ge(SEM["s_agx"], carry["_agx_st"][tg - 1])
                    V.reduce_sum(mcol, xt_t, axis=AX)
                    V.scalar_tensor_tensor(psGU[:, 0:1536], xt_t[:, 0:1536], 1.0, xt_t[:, 0:1536],
                                           ALU.mult, ALU.mult, accum_out=qcol)
                    V.scalar_tensor_tensor(gs[:, 0:512], xt_t[:, 1536:2048], 1.0,
                                           xt_t[:, 1536:2048], ALU.mult, ALU.mult,
                                           accum_out=q2col).then_inc(SEM["dve"], 2)
                em("vector", f)
                m2t = post("dve")

                def f_m2(V, mcol=mcol, tcol=tcol, qcol=qcol, q2col=q2col, s1=stats1[t]):
                    V.wait_ge(SEM["dve"], s1)
                    V.tensor_tensor(qcol, qcol, q2col, ALU.add)
                    V.scalar_tensor_tensor(tcol, mcol, 1.0 / (D * D), mcol, ALU.mult, ALU.mult
                                           ).then_inc(SEM["dve"], 1)
                em("vector", f_m2)
                stats1[t] = m2t
            var_done = {}
            for t in range(NT):
                qcol = sst[:, t * 8 + 1:t * 8 + 2]
                tcol = sst[:, t * 8 + 2:t * 8 + 3]
                vcol = sst[:, t * 8 + 3:t * 8 + 4]
                var_done[t] = post("dve")

                def f(V, t=t, qcol=qcol, tcol=tcol, vcol=vcol, stats1=stats1):
                    V.wait_ge(SEM["dve"], stats1[t])
                    V.scalar_tensor_tensor(vcol, qcol, 1.0 / D, tcol, ALU.mult, ALU.subtract
                                           ).then_inc(SEM["dve"], 1)
                em("vector", f)
            ln4b = post("act")
            rstd_done = post("act")

            def f_ln4b(S, var_done=var_done):
                S.wait_ge(SEM["dve"], var_done[NT - 1])
                last = None
                for t in range(NT):
                    last = S.activation(sst[:, t * 8 + 2:t * 8 + 3], sst[:, t * 8 + 3:t * 8 + 4],
                                        AF.Ln, bias=kconst[:, 0:1])
                last.then_inc(SEM["act"], 1)
            em("scalar", f_ln4b)

            def f_rstd(S, ln4b=ln4b):
                S.wait_ge(SEM["act"], ln4b)
                last = None
                for t in range(NT):
                    last = S.activation(sst[:, t * 8 + 4:t * 8 + 5], sst[:, t * 8 + 2:t * 8 + 3],
                                        AF.Exp, scale=-0.5)
                last.then_inc(SEM["act"], 1)
            em("scalar", f_rstd)
            negm_done = post("dve")

            def f_negm(V, rstd_done=rstd_done):
                V.wait_ge(SEM["act"], rstd_done)
                last = None
                for t in range(NT):
                    last = V.tensor_scalar_mul(sst[:, t * 8 + 5:t * 8 + 6],
                                               sst[:, t * 8 + 0:t * 8 + 1], -1.0 / D)
                last.then_inc(SEM["dve"], 1)
            em("vector", f_negm)
            if ATTN_STOP == "b1stats":
                carry.setdefault("_ln2_done", {})[tg] = cnt["dve"]
                carry.setdefault("_agx_st", {})[tg] = cnt["s_agx"]
                continue
            evk_all = None
            prev_tile_ev = None
            for t in range(NT):
                ncol = sst[:, t * 8 + 5:t * 8 + 6]
                rcol = sst[:, t * 8 + 4:t * 8 + 5]
                vck, trk, evk = {}, {}, {}
                for c in range(NCI):
                    slot = c % 2
                    b3 = (c % 3) * 512
                    vck[c] = post("dve")

                    def f_vc(V, c=c, t=t, slot=slot, ncol=ncol, rcol=rcol, trk=trk,
                             negm_done=negm_done):
                        if c == 0:
                            V.wait_ge(SEM["dve"], negm_done)
                        V.tensor_scalar(xf32[:, slot * 128:(slot + 1) * 128],
                                        xt[:, t * D + c * 128:t * D + (c + 1) * 128],
                                        ncol, rcol, ALU.add, ALU.mult).then_inc(SEM["dve"], 1)
                    em("vector", f_vc)
                    trk[c] = post("pe")

                    def f_tc(T, c=c, slot=slot, b3=b3, vck=vck, evk=evk, t=t, tg=tg):
                        T.wait_ge(SEM["dve"], vck[c])
                        if c >= 3:
                            T.wait_ge(SEM["dve"], evk[c - 3])
                        elif c < 3 and t == 0:
                            pass
                        T.transpose(psD[:, b3:b3 + 128], xf32[:, slot * 128:(slot + 1) * 128],
                                    ident_f[:, :]).then_inc(SEM["pe"], 1)
                    em("tensor", f_tc)
                    evk[c] = post("dve")

                    def f_ec(V, c=c, t=t, slot=slot, b3=b3, trk=trk):
                        V.wait_ge(SEM["pe"], trk[c])
                        V.tensor_copy(xl2T_v[:, c, t * 128:(t + 1) * 128],
                                      psD[:, b3:b3 + 128]).then_inc(SEM["dve"], 1)
                    em("vector", f_ec)
                evk_all = evk[NCI - 1]
                prev_tile_ev = evk[NCI - 1]
            carry.setdefault("_ln2_done", {})[tg] = evk_all
            if ATTN_STOP == "b1nostore":
                carry.setdefault("_agx_st", {})[tg] = cnt["s_agx"]
                continue
            post("s_agx", 16)

            def f_st(S, tg=tg, evk_all=evk_all):
                S.wait_ge(SEM["dve"], evk_all)
                S.dma_start(out=agx_in.rearrange("(c p) n -> p c n", p=128)[:, :, tg * TPR:(tg + 1) * TPR],
                            in_=xl2T_v).then_inc(SEM["s_agx"], 16)
            em("sync", f_st)
            carry.setdefault("_agx_st", {})[tg] = cnt["s_agx"]

        if ATTN_STOP in ("b1", "b1stats", "b1nostore"):
            carry["rs_done_cc"] = cnt["cc"]
            carry["rs_out"] = x1s
            carry["last_store_sq"] = [("s_agx", cnt["s_agx"])]
            carry["last_xev_dve"] = cnt["dve"]
            carry["prev_sd"] = None
            carry["prev2_sd"] = None
            return carry
        post("cc", 1)
        ag_done = cnt["cc"]
        agx_st_last = carry["_agx_st"][3]

        def f_ag(G, agx_st_last=agx_st_last):
            G.wait_ge(SEM["s_agx"], agx_st_last)
            G.collective_compute("AllGather", ALU.bypass,
                                 replica_groups=[[0, 1], [2, 3], [4, 5], [6, 7]],
                                 ins=[agx_in.opt()], outs=[agx_out.opt()]).then_inc(SEM["cc"], 1)
        em("gpsimd", f_ag)

        if ATTN_STOP == "b1ag":
            carry["rs_done_cc"] = ag_done
            carry["rs_out"] = x1s
            carry["rs_ready"] = ("cc", 0)
            carry["last_store_sq"] = [("s_agx", cnt["s_agx"])]
            carry["last_xev_dve"] = cnt["dve"]
            carry["prev_sd"] = None
            carry["prev2_sd"] = None
            return carry
        wnat = wgu_flat[:, 0:16384].rearrange("p (c n) -> p c n", c=8)
        wonat = wgu_flat[:, 16384:32768].rearrange("p (c n) -> p c n", c=16)
        wstage = wdb_flat[:, 0:1024]
        wl = {}
        for mi, (wsrc, rows) in enumerate([(wq, 1024), (wk, 1024), (wv, 1024), (wo, D)]):
            post("g_attw", 16)

            def f_wl(G, mi=mi, wsrc=wsrc, rows=rows):
                if mi > 0:
                    G.wait_ge(SEM["pe"], wl[mi - 1][1])
                if mi == 0 and carry.get("_wT_wait_pe") is not None:
                    G.wait_ge(SEM["pe"], carry["_wT_wait_pe"])
                if rows == 1024:
                    G.dma_start(out=wnat, in_=wsrc.rearrange("(c p) n -> p c n", p=128)
                                ).then_inc(SEM["g_attw"], 16)
                else:
                    G.dma_start(out=wonat, in_=wsrc.rearrange("(c p) n -> p c n", p=128)
                                ).then_inc(SEM["g_attw"], 16)
            em("gpsimd", f_wl)
            ld = cnt["g_attw"]
            if mi < 3:
                dst = [wqT_s, wkT_s, wvT_s][mi]
                last_pe = None
                for c in range(NCI):
                    tpe = post("pe")

                    def f_tr(T, mi=mi, c=c, ld=ld, tpe=tpe,
                             pv=wl.get((mi - 1, 15)) if mi >= 1 else None):
                        if c == 0:
                            T.wait_ge(SEM["g_attw"], ld)
                            if pv is not None:
                                T.wait_ge(SEM["dve"], pv)
                        if c >= 1:
                            T.wait_ge(SEM["dve"], wl[(mi, c - 1)])
                        last = None
                        for hc in range(8):
                            last = T.transpose(psTr[:, hc * 128:(hc + 1) * 128],
                                               wnat[:, hc, c * 128:(c + 1) * 128], ident_bf[:, :])
                        last.then_inc(SEM["pe"], 1)
                    em("tensor", f_tr)
                    wl[(mi, c)] = post("dve")

                    def f_ev(V, tpe=tpe, pst=wl.get(("st", mi, c - 1)) if c > 0 else
                             (wl.get(("st", mi - 1, 15)) if mi > 0 else None)):
                        V.wait_ge(SEM["pe"], tpe)
                        if pst is not None:
                            V.wait_ge(SEM["s_wT"], pst)
                        V.tensor_copy(wstage, psTr[:, 0:1024]).then_inc(SEM["dve"], 1)
                    em("vector", f_ev)
                    post("s_wT", 16)

                    def f_sd(S, dst=dst, c=c, ev=wl[(mi, c)]):
                        S.wait_ge(SEM["dve"], ev)
                        S.dma_start(out=dst[c * 128:(c + 1) * 128, :], in_=wstage
                                    ).then_inc(SEM["s_wT"], 16)
                    em("sync", f_sd)
                    wl[("st", mi, c)] = cnt["s_wT"]
                    last_pe = tpe
                wl[mi] = (ld, last_pe)
            else:
                last_pe = None
                wost = wdb_flat[:, 1024:3072]
                for jc in range(8):
                    tpe = post("pe")

                    def f_tr(T, jc=jc, ld=ld, tpe=tpe,
                             pv=wl.get((2, 15))):
                        if jc == 0:
                            T.wait_ge(SEM["g_attw"], ld)
                            if pv is not None:
                                T.wait_ge(SEM["dve"], pv)
                        if jc >= 1:
                            T.wait_ge(SEM["dve"], wl[(3, jc - 1, 1)])
                        last = None
                        for oc in range(8):
                            last = T.transpose(psTr[:, oc * 128:(oc + 1) * 128],
                                               wonat[:, oc, jc * 128:(jc + 1) * 128], ident_bf[:, :])
                        last.then_inc(SEM["pe"], 1)
                    em("tensor", f_tr)
                    e1 = post("dve")
                    wl[(3, jc, 0)] = e1

                    def f_ev(V, tpe=tpe, jc=jc, pst=wl.get(("sto", jc - 1))):
                        V.wait_ge(SEM["pe"], tpe)
                        if pst is not None:
                            V.wait_ge(SEM["s_wT"], pst)
                        V.tensor_copy(wost[:, 0:1024], psTr[:, 0:1024]).then_inc(SEM["dve"], 1)
                    em("vector", f_ev)
                    tpe2 = post("pe")

                    def f_tr2(T, jc=jc, tpe2=tpe2, e1=e1):
                        T.wait_ge(SEM["dve"], e1)
                        last = None
                        for oc in range(8, 16):
                            last = T.transpose(psTr[:, (oc - 8) * 128:(oc - 7) * 128],
                                               wonat[:, oc, jc * 128:(jc + 1) * 128], ident_bf[:, :])
                        last.then_inc(SEM["pe"], 1)
                    em("tensor", f_tr2)
                    e2 = post("dve")
                    wl[(3, jc, 1)] = e2

                    def f_ev2(V, tpe2=tpe2):
                        V.wait_ge(SEM["pe"], tpe2)
                        V.tensor_copy(wost[:, 1024:2048], psTr[:, 0:1024]).then_inc(SEM["dve"], 1)
                    em("vector", f_ev2)
                    post("s_wT", 16)

                    def f_sd(S, jc=jc, e2=e2):
                        S.wait_ge(SEM["dve"], e2)
                        S.dma_start(out=woT_s[jc * 128:(jc + 1) * 128, :], in_=wost
                                    ).then_inc(SEM["s_wT"], 16)
                    em("sync", f_sd)
                    wl[("sto", jc)] = cnt["s_wT"]
                    last_pe = tpe2
                wl[3] = (ld, last_pe)
        wT_done_sq = cnt["s_wT"]

        slab = wgu_flat[:, 0:32768].rearrange("p (c n) -> p c n", c=NCI)
        wsl = wdb_flat[:, 0:4096].rearrange("p (s n) -> p s n", s=2)
        qstg = wdb_flat[:, 4096:8192].rearrange("p (s n) -> p s n", s=2)
        b3_store = {}
        prev_ev = None
        prev_mm = None
        for tb in range(2):
            post("s_slab", 16)

            def f_sl(S, tb=tb, prev_mm=prev_mm):
                S.wait_ge(SEM["cc"], ag_done)
                S.wait_ge(SEM["s_wT"], wT_done_sq)
                if prev_mm is not None:
                    S.wait_ge(SEM["pe"], prev_mm)
                S.dma_start(out=slab,
                            in_=agx_out[tb * D:(tb + 1) * D, :].rearrange("(c p) n -> p c n", p=128)
                            ).then_inc(SEM["s_slab"], 16)
            em("sync", f_sl)
            slab_ld = cnt["s_slab"]
            for h in range(8):
                for xi, (wTs, dstq) in enumerate([(wqT_s, qT_sd), (wkT_s, kT_sd), (wvT_s, vT_sd)]):
                    si = (h * 3 + xi) % 2
                    ldsem = "s_ld3a" if si == 0 else "s_ld3b"
                    post(ldsem, 16)

                    def f_wv(S, wTs=wTs, h=h, si=si, ldsem=ldsem,
                             g=b3_store.get(("mmslot", si))):
                        if g is not None:
                            S.wait_ge(SEM["pe"], g)
                        S.dma_start(out=wsl[:, si, :].rearrange("p (c n) -> p c n", c=NCI),
                                    in_=wTs[:, h * 128:(h + 1) * 128]
                                    .rearrange("(c p) n -> p c n", p=128)
                                    ).then_inc(SEM[ldsem], 16)
                    em("sync", f_wv)
                    wld = (ldsem, cnt[ldsem])
                    mm = post("pe")

                    def f_mm(T, si=si, wld=wld, slab_ld=slab_ld, mm=mm, prev_ev=prev_ev):
                        T.wait_ge(SEM[wld[0]], wld[1])
                        T.wait_ge(SEM["s_slab"], slab_ld)
                        if prev_ev is not None:
                            T.wait_ge(SEM["dve"], prev_ev)
                        last = None
                        for c in range(NCI):
                            ws = wsl[:, si, c * 128:(c + 1) * 128]
                            for qq in range(4):
                                last = T.matmul(psD[:, qq * 512:(qq + 1) * 512], ws,
                                                slab[:, c, qq * 512:(qq + 1) * 512],
                                                start=(c == 0), stop=(c == NCI - 1))
                        last.then_inc(SEM["pe"], 1)
                    em("tensor", f_mm)
                    b3_store[("mmslot", si)] = mm
                    ev = post("dve")

                    def f_ev(V, mm=mm, si=si, pst=b3_store.get(("stslot", si))):
                        V.wait_ge(SEM["pe"], mm)
                        if pst is not None:
                            V.wait_ge(SEM["s_qkva" if si == 0 else "s_qkvb"], pst)
                        V.tensor_copy(qstg[:, si, :], psD[:, :]).then_inc(SEM["dve"], 1)
                    em("vector", f_ev)
                    prev_ev = ev
                    prev_mm = mm
                    qsem = "s_qkva" if si == 0 else "s_qkvb"
                    post(qsem, 16)

                    def f_qs(S, dstq=dstq, h=h, tb=tb, ev=ev, si=si, qsem=qsem):
                        S.wait_ge(SEM["dve"], ev)
                        S.dma_start(out=dstq[h * 128:(h + 1) * 128, tb * D:(tb + 1) * D],
                                    in_=qstg[:, si, :]).then_inc(SEM[qsem], 16)
                    em("sync", f_qs)
                    b3_store[(tb, h, xi)] = cnt[qsem]
                    b3_store[("stslot", si)] = cnt[qsem]
        qkv_done_sq = (cnt["s_qkva"], cnt["s_qkvb"])
        if ATTN_STOP == "b3":
            post("g_acc", 16)

            def f_dummy(G, q=qkv_done_sq):
                G.wait_ge(SEM["s_qkva"], q[0])
                G.wait_ge(SEM["s_qkvb"], q[1])
                G.dma_start(out=rs_out, in_=x1s).then_inc(SEM["g_acc"], 16)
            em("gpsimd", f_dummy)
            carry["rs_done_cc"] = cnt["cc"]
            carry["rs_ready"] = ("g_acc", cnt["g_acc"])
            carry["rs_out"] = rs_out
            carry["last_store_sq"] = [("s_agx", cnt["s_agx"])]
            carry["last_xev_dve"] = cnt["dve"]
            carry["prev_sd"] = None
            carry["prev2_sd"] = None
            return carry

        last_zT_ev_holder = {"v": None, "zst": None}
        qTb = wdb_flat[:, 0:4096]
        kTb = wdb_flat[:, 4096:8192]
        vTb = wdb_flat[:, 8192:12288]
        vb = wgu_flat[:, 0:4096].rearrange("p (c n) -> p c n", c=32)
        PTb = wgu_flat[:, 4096:6144]
        zstg = wgu_flat[:, 38912:39936]
        P_pool = [[wgu_flat[:, 6144 + (par * 4 + i) * 4096: 6144 + (par * 4 + i + 1) * 4096]
                   for i in range(4)] for par in range(2)]
        psS = [psGU[:, 0:1024], psD[:, 0:1024], psD[:, 1024:2048]]
        psO = psGU[:, 1024:1536]
        C = lambda i: sst[:, i:i + 1]

        def emit_av(h, qg, p_done, P_bufs):
            nk = 4 * qg + 4
            npair = nk // 2
            av_mm = {}
            tr_t = {}
            ev_t = {}
            for pj in range(npair):
                sl = pj % 2
                tr_t[pj] = post("pe")

                def f_tr(T, pj=pj, p_done=p_done, P_bufs=P_bufs, ev_t=ev_t):
                    if pj == 0:
                        for pf in p_done:
                            T.wait_ge(SEM["dve"], pf)
                    if pj >= 1:
                        T.wait_ge(SEM["act"], ev_t[pj - 1])
                    last = None
                    for half in range(2):
                        kc = pj * 2 + half
                        for i in range(4):
                            last = T.transpose(
                                psTr[:, half * 512 + i * 128:half * 512 + (i + 1) * 128],
                                P_bufs[i][:, kc * 128:(kc + 1) * 128], ident_bf[:, :])
                    last.then_inc(SEM["pe"], 1)
                em("tensor", f_tr)
                ev_t[pj] = post("act")

                def f_ev(S, pj=pj, sl=sl, tr=tr_t[pj], av_mm=av_mm):
                    S.wait_ge(SEM["pe"], tr)
                    pv = av_mm.get(pj - 2)
                    if pv is not None:
                        S.wait_ge(SEM["pe"], pv)
                    S.copy(PTb[:, sl * 1024:(sl + 1) * 1024],
                           psTr[:, 0:1024]).then_inc(SEM["act"], 1)
                em("scalar", f_ev)
                av_mm[pj] = post("pe")

                def f_av(T, pj=pj, sl=sl, npair=npair, ev=ev_t[pj],
                         lzv=last_zT_ev_holder["v"]):
                    T.wait_ge(SEM["act"], ev)
                    if pj == 0 and lzv is not None:
                        T.wait_ge(SEM["dve"], lzv)
                    last = None
                    for half in range(2):
                        kc = pj * 2 + half
                        last = T.matmul(psO, vb[:, kc, :],
                                        PTb[:, sl * 1024 + half * 512:sl * 1024 + (half + 1) * 512],
                                        start=(kc == 0), stop=(kc == nk - 1))
                    last.then_inc(SEM["pe"], 1)
                em("tensor", f_av)
            zev = post("dve")
            zsl = qg % 2

            def f_z(V, qg=qg, zsl=zsl, mm=av_mm[npair - 1], pzs=last_zT_ev_holder["zst"]):
                V.wait_ge(SEM["pe"], mm)
                if pzs is not None:
                    V.wait_ge(SEM["s_zt"], pzs)
                V.tensor_copy(zstg[:, zsl * 512:(zsl + 1) * 512], psO).then_inc(SEM["dve"], 1)
            em("vector", f_z)
            post("s_zt", 16)

            def f_zs(S, h=h, qg=qg, zsl=zsl, zev=zev):
                S.wait_ge(SEM["dve"], zev)
                S.dma_start(out=zT_sd[h * 128:(h + 1) * 128, qg * 512:(qg + 1) * 512],
                            in_=zstg[:, zsl * 512:(zsl + 1) * 512]).then_inc(SEM["s_zt"], 16)
            em("sync", f_zs)
            last_zT_ev_holder["v"] = zev
            last_zT_ev_holder["zst"] = cnt["s_zt"]
            return tr_t[npair - 1]

        last_head_S_pe = None
        prev_head_vdve = None
        gk_hist = {}
        gk = 0
        P_free = {0: None, 1: None}
        final_zst = None
        for h in range(8):
            post("s_hld", 16 * 3)
            prev_S = last_head_S_pe

            def f_ld(S, h=h, prev_S=prev_S, pvd=prev_head_vdve):
                if h == 0:
                    S.wait_ge(SEM["s_qkva"], qkv_done_sq[0])
                    S.wait_ge(SEM["s_qkvb"], qkv_done_sq[1])
                else:
                    S.wait_ge(SEM["pe"], prev_S)
                    S.wait_ge(SEM["dve"], pvd)
                S.dma_start(out=qTb, in_=qT_sd[h * 128:(h + 1) * 128, :]).then_inc(SEM["s_hld"], 16)
                S.dma_start(out=kTb, in_=kT_sd[h * 128:(h + 1) * 128, :]).then_inc(SEM["s_hld"], 16)
                S.dma_start(out=vTb, in_=vT_sd[h * 128:(h + 1) * 128, :]).then_inc(SEM["s_hld"], 16)
            em("sync", f_ld)
            ld_t = cnt["s_hld"]
            vtr = {}
            for b4 in range(4):
                tpe = post("pe")

                def f_vt(T, b4=b4, ld_t=ld_t, h=h, vtr=vtr, lz=last_zT_ev_holder["v"]):
                    if b4 == 0:
                        T.wait_ge(SEM["s_hld"], ld_t)
                        if lz is not None:
                            T.wait_ge(SEM["dve"], lz)
                    if b4 >= 1:
                        T.wait_ge(SEM["dve"], vtr[b4 - 1])
                    last = None
                    for j in range(8):
                        kc = b4 * 8 + j
                        last = T.transpose(psTr[:, j * 128:(j + 1) * 128],
                                           vTb[:, kc * 128:(kc + 1) * 128], ident_bf[:, :])
                    last.then_inc(SEM["pe"], 1)
                em("tensor", f_vt)
                vtr[b4] = post("dve")

                def f_ve(V, b4=b4, tpe=tpe):
                    V.wait_ge(SEM["pe"], tpe)
                    V.tensor_copy(wgu_flat[:, b4 * 1024:(b4 + 1) * 1024],
                                  psTr[:, 0:1024]).then_inc(SEM["dve"], 1)
                em("vector", f_ve)
            v_ready = vtr[3]
            prev_head_vdve = v_ready

            pend = None
            for qg in range(8):
                par = qg % 2
                P_bufs = P_pool[par]
                p_done = []
                for qt4 in range(4):
                    qt = qg * 4 + qt4
                    klen = 128 * (qt + 1)
                    kgs = (klen + 1023) // 1024
                    Pb = P_bufs[qt4]
                    smms = {}
                    exp_t = {}
                    for kg in range(kgs):
                        w = min(1024, klen - kg * 1024)
                        sset = gk % 3
                        prev_exp = gk_hist.get(gk - 3)
                        smms[kg] = post("pe")

                        def f_s(T, qt=qt, kg=kg, w=w, sset=sset, v_ready=v_ready,
                                prev_exp=prev_exp):
                            if qt == 0 and kg == 0:
                                T.wait_ge(SEM["dve"], v_ready)
                            if prev_exp is not None:
                                T.wait_ge(SEM["act"], prev_exp)
                            last = None
                            for off in range(0, w, 512):
                                ww = min(512, w - off)
                                last = T.matmul(psS[sset][:, off:off + ww],
                                                qTb[:, qt * 128:(qt + 1) * 128],
                                                kTb[:, kg * 1024 + off:kg * 1024 + off + ww],
                                                start=True, stop=True)
                            last.then_inc(SEM["pe"], 1)
                        em("tensor", f_s)
                        mx_d = post("dve")

                        def f_mx(V, kg=kg, w=w, sset=sset, smm=smms[kg]):
                            V.wait_ge(SEM["pe"], smm)
                            V.reduce_max(C(kg), psS[sset][:, 0:w], axis=AX
                                         ).then_inc(SEM["dve"], 1)
                        em("vector", f_mx)
                        mg_d = post("dve")

                        def f_ng(V, kg=kg, mx_d=mx_d):
                            V.wait_ge(SEM["dve"], mx_d)
                            V.tensor_scalar_mul(C(4 + kg), C(kg), -0.08838834764831845
                                                ).then_inc(SEM["dve"], 1)
                        em("vector", f_ng)
                        exp_t[kg] = post("act")
                        pfree = P_free[par]

                        def f_ex(S, kg=kg, w=w, sset=sset, mg_d=mg_d, Pb=Pb, pfree=pfree):
                            S.wait_ge(SEM["dve"], mg_d)
                            if kg == 0 and pfree is not None:
                                S.wait_ge(SEM["pe"], pfree)
                            S.activation(Pb[:, kg * 1024:kg * 1024 + w], psS[sset][:, 0:w],
                                         AF.Exp, bias=C(4 + kg), scale=0.08838834764831845,
                                         accum_out=C(8 + kg)).then_inc(SEM["act"], 1)
                        em("scalar", f_ex)
                        gk_hist[gk] = exp_t[kg]
                        gk += 1
                    dk = kgs - 1
                    dw = klen - dk * 1024
                    mk_d = post("dve")

                    def f_msk(V, qt=qt, Pb=Pb, et=exp_t[dk]):
                        V.wait_ge(SEM["act"], et)
                        V.tensor_tensor(Pb[:, qt * 128:qt * 128 + 128],
                                        Pb[:, qt * 128:qt * 128 + 128], tri01[:, :], ALU.mult
                                        ).then_inc(SEM["dve"], 1)
                    em("vector", f_msk)
                    post("dve")

                    def f_dsum(V, dk=dk, dw=dw, Pb=Pb, mk_d=mk_d):
                        V.wait_ge(SEM["dve"], mk_d)
                        V.reduce_sum(C(8 + dk), Pb[:, dk * 1024:dk * 1024 + dw], axis=AX
                                     ).then_inc(SEM["dve"], 1)
                    em("vector", f_dsum)

                    gmax = 128 * (4 * qg + 4)
                    if klen < gmax:
                        post("dve")

                        def f_zz(V, Pb=Pb, klen=klen, gmax=gmax, prev_tgt=cnt["dve"] - 1):
                            V.wait_ge(SEM["dve"], prev_tgt)
                            V.memset(Pb[:, klen:gmax], 0.0).then_inc(SEM["dve"], 1)
                        em("vector", f_zz)

                    def em_v2(fn, wait_act=None):
                        tgt = post("dve")

                        def f(V, fn=fn, wait_act=wait_act, prev_tgt=tgt - 1):
                            V.wait_ge(SEM["dve"], prev_tgt)
                            if wait_act is not None:
                                V.wait_ge(SEM["act"], wait_act)
                            fn(V).then_inc(SEM["dve"], 1)
                        em("vector", f)
                        return tgt

                    if kgs == 1:
                        em_v2(lambda V: V.reciprocal(C(22), C(8)))
                        pf = em_v2(lambda V, Pb=Pb, klen=klen:
                                   V.tensor_scalar(Pb[:, 0:klen], Pb[:, 0:klen], C(22), None,
                                                   ALU.mult))
                    else:
                        em_v2(lambda V, kgs=kgs: V.reduce_max(C(20), sst[:, 0:kgs], axis=AX))
                        for kg in range(kgs):
                            em_v2(lambda V, kg=kg: V.tensor_tensor(C(12 + kg), C(kg), C(20),
                                                                   ALU.subtract))
                        am_t = post("act")
                        amw = cnt["dve"]

                        def f_am(S, kgs=kgs, amw=amw):
                            S.wait_ge(SEM["dve"], amw)
                            last = None
                            for kg in range(kgs):
                                last = S.activation(C(12 + kg), C(12 + kg), AF.Exp,
                                                    scale=0.08838834764831845)
                            last.then_inc(SEM["act"], 1)
                        em("scalar", f_am)
                        em_v2(lambda V: V.tensor_tensor(C(21), C(12), C(8), ALU.mult),
                              wait_act=am_t)
                        for kg in range(1, kgs):
                            em_v2(lambda V, kg=kg: V.tensor_tensor(C(23), C(12 + kg),
                                                                   C(8 + kg), ALU.mult))
                            em_v2(lambda V: V.tensor_tensor(C(21), C(21), C(23), ALU.add))
                        em_v2(lambda V: V.reciprocal(C(22), C(21)))
                        pf = None
                        for kg in range(kgs):
                            w = min(1024, klen - kg * 1024)
                            em_v2(lambda V, kg=kg: V.tensor_tensor(C(16 + kg), C(12 + kg),
                                                                   C(22), ALU.mult))
                            pf = em_v2(lambda V, kg=kg, w=w, Pb=Pb:
                                       V.tensor_scalar(Pb[:, kg * 1024:kg * 1024 + w],
                                                       Pb[:, kg * 1024:kg * 1024 + w],
                                                       C(16 + kg), None, ALU.mult))
                    p_done.append(pf)
                if pend is not None:
                    P_free[pend[0] % 2] = emit_av(h, pend[0], pend[1], P_pool[pend[0] % 2])
                pend = (qg, p_done)
            P_free[pend[0] % 2] = emit_av(h, pend[0], pend[1], P_pool[pend[0] % 2])
            pend = None
            last_head_S_pe = smms[kgs - 1]
        attn_done_dve = last_zT_ev_holder["v"]
        zT_done_sq = last_zT_ev_holder["zst"]

        if ATTN_STOP == "b4":
            post("g_acc", 16)

            def f_dummy(G, q=last_zT_ev_holder["zst"]):
                G.wait_ge(SEM["s_zt"], q)
                G.dma_start(out=rs_out, in_=x1s).then_inc(SEM["g_acc"], 16)
            em("gpsimd", f_dummy)
            carry["rs_done_cc"] = cnt["cc"]
            carry["rs_ready"] = ("g_acc", cnt["g_acc"])
            carry["rs_out"] = rs_out
            carry["last_store_sq"] = [("s_agx", cnt["s_agx"])]
            carry["last_xev_dve"] = cnt["dve"]
            carry["prev_sd"] = None
            carry["prev2_sd"] = None
            return carry

        woT_res = wdb_flat[:, 0:16384].rearrange("p (z n) -> p z n", z=8)
        zT_v = wgu_flat[:, 6144:38912].rearrange("p (h n) -> p h n", h=8)
        post("s_hld", 32)

        def f_wo(S, attn_done_dve=attn_done_dve, lsp=last_head_S_pe, zds=zT_done_sq):
            S.wait_ge(SEM["dve"], attn_done_dve)
            S.wait_ge(SEM["pe"], lsp)
            S.wait_ge(SEM["s_zt"], zds)
            S.dma_start(out=woT_res,
                        in_=woT_s.rearrange("(z p) n -> p z n", p=128)).then_inc(SEM["s_hld"], 16)
            S.dma_start(out=zT_v,
                        in_=zT_sd.rearrange("(h p) n -> p h n", p=128)).then_inc(SEM["s_hld"], 16)
        em("sync", f_wo)
        wo_ld = cnt["s_hld"]
        rs_stage = xt[:, :].rearrange("p (s n) -> p s n", s=NT)
        prev_ev5 = {}
        last_rs_sq = None
        for tt in range(32):
            sl = tt % 2
            mm5 = post("pe")

            def f_m5(T, tt=tt, sl=sl, wo_ld=wo_ld, prev_ev5=prev_ev5):
                if tt == 0:
                    T.wait_ge(SEM["s_hld"], wo_ld)
                pv = prev_ev5.get(tt - 1)
                if pv is not None:
                    T.wait_ge(SEM["dve"], pv)
                last = None
                for zc in range(8):
                    zs = zT_v[:, zc, tt * 128:(tt + 1) * 128]
                    for qq in range(4):
                        last = T.matmul(psD[:, qq * 512:(qq + 1) * 512], zs,
                                        woT_res[:, zc, qq * 512:(qq + 1) * 512],
                                        start=(zc == 0), stop=(zc == 7))
                last.then_inc(SEM["pe"], 1)
            em("tensor", f_m5)
            ev5 = post("dve")

            def f_e5(V, tt=tt, sl=sl, mm5=mm5):
                V.wait_ge(SEM["pe"], mm5)
                if tt >= 2:
                    V.wait_ge(SEM[f"s_rs{sl}"], carry["_rs_st"][tt - 2])
                V.tensor_copy(rs_stage[:, sl, :], psD[:, :]).then_inc(SEM["dve"], 1)
            em("vector", f_e5)
            prev_ev5[tt] = ev5
            post(f"s_rs{sl}", 16)

            def f_s5(S, tt=tt, sl=sl, ev5=ev5):
                S.wait_ge(SEM["dve"], ev5)
                S.dma_start(out=rs_in[tt * 128:(tt + 1) * 128, :], in_=rs_stage[:, sl, :]
                            ).then_inc(SEM[f"s_rs{sl}"], 16)
            em("sync", f_s5)
            carry.setdefault("_rs_st", {})[tt] = cnt[f"s_rs{sl}"]
            last_rs_sq = (cnt["s_rs0"], cnt["s_rs1"])

        post("cc", 1)
        rs_done = cnt["cc"]

        def f_rs(G, last_rs_sq=last_rs_sq):
            G.wait_ge(SEM["s_rs0"], last_rs_sq[0])
            G.wait_ge(SEM["s_rs1"], last_rs_sq[1])
            G.collective_compute("ReduceScatter", ALU.add,
                                 replica_groups=[[0, 1], [2, 3], [4, 5], [6, 7]],
                                 ins=[rs_in.opt()], outs=[rs_out.opt()]).then_inc(SEM["cc"], 1)
        em("gpsimd", f_rs)
        carry["rs_done_cc"] = rs_done
        carry["rs_ready"] = ("cc", rs_done)
        carry["rs_out"] = rs_out
        carry["last_xev_dve"] = prev_ev5[31]
        carry["last_store_sq"] = [("s_rs0", cnt["s_rs0"]), ("s_rs1", cnt["s_rs1"])]
        carry["prev_sd"] = None
        carry["prev2_sd"] = None
        return carry

    def emit_ffn(ffn_idx, dst, carry):
        for r in range(NR):
            use_cache = not (ffn_idx == 0 and r == 0)
            post("s_x", 16)
            ls = carry.get("last_store_sq")

            def f_x(S, r=r, ls=ls, ffn_idx=ffn_idx):
                if ls is not None:
                    for sname, v in ls:
                        S.wait_ge(SEM[sname], v)
                src = x_in if ffn_idx == 0 else x1s
                S.dma_start(out=xt[:, :].rearrange("p (t n) -> p t n", t=NT),
                            in_=src[r * TPR:(r + 1) * TPR, :].rearrange("(t p) n -> p t n", p=128)
                            ).then_inc(SEM["s_x"], 16)
            em("sync", f_x)
            x_ready = ("s_x", cnt["s_x"])
            if ffn_idx == 1:
                post("g_acc", 16)
                xr = x_ready

                def f_x2(G, r=r, xr=xr, rsr=carry.get("rs_ready")):
                    G.wait_ge(SEM[xr[0]], xr[1])
                    if rsr is not None and rsr[1]:
                        G.wait_ge(SEM[rsr[0]], rsr[1])
                    G.dma_start(out=xt[:, :].rearrange("p (t n) -> p t n", t=NT),
                                in_=carry["rs_out"][r * TPR:(r + 1) * TPR, :]
                                .rearrange("(t p) n -> p t n", p=128),
                                accum_op=ALU.add).then_inc(SEM["g_acc"], 16)
                em("gpsimd", f_x2)
                x_ready = ("g_acc", cnt["g_acc"])

            if ffn_idx == 0 and r == 0:
                pf_dve, pf_pe = gwT_ready_dve, None
            else:
                pf_dve, pf_pe = carry["last_xev_dve"], None
            cmb_ready, lg_free = emit_ln_round(ffn_idx, r, x_ready, pf_dve, pf_pe)
            if DEBUG and ffn_idx == 0 and r == 0:
                emit_dbg_snapshot("dve", cmb_ready, dbg_xlnT, xlnT[:, :])
                emit_dbg_snapshot("dve", cmb_ready, dbg_lgb, lgb[:, :])
                emit_dbg_snapshot("dve", cmb_ready, dbg_cmb, cmb[:, :])

            for e in range(NEXP):
                key = (ffn_idx, r, e)
                wgu_tgts = carry["wgu_tgts"][key]
                nf, nr_, ne = ffn_idx, r, e + 1
                if ne == NEXP:
                    ne, nr_ = 0, r + 1
                    if nr_ == NR:
                        nr_, nf = 0, ffn_idx + 1
                nkey = (nf, nr_, ne)
                for t in range(NT):
                    step = carry["step"]
                    sd = dict(e=e, t=t, slot=step % 2, wgu_tgts=wgu_tgts)
                    prev_sd = carry.get("prev_sd")
                    prev2_sd = carry.get("prev2_sd")
                    flushed = prev_sd is None or prev_sd.get("_emitted_lag", False)

                    emit_g(sd, prev_sd["h"] if prev_sd else None, cmb_ready)
                    if t == NT - 1 and nf == ffn_idx and use_cache:
                        carry["wgu_tgts"].setdefault(nkey, [None, None])[0] = \
                            emit_wgu_dma(True, ne, cnt["pe"], carry.get("cache_sq"), 0)
                    emit_silu(sd, prev2_sd["tr2"] if prev2_sd and "tr2" in prev2_sd else None)
                    if DEBUG and ffn_idx == 0 and r == 0 and e == 0 and t == 0:
                        sd["_dbg_gs"] = emit_dbg_snapshot("act", sd["silu"], dbg_gs,
                                                          gs[:, sd["slot"] * I:(sd["slot"] + 1) * I])
                    if not flushed:
                        emit_lagged(prev_sd, carry)
                    emit_u(sd)
                    if DEBUG and sd.get("_dbg_gs"):
                        sd["_h_extra_wait"] = sd["_dbg_gs"]
                    emit_h(sd, cmb_ready)
                    if not flushed:
                        emit_dnB(prev_sd)
                        emit_xevac(prev_sd)
                        carry["last_xev_dve"] = prev_sd["xev"]
                        prev_sd["_emitted_lag"] = True
                        if DEBUG and prev_sd.get("_dbg_h"):
                            emit_dbg_snapshot("dve", prev_sd["ev2"], dbg_hT,
                                              hT[:, prev_sd["slot"] * NIC * 128:
                                                 (prev_sd["slot"] + 1) * NIC * 128])
                            emit_dbg_snapshot("dve", prev_sd["xev"], dbg_xt,
                                              xt[:, 0:D])
                    if t == 0:
                        wd_key = carry["wd_tgts"].get(key)
                        if wd_key is None:
                            wp = prev_sd["dnB"] if (prev_sd and "dnB" in prev_sd) else None
                            ws = carry.get("cache_sq")
                            carry["wd_tgts"][key] = emit_wd_dma(use_cache, e, wp, ws)
                    sd["wd_tgts"] = carry["wd_tgts"][key]
                    sd["_psTr_free"] = prev_sd["ev2"] if (prev_sd and "ev2" in prev_sd) else None
                    sd["_hT_free"] = (prev2_sd["dnB"] if prev2_sd and "dnB" in prev2_sd else None)
                    sd["_psD_free"] = (prev_sd["xev"] if (prev_sd and "xev" in prev_sd
                                       and not prev_sd.get("_flush_boundary")) else lg_free)
                    carry["prev2_sd"] = prev_sd
                    carry["prev_sd"] = sd
                    carry["step"] = step + 1
                if not use_cache:
                    carry["cache_sq"] = emit_cache_write(
                        e, wgu_tgts, carry["wd_tgts"][key])
                if nf == ffn_idx:
                    nuc = not (nf == 0 and nr_ == 0)
                    carry["wgu_tgts"].setdefault(nkey, [None, None])
                    if carry["wgu_tgts"][nkey][0] is None:
                        carry["wgu_tgts"][nkey][0] = emit_wgu_dma(
                            nuc, ne, cnt["pe"], carry.get("cache_sq"), 0)
                    carry["wgu_tgts"][nkey][1] = emit_wgu_dma(
                        nuc, ne, cnt["pe"], carry.get("cache_sq"), 1)

            psd_ = carry["prev_sd"]
            emit_lagged(psd_, carry)
            emit_dnB(psd_)
            emit_xevac(psd_)
            psd_["_emitted_lag"] = True
            psd_["_flush_boundary"] = True
            carry["last_xev_dve"] = psd_["xev"]

            post("s_stx", 16)
            lxv = carry["last_xev_dve"]

            def f_s(S, r=r, dst=dst, lxv=lxv):
                S.wait_ge(SEM["dve"], lxv)
                S.dma_start(out=dst[r * TPR:(r + 1) * TPR, :].rearrange("(t p) n -> p t n", p=128),
                            in_=xt[:, :].rearrange("p (t n) -> p t n", t=NT)
                            ).then_inc(SEM["s_stx"], 16)
            em("sync", f_s)
            carry["last_store_sq"] = [("s_stx", cnt["s_stx"])]
        return carry

    carry = {"step": 0, "wgu_tgts": {}, "wd_tgts": {}, "cache_sq": None,
             "prev_sd": None, "prev2_sd": None, "last_xev_dve": None,
             "last_store_sq": None}
    carry["wgu_tgts"][(0, 0, 0)] = [emit_wgu_dma(False, 0, None, None, 0),
                                    emit_wgu_dma(False, 0, None, None, 1)]
    carry["wd_tgts"][(0, 0, 0)] = emit_wd_dma(False, 0, None, None)

    dst1 = y_out if PHASES == "ffn1" else x1s
    carry = emit_ffn(0, dst1, carry)

    if PHASES in ("attn", "full"):
        carry["_wT_wait_pe"] = cnt["pe"]
        carry = emit_attention(carry)
        if PHASES == "attn":
            for r in range(NR):
                post("s_x", 16)
                ls = carry.get("last_store_sq")

                def f_xa(S, r=r, ls=ls, lxd=carry.get("last_xev_dve")):
                    if ls is not None:
                        for sname, v in ls:
                            S.wait_ge(SEM[sname], v)
                    if r == 0 and lxd is not None:
                        S.wait_ge(SEM["dve"], lxd)
                    if r > 0:
                        S.wait_ge(SEM["s_stx"], carry["_at_st"][r - 1])
                    S.dma_start(out=xt[:, :].rearrange("p (t n) -> p t n", t=NT),
                                in_=x1s[r * TPR:(r + 1) * TPR, :]
                                .rearrange("(t p) n -> p t n", p=128)).then_inc(SEM["s_x"], 16)
                em("sync", f_xa)
                xl = cnt["s_x"]
                post("g_acc", 16)

                def f_xb(G, r=r, xl=xl, rsr=carry["rs_ready"]):
                    G.wait_ge(SEM["s_x"], xl)
                    if rsr[1]:
                        G.wait_ge(SEM[rsr[0]], rsr[1])
                    G.dma_start(out=xt[:, :].rearrange("p (t n) -> p t n", t=NT),
                                in_=carry["rs_out"][r * TPR:(r + 1) * TPR, :]
                                .rearrange("(t p) n -> p t n", p=128),
                                accum_op=ALU.add).then_inc(SEM["g_acc"], 16)
                em("gpsimd", f_xb)
                xg = cnt["g_acc"]
                post("s_stx", 16)

                def f_xs(S, r=r, xg=xg):
                    S.wait_ge(SEM["g_acc"], xg)
                    S.dma_start(out=y_out[r * TPR:(r + 1) * TPR, :]
                                .rearrange("(t p) n -> p t n", p=128),
                                in_=xt[:, :].rearrange("p (t n) -> p t n", t=NT)
                                ).then_inc(SEM["s_stx"], 16)
                em("sync", f_xs)
                carry.setdefault("_at_st", {})[r] = cnt["s_stx"]
                carry["last_store_sq"] = [("s_stx", cnt["s_stx"])]
        else:
            carry["wgu_tgts"][(1, 0, 0)] = [emit_wgu_dma(True, 0, cnt["pe"], None, 0),
                                            emit_wgu_dma(True, 0, cnt["pe"], None, 1)]
            carry["wd_tgts"][(1, 0, 0)] = emit_wd_dma(True, 0, cnt["pe"], None)
            carry = emit_ffn(1, y_out, carry)

    fins = {k: v for k, v in cnt.items()
            if v > 0 and k not in ("pe", "act", "dve", "cc")}
    fin_cc = cnt["cc"]

    def f_fin(G, fins=fins, fin_cc=fin_cc):
        for k, v in fins.items():
            G.wait_ge(SEM[k], v)
        if fin_cc:
            G.wait_ge(SEM["cc"], fin_cc)
    em("gpsimd", f_fin)

    with nc.Block() as blk:
        blk.sync(lambda E: [f(E) for f in streams["sync"]])
        blk.gpsimd(lambda E: [f(E) for f in streams["gpsimd"]])
        blk.tensor(lambda E: [f(E) for f in streams["tensor"]])
        blk.vector(lambda E: [f(E) for f in streams["vector"]])
        blk.scalar(lambda E: [f(E) for f in streams["scalar"]])

    fes.close()
    es.close()
    return nc




from concourse.bass_utils import run_bass_kernel_spmd


def kernel(L0_attn_x, L0_attn_Wq, L0_attn_Wk, L0_attn_Wv, L0_attn_Wo,
           L0_ffn_gate_w, L0_ffn_w_gate, L0_ffn_w_up, L0_ffn_w_down,
           L0_ffn_sw_gate, L0_ffn_sw_up, L0_ffn_sw_down, L0_ffn_shared_gate_w):
    xs = np.ascontiguousarray(np.asarray(L0_attn_x, dtype=np.float32))
    wq = np.ascontiguousarray(np.asarray(L0_attn_Wq, dtype=np.float32))
    wk = np.ascontiguousarray(np.asarray(L0_attn_Wk, dtype=np.float32))
    wv = np.ascontiguousarray(np.asarray(L0_attn_Wv, dtype=np.float32))
    wo = np.ascontiguousarray(np.asarray(L0_attn_Wo, dtype=np.float32))
    gate_w = np.ascontiguousarray(np.asarray(L0_ffn_gate_w, dtype=np.float32))
    w_gate = np.ascontiguousarray(np.asarray(L0_ffn_w_gate, dtype=np.float32))
    w_up = np.ascontiguousarray(np.asarray(L0_ffn_w_up, dtype=np.float32))
    w_down = np.ascontiguousarray(np.asarray(L0_ffn_w_down, dtype=np.float32))
    sw_gate = np.ascontiguousarray(np.asarray(L0_ffn_sw_gate, dtype=np.float32))
    sw_up = np.ascontiguousarray(np.asarray(L0_ffn_sw_up, dtype=np.float32))
    sw_down = np.ascontiguousarray(np.asarray(L0_ffn_sw_down, dtype=np.float32))
    sgw = np.ascontiguousarray(np.asarray(L0_ffn_shared_gate_w, dtype=np.float32))

    nc = build_program()
    in_maps = []
    for core in range(8):
        b, hf = core // 2, core % 2
        in_maps.append({
            "x": np.ascontiguousarray(xs[b, hf * 2048:(hf + 1) * 2048, :]),
            "gate_w": gate_w, "sgw": sgw,
            "w_gate": w_gate, "w_up": w_up, "w_down": w_down,
            "sw_gate": sw_gate, "sw_up": sw_up, "sw_down": sw_down,
            "wq": np.ascontiguousarray(wq[hf * 1024:(hf + 1) * 1024, :]),
            "wk": np.ascontiguousarray(wk[hf * 1024:(hf + 1) * 1024, :]),
            "wv": np.ascontiguousarray(wv[hf * 1024:(hf + 1) * 1024, :]),
            "wo": np.ascontiguousarray(wo[:, hf * 1024:(hf + 1) * 1024]),
        })
    res = run_bass_kernel_spmd(nc, in_maps, core_ids=list(range(8)))
    out = np.zeros((4, 4096, 2048), np.float32)
    for core in range(8):
        b, hf = core // 2, core % 2
        out[b, hf * 2048:(hf + 1) * 2048, :] = res.results[core]["y"]
    return out



# version 05
# speedup vs baseline: 1.2190x; 1.0742x over previous
import numpy as np
import ml_dtypes
from contextlib import ExitStack

import concourse.bass as bass
import concourse.mybir as mybir

F32 = mybir.dt.float32
BF16 = mybir.dt.bfloat16
AF = mybir.ActivationFunctionType
ALU = mybir.AluOpType
AX = mybir.AxisListType.X
BF = ml_dtypes.bfloat16

D = 2048
I = 1408
NEXP = 12
TOK = 2048
TPR = 512
NT = 4
NR = TOK // TPR
NCI = 16
NIC = 11
EPS = 1e-5

PHASES = "full"
DEBUG = False
import os
ATTN_STOP = os.environ.get("ATTN_STOP", "")


def build_program():
    nc = bass.Bass("TRN2", target_bir_lowering=False)
    es = ExitStack()
    streams = {"sync": [], "gpsimd": [], "tensor": [], "vector": [], "scalar": []}
    cnt = {}
    SEM = {}
    for name in ["pe", "act", "dve", "gq", "sq", "cc",
                 "s_const", "s_x", "s_stx", "s_cache",
                 "g_wgu0", "g_wgu1", "g_wgu2", "g_wgu3",
                 "g_wd0", "g_wd1", "g_wd2", "g_acc", "g_attw",
                 "s_agx", "s_wT", "s_qkv", "s_zt", "s_rs0", "s_rs1",
                 "s_ld3a", "s_ld3b", "s_slab", "s_hld", "s_qkva", "s_qkvb",
                 "g_wuu0", "g_wuu1", "g_wuu2", "g_wuu3"]:
        SEM[name] = es.enter_context(nc.semaphore(name))
        cnt[name] = 0

    def post(name, k=1):
        cnt[name] += k
        return cnt[name]

    def em(eng, fn):
        streams[eng].append(fn)

    x_in = nc.dram_tensor("x", [TOK, D], F32, kind="ExternalInput").ap()
    gate_w = nc.dram_tensor("gate_w", [8, D], F32, kind="ExternalInput").ap()
    sgw = nc.dram_tensor("sgw", [D, 1], F32, kind="ExternalInput").ap()
    w_gate = nc.dram_tensor("w_gate", [8, D, I], F32, kind="ExternalInput").ap()
    w_up = nc.dram_tensor("w_up", [8, D, I], F32, kind="ExternalInput").ap()
    w_down = nc.dram_tensor("w_down", [8, I, D], F32, kind="ExternalInput").ap()
    sw_gate = nc.dram_tensor("sw_gate", [D, 4 * I], F32, kind="ExternalInput").ap()
    sw_up = nc.dram_tensor("sw_up", [D, 4 * I], F32, kind="ExternalInput").ap()
    sw_down = nc.dram_tensor("sw_down", [4 * I, D], F32, kind="ExternalInput").ap()
    wq = nc.dram_tensor("wq", [1024, D], F32, kind="ExternalInput").ap()
    wk = nc.dram_tensor("wk", [1024, D], F32, kind="ExternalInput").ap()
    wv = nc.dram_tensor("wv", [1024, D], F32, kind="ExternalInput").ap()
    wo = nc.dram_tensor("wo", [D, 1024], F32, kind="ExternalInput").ap()
    y_out = nc.dram_tensor("y", [TOK, D], F32, kind="ExternalOutput").ap()

    if DEBUG:
        dbg_xlnT = nc.dram_tensor("dbg_xlnT", [128, NCI * TPR], BF16, kind="ExternalOutput").ap()
        dbg_lgb = nc.dram_tensor("dbg_lgb", [128, NT * 16], F32, kind="ExternalOutput").ap()
        dbg_cmb = nc.dram_tensor("dbg_cmb", [128, NT * 16], F32, kind="ExternalOutput").ap()
        dbg_gs = nc.dram_tensor("dbg_gs", [128, I], BF16, kind="ExternalOutput").ap()
        dbg_h = nc.dram_tensor("dbg_h", [128, I], BF16, kind="ExternalOutput").ap()
        dbg_hT = nc.dram_tensor("dbg_hT", [128, NIC * 128], BF16, kind="ExternalOutput").ap()
        dbg_xt = nc.dram_tensor("dbg_xt", [128, D], F32, kind="ExternalOutput").ap()
    x1s = nc.dram_tensor("x1s", [TOK, D], F32).ap()
    wgu_c = nc.dram_tensor("wgu_c", [NEXP, D, 2 * I], BF16).ap()
    wd_c = nc.dram_tensor("wd_c", [NEXP, I, D], BF16).ap()
    agx_in = nc.dram_tensor("agx_in", [D, TOK], BF16).ap()
    agx_out = nc.dram_tensor("agx_out", [2 * D, TOK], BF16).ap()
    wqT_s = nc.dram_tensor("wqT_s", [D, 1024], BF16).ap()
    wkT_s = nc.dram_tensor("wkT_s", [D, 1024], BF16).ap()
    wvT_s = nc.dram_tensor("wvT_s", [D, 1024], BF16).ap()
    woT_s = nc.dram_tensor("woT_s", [1024, D], BF16).ap()
    qT_sd = nc.dram_tensor("qT_sd", [1024, 4096], BF16).ap()
    kT_sd = nc.dram_tensor("kT_sd", [1024, 4096], BF16).ap()
    vT_sd = nc.dram_tensor("vT_sd", [1024, 4096], BF16).ap()
    zT_sd = nc.dram_tensor("zT_sd", [1024, 4096], BF16).ap()
    rs_in = nc.dram_tensor("rs_in", [4096, D], F32).ap()
    rs_out = nc.dram_tensor("rs_out", [TOK, D], F32).ap()

    ident_bf_d = nc.inline_tensor(np.eye(128, dtype=BF), "ident_bf_d")
    ident_f_d = nc.inline_tensor(np.eye(128, dtype=np.float32), "ident_f_d")
    tri_d = nc.inline_tensor(np.tril(np.ones((128, 128), dtype=np.float32)).astype(BF), "tri01_d")
    kconst_d = nc.inline_tensor(np.full((128, 2), EPS, dtype=np.float32), "kconst_d")

    ident_bf = es.enter_context(nc.sbuf_tensor("ident_bf", [128, 128], BF16))
    ident_f = es.enter_context(nc.sbuf_tensor("ident_f", [128, 128], F32))
    tri01 = es.enter_context(nc.sbuf_tensor("tri01", [128, 128], BF16))
    gwT = es.enter_context(nc.sbuf_tensor("gwT", [128, NCI * 16], F32))
    kconst = es.enter_context(nc.sbuf_tensor("kconst", [128, 2], F32))

    c_tgt = post("s_const", 16 * 4)

    def f_const(S):
        S.dma_start(out=kconst[:, :], in_=kconst_d.ap()).then_inc(SEM["s_const"], 16)
        S.dma_start(out=ident_bf[:, :], in_=ident_bf_d.ap()).then_inc(SEM["s_const"], 16)
        S.dma_start(out=ident_f[:, :], in_=ident_f_d.ap()).then_inc(SEM["s_const"], 16)
        S.dma_start(out=tri01[:, :], in_=tri_d.ap()).then_inc(SEM["s_const"], 16)
    em("sync", f_const)

    fes = ExitStack()
    xt = fes.enter_context(nc.sbuf_tensor("xt", [128, NT * D], F32))
    xlnT = fes.enter_context(nc.sbuf_tensor("xlnT", [128, NCI * TPR], BF16))
    wgu = fes.enter_context(nc.sbuf_tensor("wgu", [128, NCI * 2 * I], BF16))
    wdb = fes.enter_context(nc.sbuf_tensor("wdb", [128, NIC * D], BF16))
    gs = fes.enter_context(nc.sbuf_tensor("gs", [128, 2 * I], BF16))
    hT = fes.enter_context(nc.sbuf_tensor("hT", [128, 2 * NIC * 128], BF16))
    cmb = fes.enter_context(nc.sbuf_tensor("cmb", [128, NT * 16], F32))
    lgb = fes.enter_context(nc.sbuf_tensor("lgb", [128, NT * 16], F32))
    xf32 = fes.enter_context(nc.sbuf_tensor("xf32", [128, 2 * 128], F32))
    xT32 = fes.enter_context(nc.sbuf_tensor("xT32", [128, 2 * 128], F32))
    sst = fes.enter_context(nc.sbuf_tensor("sst", [128, NT * 8], F32))
    t2s = fes.enter_context(nc.sbuf_tensor("t2s", [128, NT * 48], F32))

    psGU = fes.enter_context(nc.psum_tensor("psGU", [128, 1536], F32))
    psD = fes.enter_context(nc.psum_tensor("psD", [128, 2048], F32))
    psTr = fes.enter_context(nc.psum_tensor("psTr", [128, 1024], BF16))

    wgu_v = wgu[:, :].rearrange("p (c n) -> p c n", c=NCI)
    wdb_v = wdb[:, :].rearrange("p (c n) -> p c n", c=NIC)
    xlnT_v = xlnT[:, :].rearrange("p (c n) -> p c n", c=NCI)
    lgps = psD[:, 1536:1552]

    gw_nat_t = fes.enter_context(nc.sbuf_tensor("gw_nat", [8, D], F32))
    gw_nat = gw_nat_t[:, :]
    sg_nat = fes.enter_context(nc.sbuf_tensor("sg_nat", [128, NCI], F32))
    gl = post("s_const", 16 * 2)

    def f_gwl(S):
        S.dma_start(out=gw_nat, in_=gate_w).then_inc(SEM["s_const"], 16)
        S.dma_start(out=sg_nat[:, :].rearrange("p (c one) -> p c one", one=1),
                    in_=sgw.rearrange("(c p) one -> p c one", p=128),
                    allow_slow_non_contiguous=True).then_inc(SEM["s_const"], 16)
    em("sync", f_gwl)

    gz = post("dve")

    def f_gz(V):
        V.memset(gwT[:, :], 0.0).then_inc(SEM["dve"], 1)
    em("vector", f_gz)

    gwtr, gwev = {}, {}
    for c in range(NCI):
        gwtr[c] = post("pe")

        def f_t(T, c=c):
            if c == 0:
                T.wait_ge(SEM["s_const"], gl)
                T.wait_ge(SEM["dve"], gz)
            if c >= 3:
                T.wait_ge(SEM["dve"], gwev[c - 3])
            T.transpose(psD[:, (c % 3) * 512:(c % 3) * 512 + 8],
                        gw_nat_t[0:8, c * 128:(c + 1) * 128], ident_f[0:8, 0:8]).then_inc(SEM["pe"], 1)
        em("tensor", f_t)
        gwev[c] = post("dve")

        def f_v(V, c=c):
            V.wait_ge(SEM["pe"], gwtr[c])
            V.tensor_copy(gwT[:, c * 16:c * 16 + 8],
                          psD[:, (c % 3) * 512:(c % 3) * 512 + 8]).then_inc(SEM["dve"], 1)
        em("vector", f_v)
    gws = post("dve")

    def f_sg(V):
        V.wait_ge(SEM["s_const"], gl)
        V.tensor_copy(gwT[:, :].rearrange("p (c n) -> p c n", c=NCI)[:, :, 8:9],
                      sg_nat[:, :].rearrange("p (c one) -> p c one", one=1)).then_inc(SEM["dve"], 1)
    em("vector", f_sg)
    gwT_ready_dve = gws
    gwT_psD_free_dve = gwev[NCI - 1]

    def src_gu(e, half):
        if e < 8:
            t = w_gate if half == 0 else w_up
            return t[e].rearrange("(c p) i -> p c i", p=128)
        j = e - 8
        t = sw_gate if half == 0 else sw_up
        return t.rearrange("(c p) i -> p c i", p=128)[:, :, j * I:(j + 1) * I]

    def src_d(e):
        if e < 8:
            return w_down[e].rearrange("(c p) d -> p c d", p=128)
        j = e - 8
        return sw_down[j * I:(j + 1) * I, :].rearrange("(c p) d -> p c d", p=128)

    def emit_wgu_dma(use_cache, e, wait_pe, wait_sq, half):
        tgts = []
        for g in range(4):
            cs = slice(g * 4, (g + 1) * 4)
            sname = f"g_wgu{g}" if half == 0 else f"g_wuu{g}"
            post(sname, 16)

            def f(G, g=g, e=e, cs=cs, wp=wait_pe, ws=wait_sq, uc=use_cache,
                  sname=sname, half=half):
                if g == 0:
                    if wp is not None:
                        G.wait_ge(SEM["pe"], wp)
                    if ws is not None:
                        G.wait_ge(SEM["s_cache"], ws)
                if uc:
                    G.dma_start(out=wgu_v[:, cs, half * I:(half + 1) * I],
                                in_=wgu_c[e].rearrange("(c p) n -> p c n", p=128)
                                [:, cs, half * I:(half + 1) * I]).then_inc(SEM[sname], 16)
                else:
                    G.dma_start(out=wgu_v[:, cs, half * I:(half + 1) * I],
                                in_=src_gu(e, half)[:, cs, :]).then_inc(SEM[sname], 16)
            em("gpsimd", f)
            tgts.append((sname, cnt[sname]))
        return tgts

    def emit_wd_dma(use_cache, e, wait_pe, wait_sq):
        tgts = []
        for gi, (c0, c1) in enumerate([(0, 4), (4, 8), (8, 11)]):
            sname = f"g_wd{gi}"
            post(sname, 16)

            def f(G, gi=gi, c0=c0, c1=c1, e=e, wp=wait_pe, ws=wait_sq, uc=use_cache,
                  sname=sname):
                if gi == 0:
                    if wp is not None:
                        G.wait_ge(SEM["pe"], wp)
                    if ws is not None:
                        G.wait_ge(SEM["s_cache"], ws)
                if uc:
                    G.dma_start(out=wdb_v[:, c0:c1, :],
                                in_=wd_c[e].rearrange("(c p) n -> p c n", p=128)[:, c0:c1, :]
                                ).then_inc(SEM[sname], 16)
                else:
                    G.dma_start(out=wdb_v[:, c0:c1, :],
                                in_=src_d(e)[:, c0:c1, :]).then_inc(SEM[sname], 16)
            em("gpsimd", f)
            tgts.append((sname, cnt[sname]))
        return tgts

    def emit_cache_write(e, wgu_tgts, wd_tgts):
        post("s_cache", 32)

        def f(S, e=e, wgu_tgts=wgu_tgts, wd_tgts=wd_tgts):
            for part in wgu_tgts:
                for sname, v in part:
                    S.wait_ge(SEM[sname], v)
            S.dma_start(out=wgu_c[e].rearrange("(c p) n -> p c n", p=128),
                        in_=wgu_v[:, :, :]).then_inc(SEM["s_cache"], 16)
            for sname, v in wd_tgts:
                S.wait_ge(SEM[sname], v)
            S.dma_start(out=wd_c[e].rearrange("(c p) n -> p c n", p=128),
                        in_=wdb_v[:, :, :]).then_inc(SEM["s_cache"], 16)
        em("sync", f)
        return cnt["s_cache"]

    def emit_ln_round(ffn_idx, r, x_ready, psD_free_dve, psD_free_pe):
        stats1 = {}
        for t in range(NT):
            xt_t = xt[:, t * D:(t + 1) * D]
            mcol = sst[:, t * 8 + 0:t * 8 + 1]
            qcol = sst[:, t * 8 + 1:t * 8 + 2]
            q2col = sst[:, t * 8 + 6:t * 8 + 7]
            tcol = sst[:, t * 8 + 2:t * 8 + 3]
            vcol = sst[:, t * 8 + 3:t * 8 + 4]
            stats1[t] = post("dve", 2)

            def f(V, t=t, xt_t=xt_t, mcol=mcol, qcol=qcol, q2col=q2col,
                  x_ready=x_ready, pfree=psD_free_dve, ppe=psD_free_pe):
                if t == 0:
                    V.wait_ge(SEM[x_ready[0]], x_ready[1])
                    if pfree is not None:
                        V.wait_ge(SEM["dve"], pfree)
                    if ppe is not None:
                        V.wait_ge(SEM["pe"], ppe)
                V.reduce_sum(mcol, xt_t, axis=AX)
                V.scalar_tensor_tensor(psGU[:, 0:1536], xt_t[:, 0:1536], 1.0, xt_t[:, 0:1536],
                                       ALU.mult, ALU.mult, accum_out=qcol)
                V.scalar_tensor_tensor(gs[:, 0:512], xt_t[:, 1536:2048], 1.0,
                                       xt_t[:, 1536:2048], ALU.mult, ALU.mult,
                                       accum_out=q2col).then_inc(SEM["dve"], 2)
            em("vector", f)
            m2t = post("dve")

            def f_m2(V, mcol=mcol, tcol=tcol, qcol=qcol, q2col=q2col, s1=stats1[t]):
                V.wait_ge(SEM["dve"], s1)
                V.tensor_tensor(qcol, qcol, q2col, ALU.add)
                V.scalar_tensor_tensor(tcol, mcol, 1.0 / (D * D), mcol, ALU.mult, ALU.mult
                                       ).then_inc(SEM["dve"], 1)
            em("vector", f_m2)
            stats1[t] = m2t
        var_done = {}
        for t in range(NT):
            qcol = sst[:, t * 8 + 1:t * 8 + 2]
            tcol = sst[:, t * 8 + 2:t * 8 + 3]
            vcol = sst[:, t * 8 + 3:t * 8 + 4]
            var_done[t] = post("dve")

            def f(V, t=t, qcol=qcol, tcol=tcol, vcol=vcol):
                V.wait_ge(SEM["dve"], stats1[t])
                V.scalar_tensor_tensor(vcol, qcol, 1.0 / D, tcol, ALU.mult, ALU.subtract
                                       ).then_inc(SEM["dve"], 1)
            em("vector", f)
        ln4 = post("act")
        rstd_done = post("act")

        def f_ln4(S):
            S.wait_ge(SEM["dve"], var_done[NT - 1])
            last = None
            for t in range(NT):
                vcol = sst[:, t * 8 + 3:t * 8 + 4]
                tcol = sst[:, t * 8 + 2:t * 8 + 3]
                last = S.activation(tcol, vcol, AF.Ln, bias=kconst[:, 0:1])
            last.then_inc(SEM["act"], 1)
        em("scalar", f_ln4)

        def f_rstd(S, ln4=ln4):
            S.wait_ge(SEM["act"], ln4)
            last = None
            for t in range(NT):
                tcol = sst[:, t * 8 + 2:t * 8 + 3]
                rcol = sst[:, t * 8 + 4:t * 8 + 5]
                last = S.activation(rcol, tcol, AF.Exp, scale=-0.5)
            last.then_inc(SEM["act"], 1)
        em("scalar", f_rstd)
        negm_done = post("dve")

        def f_negm(V):
            V.wait_ge(SEM["act"], rstd_done)
            last = None
            for t in range(NT):
                mcol = sst[:, t * 8 + 0:t * 8 + 1]
                ncol = sst[:, t * 8 + 5:t * 8 + 6]
                last = V.tensor_scalar_mul(ncol, mcol, -1.0 / D)
            last.then_inc(SEM["dve"], 1)
        em("vector", f_negm)

        lg_mm_last = {}
        prev_tile_lg_dve = None
        ptrk, pevk, pmmk = None, None, None
        SLOT3 = {0: 15, 1: 13, 2: 14}
        SLOT2 = {0: 14, 1: 15}
        for t in range(NT):
            xt_t = xt[:, t * D:(t + 1) * D]
            ncol = sst[:, t * 8 + 5:t * 8 + 6]
            rcol = sst[:, t * 8 + 4:t * 8 + 5]
            vck, trk, evk, mmk = {}, {}, {}, {}
            for c in range(NCI):
                slot = c % 2
                b3 = (c % 3) * 512
                vck[c] = post("dve")

                def f_vc(V, c=c, t=t, slot=slot, trk=trk, ptrk=ptrk,
                         ncol=ncol, rcol=rcol):
                    if c == 0:
                        V.wait_ge(SEM["dve"], negm_done)
                    if c >= 2:
                        V.wait_ge(SEM["pe"], trk[c - 2])
                    elif ptrk is not None:
                        V.wait_ge(SEM["pe"], ptrk[SLOT2[c]])
                    V.tensor_scalar(xf32[:, slot * 128:(slot + 1) * 128],
                                    xt[:, t * D + c * 128:t * D + (c + 1) * 128],
                                    ncol, rcol, ALU.add, ALU.mult).then_inc(SEM["dve"], 1)
                em("vector", f_vc)
                trk[c] = post("pe")

                def f_tc(T, c=c, t=t, slot=slot, b3=b3, vck=vck, evk=evk, pevk=pevk):
                    T.wait_ge(SEM["dve"], vck[c])
                    if c >= 3:
                        T.wait_ge(SEM["dve"], evk[c - 3])
                    elif pevk is not None:
                        T.wait_ge(SEM["dve"], pevk[SLOT3[c]])
                    T.transpose(psD[:, b3:b3 + 128], xf32[:, slot * 128:(slot + 1) * 128],
                                ident_f[:, :]).then_inc(SEM["pe"], 1)
                em("tensor", f_tc)
                evk[c] = post("dve", 2)

                def f_ec(V, c=c, t=t, slot=slot, b3=b3, trk=trk, mmk=mmk, pmmk=pmmk):
                    V.wait_ge(SEM["pe"], trk[c])
                    if c >= 2:
                        V.wait_ge(SEM["pe"], mmk[c - 2])
                    elif pmmk is not None:
                        V.wait_ge(SEM["pe"], pmmk[SLOT2[c]])
                    V.tensor_copy(xT32[:, slot * 128:(slot + 1) * 128], psD[:, b3:b3 + 128])
                    V.tensor_copy(xlnT_v[:, c, t * 128:(t + 1) * 128],
                                  psD[:, b3:b3 + 128]).then_inc(SEM["dve"], 2)
                em("vector", f_ec)
                mmk[c] = post("pe")

                def f_mc(T, c=c, t=t, slot=slot, evk=evk, ptl=prev_tile_lg_dve,
                         need_gwT=(ffn_idx == 0 and r == 0)):
                    T.wait_ge(SEM["dve"], evk[c])
                    if c == 0 and ptl is not None:
                        T.wait_ge(SEM["dve"], ptl)
                    if need_gwT and c == 0 and t == 0:
                        T.wait_ge(SEM["dve"], gwT_ready_dve)
                    T.matmul(lgps, xT32[:, slot * 128:(slot + 1) * 128],
                             gwT[:, c * 16:c * 16 + 16],
                             start=(c == 0), stop=(c == NCI - 1)).then_inc(SEM["pe"], 1)
                em("tensor", f_mc)
            lg_mm_last[t] = mmk[NCI - 1]
            prev_tile_lg_dve = post("dve")

            def f_lg(V, t=t, mm_last=lg_mm_last[t]):
                V.wait_ge(SEM["pe"], mm_last)
                V.tensor_copy(lgb[:, t * 16:(t + 1) * 16], lgps).then_inc(SEM["dve"], 1)
            em("vector", f_lg)
            ptrk, pevk, pmmk = trk, evk, mmk

        lg_all_dve = prev_tile_lg_dve
        ddp = {}

        def em_v(fn, wait_act=None, extra_dve=None):
            tgt = post("dve")

            def f(V, fn=fn, wait_act=wait_act, prev_tgt=tgt - 1, extra_dve=extra_dve):
                V.wait_ge(SEM["dve"], prev_tgt)
                if extra_dve is not None:
                    V.wait_ge(SEM["dve"], extra_dve)
                if wait_act is not None:
                    V.wait_ge(SEM["act"], wait_act)
                fn(V).then_inc(SEM["dve"], 1)
            em("vector", f)
            return tgt

        for t in range(NT):
            base = t * 48
            m1 = t2s[:, base + 0:base + 1]; m2 = t2s[:, base + 1:base + 2]
            ddc = t2s[:, base + 2:base + 3]
            eq1 = t2s[:, base + 8:base + 16]; lgm = t2s[:, base + 16:base + 24]
            eq2 = t2s[:, base + 24:base + 32]
            l8 = lgb[:, t * 16:t * 16 + 8]
            em_v(lambda V, m1=m1, l8=l8: V.reduce_max(m1, l8, axis=AX),
                 extra_dve=lg_all_dve)
            em_v(lambda V, eq1=eq1, l8=l8, m1=m1: V.tensor_scalar(eq1, l8, m1, None, ALU.is_equal))
            em_v(lambda V, lgm=lgm, eq1=eq1, l8=l8:
                 V.scalar_tensor_tensor(lgm, eq1, -1e30, l8, ALU.mult, ALU.add))
            em_v(lambda V, m2=m2, lgm=lgm: V.reduce_max(m2, lgm, axis=AX))
            em_v(lambda V, eq2=eq2, lgm=lgm, m2=m2: V.tensor_scalar(eq2, lgm, m2, None, ALU.is_equal))
            ddp[t] = em_v(lambda V, ddc=ddc, m2=m2, m1=m1: V.tensor_tensor(ddc, m2, m1, ALU.subtract))
        exps_done = post("act")

        def f_exps(S):
            S.wait_ge(SEM["dve"], ddp[NT - 1])
            last = None
            for t in range(NT):
                base = t * 48
                last = S.activation(t2s[:, base + 3:base + 4], t2s[:, base + 2:base + 3], AF.Exp)
            for t in range(NT):
                base = t * 48
                last = S.activation(t2s[:, base + 6:base + 7], lgb[:, t * 16 + 8:t * 16 + 9],
                                    AF.Exp, scale=-1.0)
            last.then_inc(SEM["act"], 1)
        em("scalar", f_exps)
        cmb_done = None
        for t in range(NT):
            base = t * 48
            eec = t2s[:, base + 3:base + 4]; ssc = t2s[:, base + 4:base + 5]
            rrc = t2s[:, base + 5:base + 6]; egc = t2s[:, base + 6:base + 7]
            w1c = t2s[:, base + 7:base + 8]
            eq1 = t2s[:, base + 8:base + 16]; eq2 = t2s[:, base + 24:base + 32]
            t1c = t2s[:, base + 32:base + 40]
            w2c = t2s[:, base + 40:base + 41]
            sgc = t2s[:, base + 41:base + 42]
            hgc = t2s[:, base + 42:base + 43]
            em_v(lambda V, ssc=ssc, eec=eec: V.tensor_scalar_add(ssc, eec, 1.0),
                 wait_act=exps_done)
            em_v(lambda V, rrc=rrc, ssc=ssc: V.reciprocal(rrc, ssc))
            em_v(lambda V, w1c=w1c, rrc=rrc: V.tensor_scalar_mul(w1c, rrc, 0.5))
            em_v(lambda V, w2c=w2c, eec=eec, rrc=rrc: V.tensor_tensor(w2c, eec, rrc, ALU.mult))
            em_v(lambda V, w2c=w2c: V.tensor_scalar_mul(w2c, w2c, 0.5))
            em_v(lambda V, t1c=t1c, eq1=eq1, w1c=w1c: V.tensor_scalar(t1c, eq1, w1c, None, ALU.mult))
            em_v(lambda V, t=t, eq2=eq2, w2c=w2c, t1c=t1c:
                 V.scalar_tensor_tensor(cmb[:, t * 16:t * 16 + 8], eq2, w2c, t1c,
                                        ALU.mult, ALU.add))
            em_v(lambda V, sgc=sgc, egc=egc: V.tensor_scalar_add(sgc, egc, 1.0))
            em_v(lambda V, hgc=hgc, sgc=sgc: V.reciprocal(hgc, sgc))
            cmb_done = em_v(lambda V, t=t, hgc=hgc:
                            V.tensor_scalar_mul(cmb[:, t * 16 + 8:t * 16 + 9], hgc, 0.5))
        return cmb_done, lg_all_dve

    dbg_state = {"done": False}

    def emit_dbg_snapshot(wait_sem, wait_val, out_ap, in_ap):
        if not DEBUG:
            return None
        post("gq", 16)

        def f(G, wait_sem=wait_sem, wait_val=wait_val, out_ap=out_ap, in_ap=in_ap):
            G.wait_ge(SEM[wait_sem], wait_val)
            G.dma_start(out=out_ap, in_=in_ap).then_inc(SEM["gq"], 16)
        em("gpsimd", f)
        return cnt["gq"]

    def emit_g(sd, psGU_free_dve, xln_ready_dve, c_range, mark=False):
        tgt = post("pe") if mark else None

        def f(T, sd=sd, pf=psGU_free_dve, xr=xln_ready_dve, c_range=c_range, mark=mark):
            if c_range[0] == 0:
                if pf is not None:
                    T.wait_ge(SEM["dve"], pf)
                T.wait_ge(SEM["dve"], xr)
            last = None
            for c in range(*c_range):
                if sd["t"] == 0 and c % 4 == 0:
                    sname, v = sd["wgu_tgts"][0][c // 4]
                    T.wait_ge(SEM[sname], v)
                xs = xlnT_v[:, c, sd["t"] * 128:(sd["t"] + 1) * 128]
                ws = wgu_v[:, c, 0:I]
                T.matmul(psGU[:, 0:512], xs, ws[:, 0:512], start=(c == 0), stop=(c == NCI - 1))
                T.matmul(psGU[:, 512:1024], xs, ws[:, 512:1024], start=(c == 0), stop=(c == NCI - 1))
                last = T.matmul(psGU[:, 1024:1408], xs, ws[:, 1024:1408],
                                start=(c == 0), stop=(c == NCI - 1))
            if mark:
                last.then_inc(SEM["pe"], 1)
        em("tensor", f)
        if mark:
            sd["g"] = tgt

    def emit_silu(sd, gs_free_pe):
        tgt = post("act")

        def f(S, sd=sd, gf=gs_free_pe):
            S.wait_ge(SEM["pe"], sd["g"])
            if gf is not None:
                S.wait_ge(SEM["pe"], gf)
            S.activation(gs[:, sd["slot"] * I:(sd["slot"] + 1) * I], psGU[:, 0:1408],
                         AF.Silu).then_inc(SEM["act"], 1)
        em("scalar", f)
        sd["silu"] = tgt

    def emit_u(sd):
        tgt = post("pe")

        def f(T, sd=sd):
            T.wait_ge(SEM["act"], sd["silu"])
            last = None
            for c in range(NCI):
                if sd["t"] == 0 and c % 4 == 0:
                    sname, v = sd["wgu_tgts"][1][c // 4]
                    T.wait_ge(SEM[sname], v)
                xs = xlnT_v[:, c, sd["t"] * 128:(sd["t"] + 1) * 128]
                ws = wgu_v[:, c, I:2 * I]
                T.matmul(psGU[:, 0:512], xs, ws[:, 0:512], start=(c == 0), stop=(c == NCI - 1))
                T.matmul(psGU[:, 512:1024], xs, ws[:, 512:1024], start=(c == 0), stop=(c == NCI - 1))
                last = T.matmul(psGU[:, 1024:1408], xs, ws[:, 1024:1408],
                                start=(c == 0), stop=(c == NCI - 1))
            last.then_inc(SEM["pe"], 1)
        em("tensor", f)
        sd["u"] = tgt

    def emit_h(sd, cmb_ready_dve):
        tgt = post("dve")
        e, t = sd["e"], sd["t"]
        col = t * 16 + (e if e < 8 else 8)

        def f(V, sd=sd, col=col, cr=cmb_ready_dve):
            V.wait_ge(SEM["pe"], sd["u"])
            V.wait_ge(SEM["dve"], cr)
            if sd.get("_h_extra_wait"):
                V.wait_ge(SEM["gq"], sd["_h_extra_wait"])
            V.scalar_tensor_tensor(gs[:, sd["slot"] * I:(sd["slot"] + 1) * I],
                                   psGU[:, 0:1408], cmb[:, col:col + 1],
                                   gs[:, sd["slot"] * I:(sd["slot"] + 1) * I],
                                   ALU.mult, ALU.mult).then_inc(SEM["dve"], 1)
        em("vector", f)
        sd["h"] = tgt

    def emit_tr1(sd, psTr_free_dve):
        tgt = post("pe")

        def f(T, sd=sd, pf=psTr_free_dve):
            T.wait_ge(SEM["dve"], sd["h"])
            if sd.get("_dbg_h"):
                T.wait_ge(SEM["gq"], sd["_dbg_h"])
            if pf is not None:
                T.wait_ge(SEM["dve"], pf)
            last = None
            for c in range(8):
                last = T.transpose(psTr[:, c * 128:(c + 1) * 128],
                                   gs[:, sd["slot"] * I + c * 128:sd["slot"] * I + (c + 1) * 128],
                                   ident_bf[:, :])
            last.then_inc(SEM["pe"], 1)
        em("tensor", f)
        sd["tr1"] = tgt

    def emit_ev1(sd, hT_free_pe):
        tgt = post("dve")

        def f(V, sd=sd, hf=hT_free_pe):
            V.wait_ge(SEM["pe"], sd["tr1"])
            if hf is not None:
                V.wait_ge(SEM["pe"], hf)
            V.tensor_copy(hT[:, sd["slot"] * NIC * 128:sd["slot"] * NIC * 128 + 1024],
                          psTr[:, 0:1024]).then_inc(SEM["dve"], 1)
        em("vector", f)
        sd["ev1"] = tgt

    def emit_dnA(sd, psD_free_dve, psD_free_pe):
        tgt = post("pe")

        def f(T, sd=sd, pf=psD_free_dve, pp=psD_free_pe):
            T.wait_ge(SEM["dve"], sd["ev1"])
            if pf is not None:
                T.wait_ge(SEM["dve"], pf)
            for sname, v in sd["wd_tgts"][0:2]:
                T.wait_ge(SEM[sname], v)
            last = None
            base = sd["slot"] * NIC * 128
            for ic in range(6):
                hs = hT[:, base + ic * 128:base + (ic + 1) * 128]
                for qq in range(4):
                    last = T.matmul(psD[:, qq * 512:(qq + 1) * 512], hs,
                                    wdb_v[:, ic, qq * 512:(qq + 1) * 512],
                                    start=(ic == 0), stop=False)
            last.then_inc(SEM["pe"], 1)
        em("tensor", f)
        sd["dnA"] = tgt

    def emit_tr2(sd):
        tgt = post("pe")

        def f(T, sd=sd):
            T.wait_ge(SEM["dve"], sd["ev1"])
            last = None
            for c in range(8, NIC):
                last = T.transpose(psTr[:, (c - 8) * 128:(c - 7) * 128],
                                   gs[:, sd["slot"] * I + c * 128:sd["slot"] * I + (c + 1) * 128],
                                   ident_bf[:, :])
            last.then_inc(SEM["pe"], 1)
        em("tensor", f)
        sd["tr2"] = tgt

    def emit_ev2(sd):
        tgt = post("dve")

        def f(V, sd=sd):
            V.wait_ge(SEM["pe"], sd["tr2"])
            V.tensor_copy(hT[:, sd["slot"] * NIC * 128 + 1024:sd["slot"] * NIC * 128 + 1408],
                          psTr[:, 0:384]).then_inc(SEM["dve"], 1)
        em("vector", f)
        sd["ev2"] = tgt

    def emit_dnB(sd):
        tgt = post("pe")

        def f(T, sd=sd):
            T.wait_ge(SEM["dve"], sd["ev2"])
            sname, v = sd["wd_tgts"][2]
            T.wait_ge(SEM[sname], v)
            last = None
            base = sd["slot"] * NIC * 128
            for ic in range(6, NIC):
                hs = hT[:, base + ic * 128:base + (ic + 1) * 128]
                for qq in range(4):
                    last = T.matmul(psD[:, qq * 512:(qq + 1) * 512], hs,
                                    wdb_v[:, ic, qq * 512:(qq + 1) * 512],
                                    start=False, stop=(ic == NIC - 1))
            last.then_inc(SEM["pe"], 1)
        em("tensor", f)
        sd["dnB"] = tgt

    def emit_xevac(sd):
        tgt = post("dve")

        def f(V, sd=sd):
            V.wait_ge(SEM["pe"], sd["dnB"])
            V.tensor_tensor(xt[:, sd["t"] * D:(sd["t"] + 1) * D],
                            xt[:, sd["t"] * D:(sd["t"] + 1) * D],
                            psD[:, :], ALU.add).then_inc(SEM["dve"], 1)
        em("vector", f)
        sd["xev"] = tgt

    def emit_lagged(prev_sd, carry):
        emit_tr1(prev_sd, prev_sd["_psTr_free"])
        emit_ev1(prev_sd, prev_sd["_hT_free"])
        emit_dnA(prev_sd, prev_sd["_psD_free"], None)
        emit_tr2(prev_sd)
        emit_ev2(prev_sd)


    def emit_attention(carry):
        SQ = "sq"; GQ = "gq"
        if ATTN_STOP == "none":
            carry["rs_done_cc"] = cnt["cc"]
            carry["rs_out"] = x1s
            carry["rs_ready"] = ("cc", 0)
            carry["prev_sd"] = None
            carry["prev2_sd"] = None
            return carry
        wgu_flat = wgu[:, :]
        wdb_flat = wdb[:, :]
        xlnT_flat = xlnT[:, :]
        xl2T_v = xlnT_flat.rearrange("p (c n) -> p c n", c=NCI)
        for tg in range(4):
            post("s_x", 16)
            ls = carry.get("last_store_sq")
            lx = carry.get("last_xev_dve")

            def f_x(S, tg=tg, ls=ls):
                if ls is not None:
                    for sname, v in ls:
                        S.wait_ge(SEM[sname], v)
                if tg > 0:
                    S.wait_ge(SEM["dve"], carry["_ln2_done"][tg - 1])
                S.dma_start(out=xt[:, :].rearrange("p (t n) -> p t n", t=NT),
                            in_=x1s[tg * TPR:(tg + 1) * TPR, :]
                            .rearrange("(t p) n -> p t n", p=128)).then_inc(SEM["s_x"], 16)
            em("sync", f_x)
            x_ready = ("s_x", cnt["s_x"])
            stats1 = {}
            for t in range(NT):
                xt_t = xt[:, t * D:(t + 1) * D]
                mcol = sst[:, t * 8 + 0:t * 8 + 1]
                qcol = sst[:, t * 8 + 1:t * 8 + 2]
                tcol = sst[:, t * 8 + 2:t * 8 + 3]
                vcol = sst[:, t * 8 + 3:t * 8 + 4]
                q2col = sst[:, t * 8 + 6:t * 8 + 7]
                stats1[t] = post("dve", 2)

                def f(V, t=t, xt_t=xt_t, mcol=mcol, qcol=qcol, q2col=q2col,
                      x_ready=x_ready, lx=lx, tg=tg):
                    if t == 0:
                        V.wait_ge(SEM[x_ready[0]], x_ready[1])
                        if tg == 0 and lx is not None:
                            V.wait_ge(SEM["dve"], lx)
                        if tg > 0:
                            V.wait_ge(SEM["dve"], carry["_ln2_done"][tg - 1])
                            V.wait_ge(SEM["s_agx"], carry["_agx_st"][tg - 1])
                    V.reduce_sum(mcol, xt_t, axis=AX)
                    V.scalar_tensor_tensor(psGU[:, 0:1536], xt_t[:, 0:1536], 1.0, xt_t[:, 0:1536],
                                           ALU.mult, ALU.mult, accum_out=qcol)
                    V.scalar_tensor_tensor(gs[:, 0:512], xt_t[:, 1536:2048], 1.0,
                                           xt_t[:, 1536:2048], ALU.mult, ALU.mult,
                                           accum_out=q2col).then_inc(SEM["dve"], 2)
                em("vector", f)
                m2t = post("dve")

                def f_m2(V, mcol=mcol, tcol=tcol, qcol=qcol, q2col=q2col, s1=stats1[t]):
                    V.wait_ge(SEM["dve"], s1)
                    V.tensor_tensor(qcol, qcol, q2col, ALU.add)
                    V.scalar_tensor_tensor(tcol, mcol, 1.0 / (D * D), mcol, ALU.mult, ALU.mult
                                           ).then_inc(SEM["dve"], 1)
                em("vector", f_m2)
                stats1[t] = m2t
            var_done = {}
            for t in range(NT):
                qcol = sst[:, t * 8 + 1:t * 8 + 2]
                tcol = sst[:, t * 8 + 2:t * 8 + 3]
                vcol = sst[:, t * 8 + 3:t * 8 + 4]
                var_done[t] = post("dve")

                def f(V, t=t, qcol=qcol, tcol=tcol, vcol=vcol, stats1=stats1):
                    V.wait_ge(SEM["dve"], stats1[t])
                    V.scalar_tensor_tensor(vcol, qcol, 1.0 / D, tcol, ALU.mult, ALU.subtract
                                           ).then_inc(SEM["dve"], 1)
                em("vector", f)
            ln4b = post("act")
            rstd_done = post("act")

            def f_ln4b(S, var_done=var_done):
                S.wait_ge(SEM["dve"], var_done[NT - 1])
                last = None
                for t in range(NT):
                    last = S.activation(sst[:, t * 8 + 2:t * 8 + 3], sst[:, t * 8 + 3:t * 8 + 4],
                                        AF.Ln, bias=kconst[:, 0:1])
                last.then_inc(SEM["act"], 1)
            em("scalar", f_ln4b)

            def f_rstd(S, ln4b=ln4b):
                S.wait_ge(SEM["act"], ln4b)
                last = None
                for t in range(NT):
                    last = S.activation(sst[:, t * 8 + 4:t * 8 + 5], sst[:, t * 8 + 2:t * 8 + 3],
                                        AF.Exp, scale=-0.5)
                last.then_inc(SEM["act"], 1)
            em("scalar", f_rstd)
            negm_done = post("dve")

            def f_negm(V, rstd_done=rstd_done):
                V.wait_ge(SEM["act"], rstd_done)
                last = None
                for t in range(NT):
                    last = V.tensor_scalar_mul(sst[:, t * 8 + 5:t * 8 + 6],
                                               sst[:, t * 8 + 0:t * 8 + 1], -1.0 / D)
                last.then_inc(SEM["dve"], 1)
            em("vector", f_negm)
            if ATTN_STOP == "b1stats":
                carry.setdefault("_ln2_done", {})[tg] = cnt["dve"]
                carry.setdefault("_agx_st", {})[tg] = cnt["s_agx"]
                continue
            evk_all = None
            prev_tile_ev = None
            for t in range(NT):
                ncol = sst[:, t * 8 + 5:t * 8 + 6]
                rcol = sst[:, t * 8 + 4:t * 8 + 5]
                vck, trk, evk = {}, {}, {}
                for c in range(NCI):
                    slot = c % 2
                    b3 = (c % 3) * 512
                    vck[c] = post("dve")

                    def f_vc(V, c=c, t=t, slot=slot, ncol=ncol, rcol=rcol, trk=trk,
                             negm_done=negm_done):
                        if c == 0:
                            V.wait_ge(SEM["dve"], negm_done)
                        V.tensor_scalar(xf32[:, slot * 128:(slot + 1) * 128],
                                        xt[:, t * D + c * 128:t * D + (c + 1) * 128],
                                        ncol, rcol, ALU.add, ALU.mult).then_inc(SEM["dve"], 1)
                    em("vector", f_vc)
                    trk[c] = post("pe")

                    def f_tc(T, c=c, slot=slot, b3=b3, vck=vck, evk=evk, t=t, tg=tg):
                        T.wait_ge(SEM["dve"], vck[c])
                        if c >= 3:
                            T.wait_ge(SEM["dve"], evk[c - 3])
                        elif c < 3 and t == 0:
                            pass
                        T.transpose(psD[:, b3:b3 + 128], xf32[:, slot * 128:(slot + 1) * 128],
                                    ident_f[:, :]).then_inc(SEM["pe"], 1)
                    em("tensor", f_tc)
                    evk[c] = post("dve")

                    def f_ec(V, c=c, t=t, slot=slot, b3=b3, trk=trk):
                        V.wait_ge(SEM["pe"], trk[c])
                        V.tensor_copy(xl2T_v[:, c, t * 128:(t + 1) * 128],
                                      psD[:, b3:b3 + 128]).then_inc(SEM["dve"], 1)
                    em("vector", f_ec)
                evk_all = evk[NCI - 1]
                prev_tile_ev = evk[NCI - 1]
            carry.setdefault("_ln2_done", {})[tg] = evk_all
            if ATTN_STOP == "b1nostore":
                carry.setdefault("_agx_st", {})[tg] = cnt["s_agx"]
                continue
            post("s_agx", 16)

            def f_st(S, tg=tg, evk_all=evk_all):
                S.wait_ge(SEM["dve"], evk_all)
                S.dma_start(out=agx_in.rearrange("(c p) n -> p c n", p=128)[:, :, tg * TPR:(tg + 1) * TPR],
                            in_=xl2T_v).then_inc(SEM["s_agx"], 16)
            em("sync", f_st)
            carry.setdefault("_agx_st", {})[tg] = cnt["s_agx"]

        if ATTN_STOP in ("b1", "b1stats", "b1nostore"):
            carry["rs_done_cc"] = cnt["cc"]
            carry["rs_out"] = x1s
            carry["last_store_sq"] = [("s_agx", cnt["s_agx"])]
            carry["last_xev_dve"] = cnt["dve"]
            carry["prev_sd"] = None
            carry["prev2_sd"] = None
            return carry
        post("cc", 1)
        ag_done = cnt["cc"]
        agx_st_last = carry["_agx_st"][3]

        def f_ag(G, agx_st_last=agx_st_last):
            G.wait_ge(SEM["s_agx"], agx_st_last)
            G.collective_compute("AllGather", ALU.bypass,
                                 replica_groups=[[0, 1], [2, 3], [4, 5], [6, 7]],
                                 ins=[agx_in.opt()], outs=[agx_out.opt()]).then_inc(SEM["cc"], 1)
        em("gpsimd", f_ag)

        if ATTN_STOP == "b1ag":
            carry["rs_done_cc"] = ag_done
            carry["rs_out"] = x1s
            carry["rs_ready"] = ("cc", 0)
            carry["last_store_sq"] = [("s_agx", cnt["s_agx"])]
            carry["last_xev_dve"] = cnt["dve"]
            carry["prev_sd"] = None
            carry["prev2_sd"] = None
            return carry
        wnat = wgu_flat[:, 0:16384].rearrange("p (c n) -> p c n", c=8)
        wonat = wgu_flat[:, 16384:32768].rearrange("p (c n) -> p c n", c=16)
        wstage = wdb_flat[:, 0:1024]
        wl = {}
        for mi, (wsrc, rows) in enumerate([(wq, 1024), (wk, 1024), (wv, 1024), (wo, D)]):
            post("g_attw", 16)

            def f_wl(G, mi=mi, wsrc=wsrc, rows=rows):
                if mi > 0:
                    G.wait_ge(SEM["pe"], wl[mi - 1][1])
                if mi == 0 and carry.get("_wT_wait_pe") is not None:
                    G.wait_ge(SEM["pe"], carry["_wT_wait_pe"])
                if rows == 1024:
                    G.dma_start(out=wnat, in_=wsrc.rearrange("(c p) n -> p c n", p=128)
                                ).then_inc(SEM["g_attw"], 16)
                else:
                    G.dma_start(out=wonat, in_=wsrc.rearrange("(c p) n -> p c n", p=128)
                                ).then_inc(SEM["g_attw"], 16)
            em("gpsimd", f_wl)
            ld = cnt["g_attw"]
            if mi < 3:
                dst = [wqT_s, wkT_s, wvT_s][mi]
                last_pe = None
                for c in range(NCI):
                    tpe = post("pe")

                    def f_tr(T, mi=mi, c=c, ld=ld, tpe=tpe,
                             pv=wl.get((mi - 1, 15)) if mi >= 1 else None):
                        if c == 0:
                            T.wait_ge(SEM["g_attw"], ld)
                            if pv is not None:
                                T.wait_ge(SEM["dve"], pv)
                        if c >= 1:
                            T.wait_ge(SEM["dve"], wl[(mi, c - 1)])
                        last = None
                        for hc in range(8):
                            last = T.transpose(psTr[:, hc * 128:(hc + 1) * 128],
                                               wnat[:, hc, c * 128:(c + 1) * 128], ident_bf[:, :])
                        last.then_inc(SEM["pe"], 1)
                    em("tensor", f_tr)
                    wl[(mi, c)] = post("dve")

                    def f_ev(V, tpe=tpe, pst=wl.get(("st", mi, c - 1)) if c > 0 else
                             (wl.get(("st", mi - 1, 15)) if mi > 0 else None)):
                        V.wait_ge(SEM["pe"], tpe)
                        if pst is not None:
                            V.wait_ge(SEM["s_wT"], pst)
                        V.tensor_copy(wstage, psTr[:, 0:1024]).then_inc(SEM["dve"], 1)
                    em("vector", f_ev)
                    post("s_wT", 16)

                    def f_sd(S, dst=dst, c=c, ev=wl[(mi, c)]):
                        S.wait_ge(SEM["dve"], ev)
                        S.dma_start(out=dst[c * 128:(c + 1) * 128, :], in_=wstage
                                    ).then_inc(SEM["s_wT"], 16)
                    em("sync", f_sd)
                    wl[("st", mi, c)] = cnt["s_wT"]
                    last_pe = tpe
                wl[mi] = (ld, last_pe)
            else:
                last_pe = None
                wost = wdb_flat[:, 1024:3072]
                for jc in range(8):
                    tpe = post("pe")

                    def f_tr(T, jc=jc, ld=ld, tpe=tpe,
                             pv=wl.get((2, 15))):
                        if jc == 0:
                            T.wait_ge(SEM["g_attw"], ld)
                            if pv is not None:
                                T.wait_ge(SEM["dve"], pv)
                        if jc >= 1:
                            T.wait_ge(SEM["dve"], wl[(3, jc - 1, 1)])
                        last = None
                        for oc in range(8):
                            last = T.transpose(psTr[:, oc * 128:(oc + 1) * 128],
                                               wonat[:, oc, jc * 128:(jc + 1) * 128], ident_bf[:, :])
                        last.then_inc(SEM["pe"], 1)
                    em("tensor", f_tr)
                    e1 = post("dve")
                    wl[(3, jc, 0)] = e1

                    def f_ev(V, tpe=tpe, jc=jc, pst=wl.get(("sto", jc - 1))):
                        V.wait_ge(SEM["pe"], tpe)
                        if pst is not None:
                            V.wait_ge(SEM["s_wT"], pst)
                        V.tensor_copy(wost[:, 0:1024], psTr[:, 0:1024]).then_inc(SEM["dve"], 1)
                    em("vector", f_ev)
                    tpe2 = post("pe")

                    def f_tr2(T, jc=jc, tpe2=tpe2, e1=e1):
                        T.wait_ge(SEM["dve"], e1)
                        last = None
                        for oc in range(8, 16):
                            last = T.transpose(psTr[:, (oc - 8) * 128:(oc - 7) * 128],
                                               wonat[:, oc, jc * 128:(jc + 1) * 128], ident_bf[:, :])
                        last.then_inc(SEM["pe"], 1)
                    em("tensor", f_tr2)
                    e2 = post("dve")
                    wl[(3, jc, 1)] = e2

                    def f_ev2(V, tpe2=tpe2):
                        V.wait_ge(SEM["pe"], tpe2)
                        V.tensor_copy(wost[:, 1024:2048], psTr[:, 0:1024]).then_inc(SEM["dve"], 1)
                    em("vector", f_ev2)
                    post("s_wT", 16)

                    def f_sd(S, jc=jc, e2=e2):
                        S.wait_ge(SEM["dve"], e2)
                        S.dma_start(out=woT_s[jc * 128:(jc + 1) * 128, :], in_=wost
                                    ).then_inc(SEM["s_wT"], 16)
                    em("sync", f_sd)
                    wl[("sto", jc)] = cnt["s_wT"]
                    last_pe = tpe2
                wl[3] = (ld, last_pe)
        wT_done_sq = cnt["s_wT"]

        slab = wgu_flat[:, 0:32768].rearrange("p (c n) -> p c n", c=NCI)
        wsl = wdb_flat[:, 0:4096].rearrange("p (s n) -> p s n", s=2)
        qstg = wdb_flat[:, 4096:8192].rearrange("p (s n) -> p s n", s=2)
        b3_store = {}
        prev_ev = None
        prev_mm = None
        for tb in range(2):
            post("s_slab", 16)

            def f_sl(S, tb=tb, prev_mm=prev_mm):
                S.wait_ge(SEM["cc"], ag_done)
                S.wait_ge(SEM["s_wT"], wT_done_sq)
                if prev_mm is not None:
                    S.wait_ge(SEM["pe"], prev_mm)
                S.dma_start(out=slab,
                            in_=agx_out[tb * D:(tb + 1) * D, :].rearrange("(c p) n -> p c n", p=128)
                            ).then_inc(SEM["s_slab"], 16)
            em("sync", f_sl)
            slab_ld = cnt["s_slab"]
            for h in range(8):
                for xi, (wTs, dstq) in enumerate([(wqT_s, qT_sd), (wkT_s, kT_sd), (wvT_s, vT_sd)]):
                    si = (h * 3 + xi) % 2
                    ldsem = "s_ld3a" if si == 0 else "s_ld3b"
                    post(ldsem, 16)

                    def f_wv(S, wTs=wTs, h=h, si=si, ldsem=ldsem,
                             g=b3_store.get(("mmslot", si))):
                        if g is not None:
                            S.wait_ge(SEM["pe"], g)
                        S.dma_start(out=wsl[:, si, :].rearrange("p (c n) -> p c n", c=NCI),
                                    in_=wTs[:, h * 128:(h + 1) * 128]
                                    .rearrange("(c p) n -> p c n", p=128)
                                    ).then_inc(SEM[ldsem], 16)
                    em("sync", f_wv)
                    wld = (ldsem, cnt[ldsem])
                    mm = post("pe")

                    def f_mm(T, si=si, wld=wld, slab_ld=slab_ld, mm=mm, prev_ev=prev_ev):
                        T.wait_ge(SEM[wld[0]], wld[1])
                        T.wait_ge(SEM["s_slab"], slab_ld)
                        if prev_ev is not None:
                            T.wait_ge(SEM["dve"], prev_ev)
                        last = None
                        for c in range(NCI):
                            ws = wsl[:, si, c * 128:(c + 1) * 128]
                            for qq in range(4):
                                last = T.matmul(psD[:, qq * 512:(qq + 1) * 512], ws,
                                                slab[:, c, qq * 512:(qq + 1) * 512],
                                                start=(c == 0), stop=(c == NCI - 1))
                        last.then_inc(SEM["pe"], 1)
                    em("tensor", f_mm)
                    b3_store[("mmslot", si)] = mm
                    ev = post("dve")

                    def f_ev(V, mm=mm, si=si, pst=b3_store.get(("stslot", si))):
                        V.wait_ge(SEM["pe"], mm)
                        if pst is not None:
                            V.wait_ge(SEM["s_qkva" if si == 0 else "s_qkvb"], pst)
                        V.tensor_copy(qstg[:, si, :], psD[:, :]).then_inc(SEM["dve"], 1)
                    em("vector", f_ev)
                    prev_ev = ev
                    prev_mm = mm
                    qsem = "s_qkva" if si == 0 else "s_qkvb"
                    post(qsem, 16)

                    def f_qs(S, dstq=dstq, h=h, tb=tb, ev=ev, si=si, qsem=qsem):
                        S.wait_ge(SEM["dve"], ev)
                        S.dma_start(out=dstq[h * 128:(h + 1) * 128, tb * D:(tb + 1) * D],
                                    in_=qstg[:, si, :]).then_inc(SEM[qsem], 16)
                    em("sync", f_qs)
                    b3_store[(tb, h, xi)] = cnt[qsem]
                    b3_store[("stslot", si)] = cnt[qsem]
        qkv_done_sq = (cnt["s_qkva"], cnt["s_qkvb"])
        if ATTN_STOP == "b3":
            post("g_acc", 16)

            def f_dummy(G, q=qkv_done_sq):
                G.wait_ge(SEM["s_qkva"], q[0])
                G.wait_ge(SEM["s_qkvb"], q[1])
                G.dma_start(out=rs_out, in_=x1s).then_inc(SEM["g_acc"], 16)
            em("gpsimd", f_dummy)
            carry["rs_done_cc"] = cnt["cc"]
            carry["rs_ready"] = ("g_acc", cnt["g_acc"])
            carry["rs_out"] = rs_out
            carry["last_store_sq"] = [("s_agx", cnt["s_agx"])]
            carry["last_xev_dve"] = cnt["dve"]
            carry["prev_sd"] = None
            carry["prev2_sd"] = None
            return carry

        last_zT_ev_holder = {"v": None, "zst": None}
        qTb = wdb_flat[:, 0:4096]
        kTb = wdb_flat[:, 4096:8192]
        vTb = wdb_flat[:, 8192:12288]
        vb = wgu_flat[:, 0:4096].rearrange("p (c n) -> p c n", c=32)
        PTb = wgu_flat[:, 4096:6144]
        zstg = wgu_flat[:, 38912:39936]
        P_pool = [[wgu_flat[:, 6144 + (par * 4 + i) * 4096: 6144 + (par * 4 + i + 1) * 4096]
                   for i in range(4)] for par in range(2)]
        psS = [psGU[:, 0:1024], psD[:, 0:1024], psD[:, 1024:2048]]
        psO = psGU[:, 1024:1536]
        C = lambda i: sst[:, i:i + 1]

        def emit_av(h, qg, p_done, P_bufs):
            nk = 4 * qg + 4
            npair = nk // 2
            av_mm = {}
            tr_t = {}
            ev_t = {}
            for pj in range(npair):
                sl = pj % 2
                tr_t[pj] = post("pe")

                def f_tr(T, pj=pj, p_done=p_done, P_bufs=P_bufs, ev_t=ev_t):
                    if pj == 0:
                        for pf in p_done:
                            T.wait_ge(SEM["dve"], pf)
                    if pj >= 1:
                        T.wait_ge(SEM["act"], ev_t[pj - 1])
                    last = None
                    for half in range(2):
                        kc = pj * 2 + half
                        for i in range(4):
                            last = T.transpose(
                                psTr[:, half * 512 + i * 128:half * 512 + (i + 1) * 128],
                                P_bufs[i][:, kc * 128:(kc + 1) * 128], ident_bf[:, :])
                    last.then_inc(SEM["pe"], 1)
                em("tensor", f_tr)
                ev_t[pj] = post("act")

                def f_ev(S, pj=pj, sl=sl, tr=tr_t[pj], av_mm=av_mm):
                    S.wait_ge(SEM["pe"], tr)
                    pv = av_mm.get(pj - 2)
                    if pv is not None:
                        S.wait_ge(SEM["pe"], pv)
                    S.copy(PTb[:, sl * 1024:(sl + 1) * 1024],
                           psTr[:, 0:1024]).then_inc(SEM["act"], 1)
                em("scalar", f_ev)
                av_mm[pj] = post("pe")

                def f_av(T, pj=pj, sl=sl, npair=npair, ev=ev_t[pj],
                         lzv=last_zT_ev_holder["v"]):
                    T.wait_ge(SEM["act"], ev)
                    if pj == 0 and lzv is not None:
                        T.wait_ge(SEM["dve"], lzv)
                    last = None
                    for half in range(2):
                        kc = pj * 2 + half
                        last = T.matmul(psO, vb[:, kc, :],
                                        PTb[:, sl * 1024 + half * 512:sl * 1024 + (half + 1) * 512],
                                        start=(kc == 0), stop=(kc == nk - 1))
                    last.then_inc(SEM["pe"], 1)
                em("tensor", f_av)
            zev = post("dve")
            zsl = qg % 2

            def f_z(V, qg=qg, zsl=zsl, mm=av_mm[npair - 1], pzs=last_zT_ev_holder["zst"]):
                V.wait_ge(SEM["pe"], mm)
                if pzs is not None:
                    V.wait_ge(SEM["s_zt"], pzs)
                V.tensor_copy(zstg[:, zsl * 512:(zsl + 1) * 512], psO).then_inc(SEM["dve"], 1)
            em("vector", f_z)
            post("s_zt", 16)

            def f_zs(S, h=h, qg=qg, zsl=zsl, zev=zev):
                S.wait_ge(SEM["dve"], zev)
                S.dma_start(out=zT_sd[h * 128:(h + 1) * 128, qg * 512:(qg + 1) * 512],
                            in_=zstg[:, zsl * 512:(zsl + 1) * 512]).then_inc(SEM["s_zt"], 16)
            em("sync", f_zs)
            last_zT_ev_holder["v"] = zev
            last_zT_ev_holder["zst"] = cnt["s_zt"]
            return tr_t[npair - 1]

        last_head_S_pe = None
        prev_head_vdve = None
        gk_hist = {}
        gk = 0
        P_free = {0: None, 1: None}
        final_zst = None
        for h in range(8):
            post("s_hld", 16 * 3)
            prev_S = last_head_S_pe

            def f_ld(S, h=h, prev_S=prev_S, pvd=prev_head_vdve):
                if h == 0:
                    S.wait_ge(SEM["s_qkva"], qkv_done_sq[0])
                    S.wait_ge(SEM["s_qkvb"], qkv_done_sq[1])
                else:
                    S.wait_ge(SEM["pe"], prev_S)
                    S.wait_ge(SEM["dve"], pvd)
                S.dma_start(out=qTb, in_=qT_sd[h * 128:(h + 1) * 128, :]).then_inc(SEM["s_hld"], 16)
                S.dma_start(out=kTb, in_=kT_sd[h * 128:(h + 1) * 128, :]).then_inc(SEM["s_hld"], 16)
                S.dma_start(out=vTb, in_=vT_sd[h * 128:(h + 1) * 128, :]).then_inc(SEM["s_hld"], 16)
            em("sync", f_ld)
            ld_t = cnt["s_hld"]
            vtr = {}
            for b4 in range(4):
                tpe = post("pe")

                def f_vt(T, b4=b4, ld_t=ld_t, h=h, vtr=vtr, lz=last_zT_ev_holder["v"]):
                    if b4 == 0:
                        T.wait_ge(SEM["s_hld"], ld_t)
                        if lz is not None:
                            T.wait_ge(SEM["dve"], lz)
                    if b4 >= 1:
                        T.wait_ge(SEM["dve"], vtr[b4 - 1])
                    last = None
                    for j in range(8):
                        kc = b4 * 8 + j
                        last = T.transpose(psTr[:, j * 128:(j + 1) * 128],
                                           vTb[:, kc * 128:(kc + 1) * 128], ident_bf[:, :])
                    last.then_inc(SEM["pe"], 1)
                em("tensor", f_vt)
                vtr[b4] = post("dve")

                def f_ve(V, b4=b4, tpe=tpe):
                    V.wait_ge(SEM["pe"], tpe)
                    V.tensor_copy(wgu_flat[:, b4 * 1024:(b4 + 1) * 1024],
                                  psTr[:, 0:1024]).then_inc(SEM["dve"], 1)
                em("vector", f_ve)
            v_ready = vtr[3]
            prev_head_vdve = v_ready

            pend = None
            for qg in range(8):
                par = qg % 2
                P_bufs = P_pool[par]
                p_done = []
                for qt4 in range(4):
                    qt = qg * 4 + qt4
                    klen = 128 * (qt + 1)
                    kgs = (klen + 1023) // 1024
                    Pb = P_bufs[qt4]
                    smms = {}
                    exp_t = {}
                    for kg in range(kgs):
                        w = min(1024, klen - kg * 1024)
                        sset = gk % 3
                        prev_exp = gk_hist.get(gk - 3)
                        smms[kg] = post("pe")

                        def f_s(T, qt=qt, kg=kg, w=w, sset=sset, v_ready=v_ready,
                                prev_exp=prev_exp):
                            if qt == 0 and kg == 0:
                                T.wait_ge(SEM["dve"], v_ready)
                            if prev_exp is not None:
                                T.wait_ge(SEM["act"], prev_exp)
                            last = None
                            for off in range(0, w, 512):
                                ww = min(512, w - off)
                                last = T.matmul(psS[sset][:, off:off + ww],
                                                qTb[:, qt * 128:(qt + 1) * 128],
                                                kTb[:, kg * 1024 + off:kg * 1024 + off + ww],
                                                start=True, stop=True)
                            last.then_inc(SEM["pe"], 1)
                        em("tensor", f_s)
                        mx_d = post("dve")

                        def f_mx(V, kg=kg, w=w, sset=sset, smm=smms[kg]):
                            V.wait_ge(SEM["pe"], smm)
                            V.reduce_max(C(kg), psS[sset][:, 0:w], axis=AX
                                         ).then_inc(SEM["dve"], 1)
                        em("vector", f_mx)
                        mg_d = post("dve")

                        def f_ng(V, kg=kg, mx_d=mx_d):
                            V.wait_ge(SEM["dve"], mx_d)
                            V.tensor_scalar_mul(C(4 + kg), C(kg), -0.08838834764831845
                                                ).then_inc(SEM["dve"], 1)
                        em("vector", f_ng)
                        exp_t[kg] = post("act")
                        pfree = P_free[par]

                        def f_ex(S, kg=kg, w=w, sset=sset, mg_d=mg_d, Pb=Pb, pfree=pfree):
                            S.wait_ge(SEM["dve"], mg_d)
                            if kg == 0 and pfree is not None:
                                S.wait_ge(SEM["pe"], pfree)
                            S.activation(Pb[:, kg * 1024:kg * 1024 + w], psS[sset][:, 0:w],
                                         AF.Exp, bias=C(4 + kg), scale=0.08838834764831845,
                                         accum_out=C(8 + kg)).then_inc(SEM["act"], 1)
                        em("scalar", f_ex)
                        gk_hist[gk] = exp_t[kg]
                        gk += 1
                    dk = kgs - 1
                    dw = klen - dk * 1024
                    mk_d = post("dve")

                    def f_msk(V, qt=qt, Pb=Pb, et=exp_t[dk]):
                        V.wait_ge(SEM["act"], et)
                        V.tensor_tensor(Pb[:, qt * 128:qt * 128 + 128],
                                        Pb[:, qt * 128:qt * 128 + 128], tri01[:, :], ALU.mult
                                        ).then_inc(SEM["dve"], 1)
                    em("vector", f_msk)
                    post("dve")

                    def f_dsum(V, dk=dk, dw=dw, Pb=Pb, mk_d=mk_d):
                        V.wait_ge(SEM["dve"], mk_d)
                        V.reduce_sum(C(8 + dk), Pb[:, dk * 1024:dk * 1024 + dw], axis=AX
                                     ).then_inc(SEM["dve"], 1)
                    em("vector", f_dsum)

                    gmax = 128 * (4 * qg + 4)
                    if klen < gmax:
                        post("dve")

                        def f_zz(V, Pb=Pb, klen=klen, gmax=gmax, prev_tgt=cnt["dve"] - 1):
                            V.wait_ge(SEM["dve"], prev_tgt)
                            V.memset(Pb[:, klen:gmax], 0.0).then_inc(SEM["dve"], 1)
                        em("vector", f_zz)

                    def em_v2(fn, wait_act=None):
                        tgt = post("dve")

                        def f(V, fn=fn, wait_act=wait_act, prev_tgt=tgt - 1):
                            V.wait_ge(SEM["dve"], prev_tgt)
                            if wait_act is not None:
                                V.wait_ge(SEM["act"], wait_act)
                            fn(V).then_inc(SEM["dve"], 1)
                        em("vector", f)
                        return tgt

                    if kgs == 1:
                        em_v2(lambda V: V.reciprocal(C(22), C(8)))
                        pf = em_v2(lambda V, Pb=Pb, klen=klen:
                                   V.tensor_scalar(Pb[:, 0:klen], Pb[:, 0:klen], C(22), None,
                                                   ALU.mult))
                    else:
                        em_v2(lambda V, kgs=kgs: V.reduce_max(C(20), sst[:, 0:kgs], axis=AX))
                        for kg in range(kgs):
                            em_v2(lambda V, kg=kg: V.tensor_tensor(C(12 + kg), C(kg), C(20),
                                                                   ALU.subtract))
                        am_t = post("act")
                        amw = cnt["dve"]

                        def f_am(S, kgs=kgs, amw=amw):
                            S.wait_ge(SEM["dve"], amw)
                            last = None
                            for kg in range(kgs):
                                last = S.activation(C(12 + kg), C(12 + kg), AF.Exp,
                                                    scale=0.08838834764831845)
                            last.then_inc(SEM["act"], 1)
                        em("scalar", f_am)
                        em_v2(lambda V: V.tensor_tensor(C(21), C(12), C(8), ALU.mult),
                              wait_act=am_t)
                        for kg in range(1, kgs):
                            em_v2(lambda V, kg=kg: V.tensor_tensor(C(23), C(12 + kg),
                                                                   C(8 + kg), ALU.mult))
                            em_v2(lambda V: V.tensor_tensor(C(21), C(21), C(23), ALU.add))
                        em_v2(lambda V: V.reciprocal(C(22), C(21)))
                        pf = None
                        for kg in range(kgs):
                            w = min(1024, klen - kg * 1024)
                            em_v2(lambda V, kg=kg: V.tensor_tensor(C(16 + kg), C(12 + kg),
                                                                   C(22), ALU.mult))
                            pf = em_v2(lambda V, kg=kg, w=w, Pb=Pb:
                                       V.tensor_scalar(Pb[:, kg * 1024:kg * 1024 + w],
                                                       Pb[:, kg * 1024:kg * 1024 + w],
                                                       C(16 + kg), None, ALU.mult))
                    p_done.append(pf)
                if pend is not None:
                    P_free[pend[0] % 2] = emit_av(h, pend[0], pend[1], P_pool[pend[0] % 2])
                pend = (qg, p_done)
            P_free[pend[0] % 2] = emit_av(h, pend[0], pend[1], P_pool[pend[0] % 2])
            pend = None
            last_head_S_pe = smms[kgs - 1]
        attn_done_dve = last_zT_ev_holder["v"]
        zT_done_sq = last_zT_ev_holder["zst"]

        if ATTN_STOP == "b4":
            post("g_acc", 16)

            def f_dummy(G, q=last_zT_ev_holder["zst"]):
                G.wait_ge(SEM["s_zt"], q)
                G.dma_start(out=rs_out, in_=x1s).then_inc(SEM["g_acc"], 16)
            em("gpsimd", f_dummy)
            carry["rs_done_cc"] = cnt["cc"]
            carry["rs_ready"] = ("g_acc", cnt["g_acc"])
            carry["rs_out"] = rs_out
            carry["last_store_sq"] = [("s_agx", cnt["s_agx"])]
            carry["last_xev_dve"] = cnt["dve"]
            carry["prev_sd"] = None
            carry["prev2_sd"] = None
            return carry

        woT_res = wdb_flat[:, 0:16384].rearrange("p (z n) -> p z n", z=8)
        zT_v = wgu_flat[:, 6144:38912].rearrange("p (h n) -> p h n", h=8)
        post("s_hld", 32)

        def f_wo(S, attn_done_dve=attn_done_dve, lsp=last_head_S_pe, zds=zT_done_sq):
            S.wait_ge(SEM["dve"], attn_done_dve)
            S.wait_ge(SEM["pe"], lsp)
            S.wait_ge(SEM["s_zt"], zds)
            S.dma_start(out=woT_res,
                        in_=woT_s.rearrange("(z p) n -> p z n", p=128)).then_inc(SEM["s_hld"], 16)
            S.dma_start(out=zT_v,
                        in_=zT_sd.rearrange("(h p) n -> p h n", p=128)).then_inc(SEM["s_hld"], 16)
        em("sync", f_wo)
        wo_ld = cnt["s_hld"]
        rs_stage = xt[:, :].rearrange("p (s n) -> p s n", s=NT)
        prev_ev5 = {}
        last_rs_sq = None
        for tt in range(32):
            sl = tt % 2
            mm5 = post("pe")

            def f_m5(T, tt=tt, sl=sl, wo_ld=wo_ld, prev_ev5=prev_ev5):
                if tt == 0:
                    T.wait_ge(SEM["s_hld"], wo_ld)
                pv = prev_ev5.get(tt - 1)
                if pv is not None:
                    T.wait_ge(SEM["dve"], pv)
                last = None
                for zc in range(8):
                    zs = zT_v[:, zc, tt * 128:(tt + 1) * 128]
                    for qq in range(4):
                        last = T.matmul(psD[:, qq * 512:(qq + 1) * 512], zs,
                                        woT_res[:, zc, qq * 512:(qq + 1) * 512],
                                        start=(zc == 0), stop=(zc == 7))
                last.then_inc(SEM["pe"], 1)
            em("tensor", f_m5)
            ev5 = post("dve")

            def f_e5(V, tt=tt, sl=sl, mm5=mm5):
                V.wait_ge(SEM["pe"], mm5)
                if tt >= 2:
                    V.wait_ge(SEM[f"s_rs{sl}"], carry["_rs_st"][tt - 2])
                V.tensor_copy(rs_stage[:, sl, :], psD[:, :]).then_inc(SEM["dve"], 1)
            em("vector", f_e5)
            prev_ev5[tt] = ev5
            post(f"s_rs{sl}", 16)

            def f_s5(S, tt=tt, sl=sl, ev5=ev5):
                S.wait_ge(SEM["dve"], ev5)
                S.dma_start(out=rs_in[tt * 128:(tt + 1) * 128, :], in_=rs_stage[:, sl, :]
                            ).then_inc(SEM[f"s_rs{sl}"], 16)
            em("sync", f_s5)
            carry.setdefault("_rs_st", {})[tt] = cnt[f"s_rs{sl}"]
            last_rs_sq = (cnt["s_rs0"], cnt["s_rs1"])

        post("cc", 1)
        rs_done = cnt["cc"]

        def f_rs(G, last_rs_sq=last_rs_sq):
            G.wait_ge(SEM["s_rs0"], last_rs_sq[0])
            G.wait_ge(SEM["s_rs1"], last_rs_sq[1])
            G.collective_compute("ReduceScatter", ALU.add,
                                 replica_groups=[[0, 1], [2, 3], [4, 5], [6, 7]],
                                 ins=[rs_in.opt()], outs=[rs_out.opt()]).then_inc(SEM["cc"], 1)
        em("gpsimd", f_rs)
        carry["rs_done_cc"] = rs_done
        carry["rs_ready"] = ("cc", rs_done)
        carry["rs_out"] = rs_out
        carry["last_xev_dve"] = prev_ev5[31]
        carry["last_store_sq"] = [("s_rs0", cnt["s_rs0"]), ("s_rs1", cnt["s_rs1"])]
        carry["prev_sd"] = None
        carry["prev2_sd"] = None
        return carry

    def emit_ffn(ffn_idx, dst, carry):
        for r in range(NR):
            use_cache = not (ffn_idx == 0 and r == 0)
            post("s_x", 16)
            ls = carry.get("last_store_sq")

            def f_x(S, r=r, ls=ls, ffn_idx=ffn_idx):
                if ls is not None:
                    for sname, v in ls:
                        S.wait_ge(SEM[sname], v)
                src = x_in if ffn_idx == 0 else x1s
                S.dma_start(out=xt[:, :].rearrange("p (t n) -> p t n", t=NT),
                            in_=src[r * TPR:(r + 1) * TPR, :].rearrange("(t p) n -> p t n", p=128)
                            ).then_inc(SEM["s_x"], 16)
            em("sync", f_x)
            x_ready = ("s_x", cnt["s_x"])
            if ffn_idx == 1:
                post("g_acc", 16)
                xr = x_ready

                def f_x2(G, r=r, xr=xr, rsr=carry.get("rs_ready")):
                    G.wait_ge(SEM[xr[0]], xr[1])
                    if rsr is not None and rsr[1]:
                        G.wait_ge(SEM[rsr[0]], rsr[1])
                    G.dma_start(out=xt[:, :].rearrange("p (t n) -> p t n", t=NT),
                                in_=carry["rs_out"][r * TPR:(r + 1) * TPR, :]
                                .rearrange("(t p) n -> p t n", p=128),
                                accum_op=ALU.add).then_inc(SEM["g_acc"], 16)
                em("gpsimd", f_x2)
                x_ready = ("g_acc", cnt["g_acc"])

            if ffn_idx == 0 and r == 0:
                pf_dve, pf_pe = gwT_ready_dve, None
            else:
                pf_dve, pf_pe = carry["last_xev_dve"], None
            cmb_ready, lg_free = emit_ln_round(ffn_idx, r, x_ready, pf_dve, pf_pe)
            if DEBUG and ffn_idx == 0 and r == 0:
                emit_dbg_snapshot("dve", cmb_ready, dbg_xlnT, xlnT[:, :])
                emit_dbg_snapshot("dve", cmb_ready, dbg_lgb, lgb[:, :])
                emit_dbg_snapshot("dve", cmb_ready, dbg_cmb, cmb[:, :])

            for e in range(NEXP):
                key = (ffn_idx, r, e)
                wgu_tgts = carry["wgu_tgts"][key]
                nf, nr_, ne = ffn_idx, r, e + 1
                if ne == NEXP:
                    ne, nr_ = 0, r + 1
                    if nr_ == NR:
                        nr_, nf = 0, ffn_idx + 1
                nkey = (nf, nr_, ne)
                for t in range(NT):
                    step = carry["step"]
                    sd = dict(e=e, t=t, slot=step % 2, wgu_tgts=wgu_tgts)
                    prev_sd = carry.get("prev_sd")
                    prev2_sd = carry.get("prev2_sd")
                    flushed = prev_sd is None or prev_sd.get("_emitted_lag", False)

                    emit_g(sd, prev_sd["h"] if prev_sd else None, cmb_ready, (0, NCI - 1))
                    if not flushed:
                        emit_tr1(prev_sd, prev_sd["_psTr_free"])
                        emit_ev1(prev_sd, prev_sd["_hT_free"])
                    emit_g(sd, None, cmb_ready, (NCI - 1, NCI), mark=True)
                    if t == NT - 1 and nf == ffn_idx and use_cache:
                        carry["wgu_tgts"].setdefault(nkey, [None, None])[0] = \
                            emit_wgu_dma(True, ne, cnt["pe"], carry.get("cache_sq"), 0)
                    emit_silu(sd, prev2_sd["tr2"] if prev2_sd and "tr2" in prev2_sd else None)
                    if not flushed:
                        emit_dnA(prev_sd, prev_sd["_psD_free"], None)
                        emit_tr2(prev_sd)
                        emit_ev2(prev_sd)
                    emit_u(sd)
                    if DEBUG and sd.get("_dbg_gs"):
                        sd["_h_extra_wait"] = sd["_dbg_gs"]
                    emit_h(sd, cmb_ready)
                    if not flushed:
                        emit_dnB(prev_sd)
                        emit_xevac(prev_sd)
                        carry["last_xev_dve"] = prev_sd["xev"]
                        if prev_sd["e"] == NEXP - 1:
                            carry.setdefault("_tile_xev", {})[prev_sd["t"]] = prev_sd["xev"]
                        prev_sd["_emitted_lag"] = True
                        if DEBUG and prev_sd.get("_dbg_h"):
                            emit_dbg_snapshot("dve", prev_sd["ev2"], dbg_hT,
                                              hT[:, prev_sd["slot"] * NIC * 128:
                                                 (prev_sd["slot"] + 1) * NIC * 128])
                            emit_dbg_snapshot("dve", prev_sd["xev"], dbg_xt,
                                              xt[:, 0:D])
                    if t == 0:
                        wd_key = carry["wd_tgts"].get(key)
                        if wd_key is None:
                            wp = prev_sd["dnB"] if (prev_sd and "dnB" in prev_sd) else None
                            ws = carry.get("cache_sq")
                            carry["wd_tgts"][key] = emit_wd_dma(use_cache, e, wp, ws)
                    sd["wd_tgts"] = carry["wd_tgts"][key]
                    sd["_psTr_free"] = prev_sd["ev2"] if (prev_sd and "ev2" in prev_sd) else None
                    sd["_hT_free"] = (prev2_sd["dnB"] if prev2_sd and "dnB" in prev2_sd else None)
                    sd["_psD_free"] = (prev_sd["xev"] if (prev_sd and "xev" in prev_sd
                                       and not prev_sd.get("_flush_boundary")) else lg_free)
                    carry["prev2_sd"] = prev_sd
                    carry["prev_sd"] = sd
                    carry["step"] = step + 1
                if not use_cache:
                    carry["cache_sq"] = emit_cache_write(
                        e, wgu_tgts, carry["wd_tgts"][key])
                if nf == ffn_idx:
                    nuc = not (nf == 0 and nr_ == 0)
                    carry["wgu_tgts"].setdefault(nkey, [None, None])
                    if carry["wgu_tgts"][nkey][0] is None:
                        carry["wgu_tgts"][nkey][0] = emit_wgu_dma(
                            nuc, ne, cnt["pe"], carry.get("cache_sq"), 0)
                    carry["wgu_tgts"][nkey][1] = emit_wgu_dma(
                        nuc, ne, cnt["pe"], carry.get("cache_sq"), 1)

            psd_ = carry["prev_sd"]
            emit_lagged(psd_, carry)
            emit_dnB(psd_)
            emit_xevac(psd_)
            psd_["_emitted_lag"] = True
            psd_["_flush_boundary"] = True
            carry["last_xev_dve"] = psd_["xev"]
            if psd_["e"] == NEXP - 1:
                carry.setdefault("_tile_xev", {})[psd_["t"]] = psd_["xev"]

            for t in range(NT):
                post("s_stx", 16)
                lxv = carry["_tile_xev"][t]

                def f_s(S, r=r, t=t, dst=dst, lxv=lxv):
                    S.wait_ge(SEM["dve"], lxv)
                    S.dma_start(out=dst[r * TPR + t * 128:r * TPR + (t + 1) * 128, :],
                                in_=xt[:, t * D:(t + 1) * D]).then_inc(SEM["s_stx"], 16)
                em("sync", f_s)
            carry["last_store_sq"] = [("s_stx", cnt["s_stx"])]
        return carry

    carry = {"step": 0, "wgu_tgts": {}, "wd_tgts": {}, "cache_sq": None,
             "prev_sd": None, "prev2_sd": None, "last_xev_dve": None,
             "last_store_sq": None}
    carry["wgu_tgts"][(0, 0, 0)] = [emit_wgu_dma(False, 0, None, None, 0),
                                    emit_wgu_dma(False, 0, None, None, 1)]
    carry["wd_tgts"][(0, 0, 0)] = emit_wd_dma(False, 0, None, None)

    dst1 = y_out if PHASES == "ffn1" else x1s
    carry = emit_ffn(0, dst1, carry)

    if PHASES in ("attn", "full"):
        carry["_wT_wait_pe"] = cnt["pe"]
        carry = emit_attention(carry)
        if PHASES == "attn":
            for r in range(NR):
                post("s_x", 16)
                ls = carry.get("last_store_sq")

                def f_xa(S, r=r, ls=ls, lxd=carry.get("last_xev_dve")):
                    if ls is not None:
                        for sname, v in ls:
                            S.wait_ge(SEM[sname], v)
                    if r == 0 and lxd is not None:
                        S.wait_ge(SEM["dve"], lxd)
                    if r > 0:
                        S.wait_ge(SEM["s_stx"], carry["_at_st"][r - 1])
                    S.dma_start(out=xt[:, :].rearrange("p (t n) -> p t n", t=NT),
                                in_=x1s[r * TPR:(r + 1) * TPR, :]
                                .rearrange("(t p) n -> p t n", p=128)).then_inc(SEM["s_x"], 16)
                em("sync", f_xa)
                xl = cnt["s_x"]
                post("g_acc", 16)

                def f_xb(G, r=r, xl=xl, rsr=carry["rs_ready"]):
                    G.wait_ge(SEM["s_x"], xl)
                    if rsr[1]:
                        G.wait_ge(SEM[rsr[0]], rsr[1])
                    G.dma_start(out=xt[:, :].rearrange("p (t n) -> p t n", t=NT),
                                in_=carry["rs_out"][r * TPR:(r + 1) * TPR, :]
                                .rearrange("(t p) n -> p t n", p=128),
                                accum_op=ALU.add).then_inc(SEM["g_acc"], 16)
                em("gpsimd", f_xb)
                xg = cnt["g_acc"]
                post("s_stx", 16)

                def f_xs(S, r=r, xg=xg):
                    S.wait_ge(SEM["g_acc"], xg)
                    S.dma_start(out=y_out[r * TPR:(r + 1) * TPR, :]
                                .rearrange("(t p) n -> p t n", p=128),
                                in_=xt[:, :].rearrange("p (t n) -> p t n", t=NT)
                                ).then_inc(SEM["s_stx"], 16)
                em("sync", f_xs)
                carry.setdefault("_at_st", {})[r] = cnt["s_stx"]
                carry["last_store_sq"] = [("s_stx", cnt["s_stx"])]
        else:
            carry["wgu_tgts"][(1, 0, 0)] = [emit_wgu_dma(True, 0, cnt["pe"], None, 0),
                                            emit_wgu_dma(True, 0, cnt["pe"], None, 1)]
            carry["wd_tgts"][(1, 0, 0)] = emit_wd_dma(True, 0, cnt["pe"], None)
            carry = emit_ffn(1, y_out, carry)

    fins = {k: v for k, v in cnt.items()
            if v > 0 and k not in ("pe", "act", "dve", "cc")}
    fin_cc = cnt["cc"]

    def f_fin(G, fins=fins, fin_cc=fin_cc):
        for k, v in fins.items():
            G.wait_ge(SEM[k], v)
        if fin_cc:
            G.wait_ge(SEM["cc"], fin_cc)
    em("gpsimd", f_fin)

    with nc.Block() as blk:
        blk.sync(lambda E: [f(E) for f in streams["sync"]])
        blk.gpsimd(lambda E: [f(E) for f in streams["gpsimd"]])
        blk.tensor(lambda E: [f(E) for f in streams["tensor"]])
        blk.vector(lambda E: [f(E) for f in streams["vector"]])
        blk.scalar(lambda E: [f(E) for f in streams["scalar"]])

    fes.close()
    es.close()
    return nc




from concourse.bass_utils import run_bass_kernel_spmd


def kernel(L0_attn_x, L0_attn_Wq, L0_attn_Wk, L0_attn_Wv, L0_attn_Wo,
           L0_ffn_gate_w, L0_ffn_w_gate, L0_ffn_w_up, L0_ffn_w_down,
           L0_ffn_sw_gate, L0_ffn_sw_up, L0_ffn_sw_down, L0_ffn_shared_gate_w):
    xs = np.ascontiguousarray(np.asarray(L0_attn_x, dtype=np.float32))
    wq = np.ascontiguousarray(np.asarray(L0_attn_Wq, dtype=np.float32))
    wk = np.ascontiguousarray(np.asarray(L0_attn_Wk, dtype=np.float32))
    wv = np.ascontiguousarray(np.asarray(L0_attn_Wv, dtype=np.float32))
    wo = np.ascontiguousarray(np.asarray(L0_attn_Wo, dtype=np.float32))
    gate_w = np.ascontiguousarray(np.asarray(L0_ffn_gate_w, dtype=np.float32))
    w_gate = np.ascontiguousarray(np.asarray(L0_ffn_w_gate, dtype=np.float32))
    w_up = np.ascontiguousarray(np.asarray(L0_ffn_w_up, dtype=np.float32))
    w_down = np.ascontiguousarray(np.asarray(L0_ffn_w_down, dtype=np.float32))
    sw_gate = np.ascontiguousarray(np.asarray(L0_ffn_sw_gate, dtype=np.float32))
    sw_up = np.ascontiguousarray(np.asarray(L0_ffn_sw_up, dtype=np.float32))
    sw_down = np.ascontiguousarray(np.asarray(L0_ffn_sw_down, dtype=np.float32))
    sgw = np.ascontiguousarray(np.asarray(L0_ffn_shared_gate_w, dtype=np.float32))

    nc = build_program()
    in_maps = []
    for core in range(8):
        b, hf = core // 2, core % 2
        in_maps.append({
            "x": np.ascontiguousarray(xs[b, hf * 2048:(hf + 1) * 2048, :]),
            "gate_w": gate_w, "sgw": sgw,
            "w_gate": w_gate, "w_up": w_up, "w_down": w_down,
            "sw_gate": sw_gate, "sw_up": sw_up, "sw_down": sw_down,
            "wq": np.ascontiguousarray(wq[hf * 1024:(hf + 1) * 1024, :]),
            "wk": np.ascontiguousarray(wk[hf * 1024:(hf + 1) * 1024, :]),
            "wv": np.ascontiguousarray(wv[hf * 1024:(hf + 1) * 1024, :]),
            "wo": np.ascontiguousarray(wo[:, hf * 1024:(hf + 1) * 1024]),
        })
    res = run_bass_kernel_spmd(nc, in_maps, core_ids=list(range(8)))
    out = np.zeros((4, 4096, 2048), np.float32)
    for core in range(8):
        b, hf = core // 2, core % 2
        out[b, hf * 2048:(hf + 1) * 2048, :] = res.results[core]["y"]
    return out

